# version 31
# baseline (speedup 1.0000x reference)
import math
from contextlib import ExitStack

import numpy as np
import ml_dtypes

import concourse.bass as bass
import concourse.mybir as mybir
from concourse.bass_utils import run_bass_kernel_spmd

F32 = mybir.dt.float32
BF16 = mybir.dt.bfloat16
I32 = mybir.dt.int32
AF = mybir.ActivationFunctionType
OP = mybir.AluOpType
AX = mybir.AxisListType
bfdt = ml_dtypes.bfloat16

N_CORES = 8
DBG = False
DBG2 = False
DBG3 = 0
XBAR = 0
EPS = 1e-5
MAX_BIAS = 8.0
NEGBIG = -1.0e9
BANKN = ["pb0", "pb1", "pb2", "pb3", "pb4", "pb5", "pav0", "pav1"]


class Cfg:
    def __init__(self, B=4, S=2048, DM=1024, H=16, FF=4096):
        self.B, self.S, self.DM, self.H, self.FF = B, S, DM, H, FF
        self.DH = 64
        self.NP = H // 2
        self.DMT = DM // 128
        self.CH = 512
        self.NCH = S // self.CH
        self.CT = self.CH // 128
        self.ST = S // 128
        self.G = 128
        self.FH = FF // 2
        self.FNTH = self.FH // 128
        self.TCH = 256
        self.NTC = S // self.TCH
        self.VBW = 65 * H
        self.SCALE = 1.0 / math.sqrt(192.0)


def _slopes(n):
    def pow2(n):
        start = 2.0 ** (-(2.0 ** (-(math.log2(n) - 3))))
        return [start * start ** i for i in range(n)]
    if math.log2(n).is_integer():
        return pow2(n)
    cp2 = 2 ** math.floor(math.log2(n))
    return pow2(cp2) + _slopes(2 * cp2)[0::2][: n - cp2]


def host_tables(cfg):
    S, H, DH = cfg.S, cfg.H, cfg.DH
    sl = np.array(_slopes(H), dtype=np.float64)
    pos = np.arange(S, dtype=np.float64)
    inv_freq = 1.0 / 10000.0 ** (np.arange(0, DH, 2, dtype=np.float64) / DH)
    fr = np.outer(pos, inv_freq)
    sin = np.concatenate([np.sin(fr), np.sin(fr)], axis=1).T
    cos = np.concatenate([np.cos(fr), np.cos(fr)], axis=1).T
    sin_s = sin.copy()
    sin_s[: DH // 2] = -sin_s[: DH // 2]
    cos2 = np.concatenate([cos, cos], axis=0)
    sin2 = np.concatenate([sin_s, sin_s], axis=0)
    t_sincos = np.ascontiguousarray(
        np.concatenate([cos2, sin2], axis=1).astype(np.float32))
    NKT = cfg.ST
    t_hkt = np.zeros((128, H * NKT), dtype=np.float64)
    dk = np.arange(128, dtype=np.float64)
    for h in range(H):
        for kt in range(NKT):
            t_hkt[:, h * NKT + kt] = MAX_BIAS * sl[h] * (128 * kt + dk) - 24.0
    q = np.arange(S, dtype=np.float64)
    t_aug = (-math.sqrt(192.0) * MAX_BIAS * sl[:, None] * q[None, :]).astype(bfdt)
    t_band = np.where(dk[:, None] > dk[None, :], np.float32(NEGBIG),
                      np.float32(0.0)).astype(np.float32)
    t_perm = np.zeros((128, 128), dtype=np.float32)
    for (do, so) in ((0, 32), (32, 0), (64, 96), (96, 64)):
        for i in range(32):
            t_perm[so + i, do + i] = 1.0
    return dict(t_sincos=t_sincos, t_hkt=t_hkt.astype(np.float32),
                t_aug=np.ascontiguousarray(t_aug), t_band=t_band,
                t_ones_row=np.ones((1, 128), dtype=bfdt),
                t_ones2=np.ones((2, 64), dtype=bfdt),
                t_ones_col=np.ones((128, 1), dtype=bfdt),
                t_perm=np.ascontiguousarray(t_perm.astype(bfdt)),
                t_ident=np.eye(128, dtype=np.float32))


def host_prep(cfg, w_qkv, b_qkv, w_out, b_out, ln_gamma, ln_beta,
              fc1_q, fc1_scales, fc1_bias, fc2_q, fc2_scales, fc2_bias):
    DM, H = cfg.DM, cfg.H
    w_qkv = np.asarray(w_qkv)
    wq = np.asarray(w_qkv[:, :, 0:64], dtype=np.float32)
    wk = np.asarray(w_qkv[:, :, 64:128], dtype=np.float32)
    wq5 = wq.reshape(8, 128, 8, 2, 64).transpose(2, 1, 0, 3, 4).reshape(8, 128, 1024)
    wk5 = wk.reshape(8, 128, 8, 2, 64).transpose(2, 1, 0, 3, 4).reshape(8, 128, 1024)
    wqkA = np.ascontiguousarray(
        np.concatenate([wq5, wk5], axis=2).reshape(8 * 128, 2048))
    wv = np.zeros((DM, 65 * H), dtype=np.float32)
    for h in range(H):
        wv[:, 65 * h:65 * h + 64] = w_qkv[:, h, 128:192]
    b_qkv = np.asarray(b_qkv)
    bq = np.zeros((128, 2 * cfg.NP), dtype=np.float32)
    for j in range(cfg.NP):
        bq[0:64, j] = b_qkv[2 * j, 0:64]
        bq[64:128, j] = b_qkv[2 * j + 1, 0:64]
        bq[0:64, cfg.NP + j] = b_qkv[2 * j, 64:128]
        bq[64:128, cfg.NP + j] = b_qkv[2 * j + 1, 64:128]
    bva = np.zeros((1, 65 * H), dtype=np.float32)
    for h in range(H):
        bva[0, 65 * h:65 * h + 64] = b_qkv[h, 128:192]
        bva[0, 65 * h + 64] = 1.0
    fc1_q = np.asarray(fc1_q, dtype=np.int32)
    fc2_q = np.asarray(fc2_q, dtype=np.int32)
    fc1qt = np.ascontiguousarray(
        fc1_q.reshape(8, 128, 8, 512).transpose(2, 0, 1, 3).reshape(64 * 128, 512))
    fc2qt = np.ascontiguousarray(
        fc2_q.reshape(32, 128, 2, 512).transpose(0, 2, 1, 3).reshape(64 * 128, 512))
    return dict(
        wqkA=wqkA, wvaug=wv, bqk_t=bq, bvaug=bva, fc1qt=fc1qt, fc2qt=fc2qt,
        wout=np.ascontiguousarray(np.asarray(w_out).reshape(H * 64, DM), dtype=np.float32),
        bout=np.ascontiguousarray(np.asarray(b_out).reshape(1, DM), dtype=np.float32),
        gammacol=np.ascontiguousarray(np.asarray(ln_gamma).reshape(cfg.DMT, 128).T, dtype=np.float32),
        betacol=np.ascontiguousarray(np.asarray(ln_beta).reshape(cfg.DMT, 128).T, dtype=np.float32),
        fc1sc=np.ascontiguousarray(np.asarray(fc1_scales), dtype=np.float32),
        fc1b=np.ascontiguousarray(np.asarray(fc1_bias).reshape(1, cfg.FF), dtype=np.float32),
        fc2sc=np.ascontiguousarray(np.asarray(fc2_scales), dtype=np.float32),
        fc2b=np.ascontiguousarray(np.asarray(fc2_bias).reshape(1, cfg.DM), dtype=np.float32))


ENGS = ("tensor", "scalar", "vector", "sync", "gpsimd")
BARS = {"tensor": "bar_pe", "scalar": "bar_act", "vector": "bar_dve",
        "sync": "bar_sp", "gpsimd": "bar_pl"}


class K:
    def __init__(self, cfg):
        self.cfg = cfg
        self.nc = bass.Bass("TRN2", target_bir_lowering=False)
        self.stack = ExitStack()
        self.sh = {}
        self.scnt = {}
        self.barn = 0
        self.blk = None
        self._bufs = {}
        self.g = {}
        self.info = {}

    def sem(self, name):
        self.sh[name] = self.stack.enter_context(self.nc.semaphore(name))
        self.scnt[name] = 0

    def v(self, name):
        return self.scnt[name]

    def flush(self):
        for engname in ENGS:
            fns = self._bufs.get(engname)
            if fns:
                getattr(self.blk, engname)(lambda eng, fns=fns: [f(eng) for f in fns])
                self._bufs[engname] = []

    def on(self, engine, fn):
        self._bufs.setdefault(engine, []).append(fn)

    def emit(self, engine, waits, mk, incs=()):
        after = {}
        for s, a in incs:
            self.scnt[s] += a
            after[s] = self.scnt[s]
        wd = {}
        for s, v0 in waits:
            if v0 > 0:
                wd[s] = max(wd.get(s, 0), v0)
        sh = self.sh

        def fn(eng, wd=wd, mk=mk, incs=incs):
            for s, v0 in wd.items():
                eng.wait_ge(sh[s], v0)
            ins = mk(eng)
            for s, a in incs:
                if ins is None:
                    eng.sem_inc(sh[s], a)
                else:
                    ins.then_inc(sh[s], a)
        self.on(engine, fn)
        return after

    def gate(self, name):
        return list(self.g.get(name, {}).items())

    def gates(self, *names):
        out = []
        for n in names:
            out += self.gate(n)
        return out

    def setg(self, name, *pairs):
        d = {}
        for s, v0 in pairs:
            if v0 > 0:
                d[s] = max(d.get(s, 0), v0)
        self.g[name] = d

    def addg(self, name, *pairs):
        d = self.g.setdefault(name, {})
        for s, v0 in pairs:
            if v0 > 0:
                d[s] = max(d.get(s, 0), v0)

    def barrier(self, drain=True):
        self.barn += 1
        n = self.barn
        qsems = {"sync": [s for s in self.sh if s.startswith("dL") or s.startswith("dT")],
                 "gpsimd": [s for s in self.sh if s.startswith("dG")],
                 "scalar": [s for s in self.sh if s.startswith("dA")]}
        for e in ENGS:
            drains = [(s, self.scnt[s]) for s in qsems.get(e, []) if self.scnt[s] > 0] \
                if drain else []
            def f(eng, e=e, n=n, drains=drains):
                for s, v0 in drains:
                    eng.wait_ge(self.sh[s], v0)
                eng.sem_inc(self.sh[BARS[e]], 1)
                for e2 in ENGS:
                    if e2 != e:
                        eng.wait_ge(self.sh[BARS[e2]], n)
            self.on(e, f)
        for e in ENGS:
            self.scnt[BARS[e]] += 1
        self.flush()

    def sb(self, name, shape, dtype, stack=None):
        return (stack or self.stack).enter_context(self.nc.sbuf_tensor(name, shape, dtype))

    def ps(self, name, shape, dtype=F32, stack=None):
        return (stack or self.stack).enter_context(self.nc.psum_tensor(name, shape, dtype))


def build(cfg):
    k = K(cfg)
    nc = k.nc
    c = cfg
    NKT = c.ST

    din = {}
    def I(name, shape, dt=F32):
        din[name] = nc.dram_tensor(name, shape, dt, kind="ExternalInput").ap()
    I("x", [c.B, c.S, c.DM])
    I("wqkA", [c.NP * 128, 2048]); I("wvaug", [c.DM, 65 * c.H])
    I("bqk_t", [128, 2 * c.NP]); I("bvaug", [1, 65 * c.H])
    I("wout", [c.H * 64, c.DM]); I("bout", [1, c.DM])
    I("gammacol", [128, c.DMT]); I("betacol", [128, c.DMT])
    I("fc1qt", [64 * 128, 512], I32); I("fc1sc", [c.DM // c.G, c.FF]); I("fc1b", [1, c.FF])
    I("fc2qt", [64 * 128, 512], I32); I("fc2sc", [c.FF // c.G, c.DM]); I("fc2b", [1, c.DM])
    I("t_sincos", [128, 2 * c.S]); I("t_hkt", [128, c.H * NKT])
    I("t_aug", [c.H, c.S], BF16); I("t_band", [128, 128])
    I("t_ones_row", [1, 128], BF16); I("t_ones2", [2, 64], BF16)
    I("t_ones_col", [128, 1], BF16)
    I("t_perm", [128, 128], BF16)
    I("t_ident", [128, 128])
    y = nc.dram_tensor("y", [c.B, c.S, c.DM], F32, kind="ExternalOutput").ap()
    s_w1 = nc.dram_tensor("s_w1", [c.DM, c.FF], BF16, kind="Internal").ap()
    s_wqk = nc.dram_tensor("s_wqk", [c.NP * 128, 2048], BF16, kind="Internal").ap()
    s_w2 = nc.dram_tensor("s_w2", [c.FF, c.DM], BF16, kind="Internal").ap()
    s_fc1b = nc.dram_tensor("s_fc1b", [1, c.FF], F32, kind="Internal").ap()
    s_x1 = nc.dram_tensor("s_x1", [c.B, c.S, c.DM], F32,
                          kind="ExternalOutput" if DBG else "Internal").ap()
    s_xh2 = nc.dram_tensor("s_xh2", [c.B, c.S, c.DM], BF16,
                           kind="ExternalOutput" if DBG else "Internal").ap()
    d_atc = None
    if DBG:
        d_atc = nc.dram_tensor("d_atc", [c.NCH, c.NP, 128, c.CH], BF16,
                               kind="ExternalOutput").ap()
        if DBG2:
            if DBG3 & 2:
                k.d_sc = nc.dram_tensor("d_sc", [128, 512], F32, kind="ExternalOutput").ap()
                k.d_scb = k.sb("d_scb", [128, 512], F32)
            if DBG3 & 1:
                k.d_p0 = nc.dram_tensor("d_p0", [128, 512], BF16, kind="ExternalOutput").ap()
                k.d_pbuf = k.sb("d_pbuf", [128, 512], BF16)
            if DBG3 & 4:
                k.d_av = nc.dram_tensor("d_av", [128, 512], F32, kind="ExternalOutput").ap()
                k.d_avb = k.sb("d_avb", [128, 512], F32)
            if DBG3 & 16:
                k.d_vb = nc.dram_tensor("d_vb", [128, 2080], BF16, kind="ExternalOutput").ap()
            if DBG3 & 8:
                k.d_rb = nc.dram_tensor("d_rb", [64, 512], F32, kind="ExternalOutput").ap()
                k.d_rcp = nc.dram_tensor("d_rcp", [1, 512], F32, kind="ExternalOutput").ap()
                k.d_rbb = k.sb("d_rbb", [64, 512], F32)
                k.d_rcpb = k.sb("d_rcpb", [1, 512], F32)
    k.d_atc = d_atc
    k.din, k.y = din, y
    k.s = dict(w1=s_w1, w2=s_w2, fc1b=s_fc1b, x1=s_x1, xh2=s_xh2, wqk=s_wqk)

    S_ = {}
    S_["hkt"] = k.sb("hkt", [128, c.H * NKT], F32)
    S_["band"] = k.sb("band", [128, 128], F32)
    S_["ones_row"] = k.sb("ones_row", [1, 128], BF16)
    S_["ones2"] = k.sb("ones2", [2, 64], BF16)
    S_["ones_col"] = k.sb("ones_col", [128, 1], BF16)
    S_["bqk"] = k.sb("bqk", [128, 2 * c.NP], F32)
    S_["wvs"] = k.sb("wvs", [128, c.DMT * c.VBW], BF16)
    S_["identf"] = k.sb("identf", [128, 128], F32)
    S_["bvaugb"] = k.sb("bvaugb", [1, 65 * c.H], BF16)
    S_["boutb"] = k.sb("boutb", [1, c.DM], BF16)
    S_["frow2"] = k.sb("frow2", [1, c.DM], BF16)
    S_["fc1bcol"] = k.sb("fc1bcol", [128, c.FF // 128], F32)
    
    S_["permb"] = k.sb("permb", [128, 128], BF16)
    S_["probs"] = [k.sb(f"probs{i}", [128, 512], BF16) for i in range(2)]
    S_["bns"] = k.sb("bns", [128, 12], F32)
    S_["lnt"] = k.sb("lnt", [128, 8], F32)

    S_["ones2f"] = k.sb("ones2f", [1, 64], F32)
    cst = k.sb("cst", [128, 4], F32)
    for i_, v_ in enumerate([EPS, EPS * EPS, 1.0 + EPS, -128.0]):
        nc.const_aps.register(F32, v_, cst[:, i_:i_ + 1])
    S_["cst"] = cst
    pbk = [k.ps(f"pb{i}", [128, 512]) for i in range(6)]
    pavk = [k.ps(f"pav{i}", [128, 512]) for i in range(2)]
    S_["pb"], S_["pav"] = pbk, pavk
    S_["banks"] = pbk + pavk

    for e in ENGS:
        k.sem(BARS[e])
    for s in ["sPE", "sACT", "sDVE", "sPL",
              "dLx0", "dLx1", "dLsc", "dLtab", "dLq0", "dLq1", "dLq2", "dLq3",
              "dGq2", "dGq3", "dLfb",
              "dTx0", "dTx1", "dLxr0", "dLxr1", "dLw0", "dLw1",
              "dLxh0", "dLxh1", "dLx10", "dLx11",
              "dGw0", "dGw1", "dGv", "dGsw0", "dGsw1", "dGsb", "dGsp", "dGout", "dGq0", "dGq1",
              "dGsplit0", "dGsplit1", "dGo0", "dGo1", "dGaux"]:
        k.sem(s)

    with nc.Block() as blk:
        k.blk = blk
        for i_, v_ in enumerate([EPS, EPS * EPS, 1.0 + EPS, -128.0]):
            k.emit("vector", [], lambda e, i_=i_, v_=v_: e.memset(cst[:, i_:i_ + 1], v_),
                   [("sDVE", 1)])
        k.emit("vector", [], lambda e: e.memset(S_["ones2f"][:, :], 1.0), [("sDVE", 1)])
        for (dst, src) in [(S_["hkt"][:, :], din["t_hkt"][:, :]),
                           (S_["band"][:, :], din["t_band"][:, :]),
                           (S_["ones_row"][:, :], din["t_ones_row"][:, :]),
                           (S_["ones2"][:, :], din["t_ones2"][:, :]),
                           (S_["ones_col"][:, :], din["t_ones_col"][:, :]),
                           (S_["identf"][:, :], din["t_ident"][:, :]),
                           (S_["permb"][:, :], din["t_perm"][:, :]),
                           (S_["bqk"][:, :], din["bqk_t"][:, :])]:
            k.emit("sync", [], lambda e, d=dst, s=src: e.dma_start(out=d, in_=s),
                   [("dLtab", 16)])
        for (dst, src) in [(S_["bvaugb"][:, :], din["bvaug"][:, :]),
                           (S_["boutb"][:, :], din["bout"][:, :]),
                           (S_["frow2"][:, :], din["fc2b"][:, :])]:
            k.emit("gpsimd", [], lambda e, d=dst, s=src: e.dma_start(out=d, in_=s),
                   [("dGaux", 16)])

        for d in range(c.DMT):
            k.emit("gpsimd", [],
                   lambda e, d=d: e.dma_start(out=S_["wvs"][:, d * c.VBW:(d + 1) * c.VBW],
                                              in_=din["wvaug"][128 * d:128 * (d + 1), :]),
                   [("dGv", 16)])
        emit_stage0(k, S_)
        k.barrier(drain=True)

        for b in range(c.B):
            with ExitStack() as ab:
                A = dict(S_)
                A["kh"] = [k.sb(f"kh{h}_b{b}", [65, c.S], BF16, ab) for h in range(c.H)]
                A["vb"] = k.sb(f"vb_b{b}", [128, c.ST * c.VBW], BF16, ab)
                A["atc"] = [k.sb(f"atc{p}_b{b}", [128, c.CH], BF16, ab)
                            for p in range(c.NP)]
                A["atcf"] = k.sb(f"atcf_b{b}", [128, c.CH], F32, ab)
                A["rsrow"] = k.sb(f"rsrow_b{b}", [1, 1024], F32, ab)
                A["rb"] = k.sb(f"rb_b{b}", [128, 512], F32, ab)
                A["qh"] = [[k.sb(f"qh{par}_{hh}_b{b}", [65, c.CH], BF16, ab)
                            for hh in range(2)] for par in range(2)]
                A["xtc"] = k.sb(f"xtc_b{b}", [128, c.DMT * c.CH], BF16, ab)
                A["xbc"] = [k.sb(f"xbc{i}_b{b}", [128, c.DM], BF16, ab) for i in range(2)]
                A["xpc"] = [k.sb(f"xpc{i}_b{b}", [128, c.DM], F32, ab) for i in range(2)]
                A["wqkp"] = [k.sb(f"wqkp{i}_b{b}", [128, 2 * c.DMT * 128], BF16, ab)
                             for i in range(2)]
                A["scb"] = [k.sb(f"scb{i}_b{b}", [128, 2 * c.CH], F32, ab)
                            for i in range(2)]
                A["qbb"] = k.sb(f"qbb_b{b}", [128, 2 * c.CH], BF16, ab)
                A["t2"] = k.sb(f"t2_b{b}", [128, c.CH], F32, ab)
                A["t1"] = k.sb(f"t1_b{b}", [128, c.CH], F32, ab)
                A["rout"] = [k.sb(f"rout{i}_b{b}", [128, c.CH], BF16, ab)
                             for i in range(2)]
                A["wob"] = [k.sb(f"wob{i}_b{b}", [128, c.DM], BF16, ab) for i in range(2)]
                A["y1t"] = k.sb(f"y1t_b{b}", [128, c.DM], F32, ab)
                A["x1s"] = k.sb(f"x1s_b{b}", [128, c.DM], F32, ab)
                A["xh2s"] = k.sb(f"xh2s_b{b}", [128, c.DM], BF16, ab)
                emit_A(k, b, A)
                k.barrier()

            with ExitStack() as fstk:
                F = dict(S_)
                F["w1h"] = k.sb(f"w1h_b{b}", [128, c.DMT * c.FH], BF16, fstk)
                F["w2h"] = k.sb(f"w2h_b{b}", [128, c.FNTH * c.DM], BF16, fstk)
                F["yT"] = k.sb(f"yT_b{b}", [128, c.FNTH * c.TCH], BF16, fstk)
                F["xh2T"] = [k.sb(f"xh2T{i}_b{b}", [128, c.DMT * c.TCH], BF16, fstk)
                             for i in range(2)]
                F["xr"] = [k.sb(f"xr{i}_b{b}", [128, 2 * c.DM], F32, fstk)
                           for i in range(2)]
                F["zacc"] = k.sb(f"zacc_b{b}", [128, c.ST * c.DM], F32, fstk)
                F["tfin"] = k.sb(f"tfin_b{b}", [128, c.DM], F32, fstk)
                F["tfin2"] = k.sb(f"tfin2_b{b}", [128, c.DM], F32, fstk)
                F["outst"] = [k.sb(f"outst{i}_b{b}", [128, c.DM], F32, fstk)
                              for i in range(2)]
                emit_F(k, b, F)
                k.barrier()

        k.emit("gpsimd", [("dGout", k.v("dGout"))], lambda e: e.engine_nop(), [])
        k.flush()
    k.stack.close()
    return k


def emit_stage0(k, S_):
    c = k.cfg
    din = k.din
    pb = S_["pb"]
    with ExitStack() as st0:
        qsb = [k.sb(f"qsb{i}", [128, 512], I32, st0) for i in range(4)]
        scr = [k.sb(f"scr{i}", [1, 512], BF16, st0) for i in range(4)]
        qf = [k.sb(f"qf{i}", [128, 512], F32, st0) for i in range(2)]
        plainb = [k.sb(f"plainb{i}", [128, 512], BF16, st0) for i in range(2)]
        w1st = [k.sb(f"w1st{i}", [128, 512], BF16, st0) for i in range(2)]
        gcolb = k.sb("gcolb", [128, c.DMT], F32, st0)
        bcolb = k.sb("bcolb", [128, c.DMT], BF16, st0)
        fbrow = k.sb("fbrow", [1, 512], F32, st0)
        fbrow2 = k.sb("fbrow2", [1, 512], F32, st0)

        k.emit("sync", [], lambda e: e.dma_start(out=gcolb[:, :], in_=din["gammacol"][:, :]),
               [("dLtab", 16)])
        k.emit("gpsimd", [], lambda e: e.dma_start(out=bcolb[:, :], in_=din["betacol"][:, :]),
               [("dGaux", 16)])
        tab_v = k.v("dLtab")
        aux_v = k.v("dGaux")
        k.auxv = aux_v
        k.tabv = tab_v

        cvt = [k.sb(f"cvt{i}", [128, 2048], BF16, st0) for i in range(2)]
        for bi_ in range(c.NP):
            cp_ = bi_ % 2
            a = k.emit("gpsimd", k.gate(f"cvt{cp_}"),
                       lambda e, bi_=bi_, cp_=cp_: e.dma_start(
                           out=cvt[cp_][:, :],
                           in_=din["wqkA"][128 * bi_:128 * (bi_ + 1), :]),
                       [("dGv", 16)])
            gv_ = a["dGv"]
            a = k.emit("sync", [("dGv", gv_)],
                       lambda e, bi_=bi_, cp_=cp_: e.dma_start(
                           out=k.s["wqk"][128 * bi_:128 * (bi_ + 1), :],
                           in_=cvt[cp_][:, :]),
                       [("dLfb", 16)])
            k.setg(f"cvt{cp_}", ("dLfb", a["dLfb"]))

        tiles = [("w1", ncq, d) for ncq in range(c.FF // 512) for d in range(c.DMT)]
        tiles += [("w2", nt, dc) for nt in range(c.FF // 128) for dc in range(c.DM // 512)]

        for i, (kind, a_, b_) in enumerate(tiles):
            lp = i % 4
            par = i % 2
            if kind == "w1":
                ti = a_ * 8 + b_
                src_q = din["fc1qt"][128 * ti:128 * (ti + 1), :]
                src_s = din["fc1sc"][b_:b_ + 1, 512 * a_:512 * (a_ + 1)]
            else:
                ti = a_ * 2 + b_
                src_q = din["fc2qt"][128 * ti:128 * (ti + 1), :]
                src_s = din["fc2sc"][a_:a_ + 1, 512 * b_:512 * (b_ + 1)]
            k.emit("sync" if lp < 2 else "scalar", k.gate(f"qsb{lp}"),
                   lambda e, s=src_q, p=lp: e.dma_start(out=qsb[p][:, :], in_=s),
                   [("dLq%d" % lp, 16)])
            k.emit("gpsimd", k.gate(f"scr{lp}"),
                   lambda e, s=src_s, p=lp: e.dma_start(out=scr[p][:, :], in_=s),
                   [("dGq%d" % lp, 16)])
            ldv = k.v("dLq%d" % lp)
            gqv = k.v("dGq%d" % lp)
            a1 = k.emit("tensor", [("dGq%d" % lp, gqv),
                                  ("dLtab", tab_v)] + k.gate(f"s0pb{par}"),
                        lambda e, p=par, lq=lp: e.matmul(pb[p][:, :],
                                                         S_["ones_row"][0:1, 0:128],
                                                         scr[lq][0:1, :],
                                                         start=True, stop=True),
                        [("sPE", 1)])
            a2 = k.emit("scalar", [("dLq%d" % lp, ldv), ("sDVE", 4)] + k.gate(f"qf{par}"),
                        lambda e, p=par, lq=lp: e.activation(out=qf[p][:, :],
                                                             in_=qsb[lq][:, :],
                                                             func=AF.Copy, bias=-128.0),
                        [("sACT", 1)])
            k.setg(f"qsb{lp}", ("sACT", a2["sACT"]))
            k.setg(f"scr{lp}", ("sPE", a1["sPE"]))
            a3 = k.emit("vector", [("sACT", a2["sACT"]), ("sPE", a1["sPE"])] +
                        k.gate(f"plainb{par}"),
                        lambda e, p=par: e.tensor_tensor(out=plainb[p][:, :],
                                                         in0=qf[p][:, :],
                                                         in1=pb[p][:, :], op=OP.mult),
                        [("sDVE", 1)])
            k.setg(f"qf{par}", ("sDVE", a3["sDVE"]))
            k.setg(f"s0pb{par}", ("sDVE", a3["sDVE"]))
            if kind == "w1":
                a4 = k.emit("tensor", [("sDVE", a3["sDVE"]), ("dGaux", aux_v)] +
                            (k.gate("fbpsum") if b_ == 0 else []),
                            lambda e, d=b_, p=par: e.matmul(pb[2][0:1, 0:512],
                                                            bcolb[:, d:d + 1],
                                                            plainb[p][:, :], start=(d == 0),
                                                            stop=(d == c.DMT - 1)),
                            [("sPE", 1)])
                a5 = k.emit("vector", [("dLtab", tab_v)] + k.gate(f"w1st{par}"),
                            lambda e, d=b_, p=par: e.tensor_scalar(
                                out=w1st[p][:, :], in0=plainb[p][:, :],
                                scalar1=gcolb[:, d:d + 1],
                                scalar2=None, op0=OP.mult),
                            [("sDVE", 1)])
                k.setg(f"plainb{par}", ("sDVE", a5["sDVE"]), ("sPE", a4["sPE"]))
                wdst = k.s["w1"][128 * b_:128 * (b_ + 1), 512 * a_:512 * (a_ + 1)]
                a6 = k.emit("gpsimd", [("sDVE", a5["sDVE"])],
                            lambda e, d=wdst, p=par: e.dma_start(out=d, in_=w1st[p][:, :]),
                            [("dGsw%d" % par, 16)])
                k.setg(f"w1st{par}", ("dGsw%d" % par, a6["dGsw%d" % par]))
                if b_ == c.DMT - 1:
                    ncq = a_
                    a7 = k.emit("scalar", [("sPE", a4["sPE"])] + k.gate("fbrow2"),
                                lambda e: e.activation(out=fbrow2[:, :],
                                                       in_=pb[2][0:1, 0:512], func=AF.Copy),
                                [("sACT", 1)])
                    k.setg("fbpsum", ("sACT", a7["sACT"]))
                    a8 = k.emit("sync", k.gate("fbrow"),
                                lambda e, n=ncq: e.dma_start(
                                    out=fbrow[:, :],
                                    in_=din["fc1b"][:, 512 * n:512 * (n + 1)]),
                                [("dLfb", 16)])
                    a9 = k.emit("vector", [("sACT", a7["sACT"]), ("dLfb", a8["dLfb"])],
                                lambda e: e.tensor_tensor(out=fbrow2[:, :], in0=fbrow2[:, :],
                                                          in1=fbrow[:, :], op=OP.add),
                                [("sDVE", 1)])
                    k.setg("fbrow", ("sDVE", a9["sDVE"]))
                    a10 = k.emit("gpsimd", [("sDVE", a9["sDVE"])],
                                 lambda e, n=ncq: e.dma_start(
                                     out=k.s["fc1b"][:, 512 * n:512 * (n + 1)],
                                     in_=fbrow2[:, :]),
                                 [("dGsb", 16)])
                    k.setg("fbrow2", ("dGsb", a10["dGsb"]))
            else:
                wdst = k.s["w2"][128 * a_:128 * (a_ + 1), 512 * b_:512 * (b_ + 1)]
                a6 = k.emit("gpsimd", [("sDVE", a3["sDVE"])],
                            lambda e, d=wdst, p=par: e.dma_start(out=d, in_=plainb[p][:, :]),
                            [("dGsw%d" % par, 16)])
                k.setg(f"plainb{par}", ("dGsw%d" % par, a6["dGsw%d" % par]))
        sw0, sw1, sbv = k.v("dGsw0"), k.v("dGsw1"), k.v("dGsb")
        for j in range(c.FF // 128):
            k.emit("sync", [("dGsw0", sw0), ("dGsw1", sw1), ("dGsb", sbv)] if j == 0 else [],
                   lambda e, j=j: e.dma_start(
                       out=S_["fc1bcol"][:, j:j + 1],
                       in_=k.s["fc1b"][:, 128 * j:128 * (j + 1)].rearrange(
                           "o (p q) -> p (o q)", q=1)),
                   [("dLtab", 16)])


def emit_A(k, b, A):
    c = k.cfg
    din = k.din
    pb, pav = A["pb"], A["pav"]

    for h in range(c.H):
        k.emit("gpsimd" if h % 2 == 0 else "vector", [],
               lambda e, h=h: e.memset(A["kh"][h][64:65, :], 1.0),
               [("sPL", 1)])
    k.info["khones"] = k.v("sPL")
    if DBG2 and (DBG3 & 4) and b == 0:
        k.emit("vector", [], lambda e: e.memset(k.d_avb[:, :], 0.0), [("sDVE", 1)])

    def a1sc(ch):
        q0 = ch * c.CH
        sc2 = A["scb"][ch % 2]
        for tt in range(c.CT):
            par = tt % 2
            a = k.emit("sync", k.gate(f"xpc{par}"),
                       lambda e, tt=tt, par=par, q0=q0: e.dma_start(
                           out=A["xpc"][par][:, :],
                           in_=din["x"][b, q0 + 128 * tt:q0 + 128 * (tt + 1), :]),
                       [("dLx%d" % par, 16)])
            ldv = a["dLx%d" % par]
            for rnd in range(2):
                tb = pb[3 + rnd]
                for dd in range(4):
                    d = 4 * rnd + dd
                    wl = []
                    if dd == 0:
                        wl = [("dLx%d" % par, ldv), ("dLtab", k.tabv)] + k.gate(BANKN[3 + rnd])
                    k.emit("tensor", wl,
                           lambda e, d=d, dd=dd, par=par, tb=tb: e.transpose(
                               tb[:, 128 * dd:128 * (dd + 1)],
                               A["xpc"][par][:, 128 * d:128 * (d + 1)],
                               A["identf"][:, :]),
                           [("sPE", 1)])
                a2 = k.emit("scalar", [("sPE", k.v("sPE"))] + (k.gate("xtc") if tt == 0 and rnd == 0 else []),
                            lambda e, rnd=rnd, tt=tt, tb=tb: e.activation(
                                out=bass.AP(A["xtc"], 4 * rnd * c.CH + 128 * tt,
                                            [[c.DMT * c.CH, 128], [c.CH, 4], [1, 128]]),
                                in_=tb[:, 0:512], func=AF.Copy),
                            [("sACT", 1)])
                k.addg(BANKN[3 + rnd], ("sACT", a2["sACT"]))
            k.setg(f"xpc{par}", ("sPE", k.v("sPE")))
        k.info[("xtcok", ch)] = k.v("sACT")
        a = k.emit("sync", k.gate(f"scb{ch % 2}"),
                   lambda e, q0=q0, sc2=sc2: e.dma_start(
                       out=sc2[:, 0:c.CH], in_=din["t_sincos"][:, q0:q0 + c.CH]),
                   [("dLsc", 16)])
        k.emit("sync", [],
               lambda e, q0=q0, sc2=sc2: e.dma_start(
                   out=sc2[:, c.CH:2 * c.CH],
                   in_=din["t_sincos"][:, c.S + q0:c.S + q0 + c.CH]),
               [("dLsc", 16)])
        k.info[("scv", ch)] = k.v("dLsc")

    a1sc(0)
    for ch in range(c.NCH):
        q0 = ch * c.CH
        xtc_ok = k.info[("xtcok", ch)]
        sc_v = k.info[("scv", ch)]
        if XBAR >= 4:
            k.barrier()
        if c.VBW <= 512:
            pieces = [(0, c.VBW)]
        elif c.VBW <= 1024:
            pieces = [(0, 512), (512, c.VBW - 512)]
        else:
            pieces = [(0, 512), (512, 512), (1024, c.VBW - 1024)]
        vgi = 0
        for tt in range(c.CT):
            ktg = ch * c.CT + tt
            for (off, wdt) in pieces:
                bi = 3 + (vgi % 3)
                bank = pb[bi]
                for d in range(c.DMT):
                    wl = []
                    if d == 0:
                        wl = [("sACT", xtc_ok),
                              ("dGaux", k.auxv)] + k.gate(BANKN[bi])
                    k.emit("tensor", wl,
                           lambda e, d=d, tt=tt, off=off, wdt=wdt, bank=bank: e.matmul(
                               bank[:, 0:wdt],
                               A["xtc"][:, c.CH * d + 128 * tt:c.CH * d + 128 * (tt + 1)],
                               A["wvs"][:, c.VBW * d + off:c.VBW * d + off + wdt],
                               start=(d == 0), stop=False),
                           [("sPE", 1)])
                a = k.emit("tensor", [],
                           lambda e, off=off, wdt=wdt, bank=bank: e.matmul(
                               bank[:, 0:wdt], A["ones_row"][0:1, 0:128],
                               A["bvaugb"][0:1, off:off + wdt], start=False, stop=True),
                           [("sPE", 1)])
                a2 = k.emit("scalar", [("sPE", a["sPE"])],
                            lambda e, off=off, wdt=wdt, bank=bank, ktg=ktg: e.activation(
                                out=A["vb"][:, c.VBW * ktg + off:c.VBW * ktg + off + wdt],
                                in_=bank[:, 0:wdt], func=AF.Copy),
                            [("sACT", 1)])
                k.addg(BANKN[bi], ("sACT", a2["sACT"]))
                vgi += 1
        k.info["vb_ready"] = k.v("sACT")

        if XBAR >= 2:
            k.barrier()
        for s in range(c.NP + 1):
            if s < c.NP:
                emit_pair_qk_part1(k, b, ch, s, A, sc_v, xtc_ok)
            if s > 0:
                emit_pair_attention(k, b, ch, s - 1, A)
        if DBG and b == 0:
            for p in range(c.NP):
                k.emit("gpsimd", k.gate("atc_all"),
                       lambda e, p=p, ch=ch: e.dma_start(
                           out=k.d_atc[ch, p, :, :], in_=A["atc"][p][:, :]),
                       [("dGsp", 16)])
        if ch + 1 < c.NCH:
            a1sc(ch + 1)
        emit_outproj_chunk(k, b, ch, A)
        if XBAR >= 1:
            k.barrier()
    if DBG2 and b == 0:
        w0 = [("sDVE", k.v("sDVE")), ("sACT", k.v("sACT"))]
        if DBG3 & 2:
            k.emit("gpsimd", w0,
                   lambda e: e.dma_start(out=k.d_sc[:, :], in_=k.d_scb[:, :]), [("dGsp", 16)])
            w0 = []
        if DBG3 & 1:
            k.emit("gpsimd", w0,
                   lambda e: e.dma_start(out=k.d_p0[:, :], in_=k.d_pbuf[:, :]), [("dGsp", 16)])
            w0 = []
        if DBG3 & 4:
            k.emit("gpsimd", w0,
                   lambda e: e.dma_start(out=k.d_av[:, :], in_=k.d_avb[:, :]), [("dGsp", 16)])
            w0 = []
        if DBG3 & 16:
            k.emit("gpsimd", w0,
                   lambda e: e.dma_start(out=k.d_vb[:, :], in_=A["vb"][:, 0:2 * c.VBW]),
                   [("dGsp", 16)])
            w0 = []
        if DBG3 & 8:
            k.emit("gpsimd", w0,
                   lambda e: e.dma_start(out=k.d_rb[:, :], in_=k.d_rbb[:, :]), [("dGsp", 16)])
            k.emit("gpsimd", [],
                   lambda e: e.dma_start(out=k.d_rcp[:, :], in_=k.d_rcpb[:, :]), [("dGsp", 16)])


def emit_pair_qk_part1(k, b, ch, j, A, sc_v, xtc_ok):
    c = k.cfg
    din = k.din
    q0 = ch * c.CH
    NKT = c.ST
    slot = ch * c.NP + j
    par = slot % 2
    pb = A["pb"]
    qh, kh = A["qh"], A["kh"]
    spl = "dGsplit%d" % par
    WQ = c.DMT * 128

    k.emit("sync", k.gate(f"wqkp{par}"),
           lambda e, par=par, j=j: e.dma_start(
               out=A["wqkp"][par][:, :],
               in_=k.s["wqk"][128 * j:128 * (j + 1), :]),
           [("dGw%d" % par, 16)])
    wqv = k.v("dGw%d" % par)

    for hh in range(2):
        k.emit("gpsimd", k.gate(f"qh{par}") if hh == 0 else [],
               lambda e, hh=hh, par=par, j=j, q0=q0: e.dma_start(
                   out=qh[par][hh][64:65, :],
                   in_=din["t_aug"][2 * j + hh:2 * j + hh + 1, q0:q0 + c.CH]),
               [(spl, 16)])

    pe_done = {}
    for (idx, bank, colbase) in ((0, pb[0], 0), (1, pb[1], WQ)):
        for d in range(c.DMT):
            wl = []
            if d == 0:
                wl = [("dGw%d" % par, wqv), ("sACT", xtc_ok)] + k.gate(BANKN[idx])
            k.emit("tensor", wl,
                   lambda e, d=d, bank=bank, cb=colbase, par=par: e.matmul(
                       bank[:, :], A["wqkp"][par][:, cb + 128 * d:cb + 128 * (d + 1)],
                       A["xtc"][:, c.CH * d:c.CH * (d + 1)],
                       start=(d == 0), stop=(d == c.DMT - 1)),
                   [("sPE", 1)])
        pe_done[idx] = k.v("sPE")
    k.setg(f"wqkp{par}", ("sPE", pe_done[1]))
    k.addg("xtc", ("sPE", pe_done[1]))

    for idx, bank, bcol in ((0, pb[0], j), (1, pb[1], c.NP + j)):
        qo = 512 * idx
        a = k.emit("scalar", [("sPE", pe_done[idx])] + k.gate(f"qbb_{idx}"),
                   lambda e, bank=bank, bcol=bcol, qo=qo: e.activation(
                       out=A["qbb"][:, qo:qo + c.CH], in_=bank[:, :],
                       func=AF.Identity, bias=A["bqk"][:, bcol:bcol + 1]),
                   [("sACT", 1)])
        ev = a["sACT"]
        a = k.emit("tensor", [("sACT", ev)],
                   lambda e, bank=bank, qo=qo: e.matmul(
                       bank[:, :], A["permb"][:, :], A["qbb"][:, qo:qo + c.CH],
                       start=True, stop=True),
                   [("sPE", 1)])
        pm_v = a["sPE"]
        a = k.emit("vector", [("sACT", ev), ("dLsc", sc_v)] + k.gate("t1"),
                   lambda e, qo=qo, ch=ch: e.tensor_tensor(
                       out=A["t1"][:, :], in0=A["qbb"][:, qo:qo + c.CH],
                       in1=A["scb"][ch % 2][:, 0:c.CH], op=OP.mult),
                   [("sDVE", 1)])
        t1_v = a["sDVE"]
        a = k.emit("vector", [("sPE", pm_v), ("sDVE", t1_v)] + k.gate("t2"),
                   lambda e, bank=bank, ch=ch: e.tensor_tensor(
                       out=A["t2"][:, :], in0=bank[:, :],
                       in1=A["scb"][ch % 2][:, c.CH:2 * c.CH], op=OP.mult),
                   [("sDVE", 1)])
        t2_v = a["sDVE"]
        k.setg(f"qbb_{idx}", ("sDVE", t2_v), ("sPE", pm_v))
        k.addg(BANKN[idx], ("sDVE", t2_v))
        k.addg(f"scb{ch % 2}", ("sDVE", t2_v))
        a = k.emit("vector", [("sDVE", t2_v)] + k.gate(f"rout{idx}"),
                   lambda e, idx=idx: e.tensor_tensor(out=A["rout"][idx][:, :],
                                                      in0=A["t1"][:, :],
                                                      in1=A["t2"][:, :], op=OP.add),
                   [("sDVE", 1)])
        rp_v = a["sDVE"]
        k.setg("t1", ("sDVE", rp_v))
        k.setg("t2", ("sDVE", rp_v))
        if idx == 0:
            tgts = [qh[par][0][0:64, :], qh[par][1][0:64, :]]
        else:
            tgts = [kh[2 * j][0:64, q0:q0 + c.CH], kh[2 * j + 1][0:64, q0:q0 + c.CH]]
        for z, t in enumerate(tgts):
            k.emit("gpsimd", [("sDVE", rp_v)] if z == 0 else [],
                   lambda e, t=t, z=z, idx=idx: e.dma_start(
                       out=t, in_=A["rout"][idx][64 * z:64 * (z + 1), :]),
                   [(spl, 16)])
        k.setg(f"rout{idx}", (spl, k.v(spl)))
    k.info[("split", slot)] = k.v(spl)


def emit_pair_attention(k, b, ch, j, A):
    c = k.cfg
    NKT = c.ST
    slot = ch * c.NP + j
    par = slot % 2
    pb, pav = A["pb"], A["pav"]
    probs = A["probs"]
    nkt = (ch + 1) * c.CT
    spl = "dGsplit%d" % par
    spl_v = k.info[("split", slot)]
    av_last = {}
    exp_v = {0: {}, 1: {}}

    def do_scores(kt, h, hh, first):
        diag = kt >= ch * c.CT
        off = 128 * (kt - ch * c.CT) if diag else 0
        wdt = c.CH - off
        bi = 4 + kt % 2
        sbank = pb[bi]
        wl = k.gate(BANKN[bi])
        if first:
            wl += [(spl, spl_v), ("sPL", k.info["khones"])]
        a = k.emit("tensor", wl,
                   lambda e, h=h, kt=kt, off=off, wdt=wdt, sbank=sbank, hh=hh, par=par:
                   e.matmul(sbank[:, off:off + wdt],
                            A["kh"][h][0:65, 128 * kt:128 * (kt + 1)],
                            A["qh"][par][hh][0:65, off:off + wdt],
                            start=True, stop=True),
                   [("sPE", 1)])
        sc_pe = a["sPE"]
        band_v = 0
        if diag:
            a2 = k.emit("vector", [("sPE", sc_pe)],
                        lambda e, off=off, sbank=sbank: e.tensor_tensor(
                            out=sbank[:, off:off + 128], in0=sbank[:, off:off + 128],
                            in1=A["band"][:, :], op=OP.add),
                        [("sDVE", 1)])
            band_v = a2["sDVE"]
        wl = [("sPE", sc_pe)] + ([("sDVE", band_v)] if band_v else []) \
            + k.gate(f"probs{kt % 2}")
        a3 = k.emit("scalar", wl,
                    lambda e, h=h, kt=kt, off=off, wdt=wdt, sbank=sbank:
                    e.activation(out=probs[kt % 2][:, off:off + wdt],
                                 in_=sbank[:, off:off + wdt], func=AF.Exp,
                                 bias=A["hkt"][:, h * NKT + kt:h * NKT + kt + 1],
                                 scale=c.SCALE),
                    [("sACT", 1)])
        k.addg(BANKN[bi], ("sACT", a3["sACT"]))
        exp_v[hh][kt] = a3["sACT"]

    def do_av(kt, h, hpar, avb, first, last):
        diag = kt >= ch * c.CT
        off = 128 * (kt - ch * c.CT) if diag else 0
        wdt = c.CH - off
        wl = [("sACT", max(exp_v[h % 2][kt], k.info.get("vb_ready", 0)))]
        if first:
            wl += k.gate(BANKN[6 + hpar])
        a4 = k.emit("tensor", wl,
                    lambda e, h=h, kt=kt, off=off, wdt=wdt, avb=avb, first=first, last=last:
                    e.matmul(avb[0:65, off:off + wdt],
                             A["vb"][:, c.VBW * kt + 65 * h:c.VBW * kt + 65 * h + 65],
                             probs[kt % 2][:, off:off + wdt],
                             start=first, stop=last,
                             skip_group_check=True),
                    [("sPE", 1)])
        k.addg(f"probs{kt % 2}", ("sPE", a4["sPE"]))
        return a4["sPE"]

    for hh in range(2):
        h = 2 * j + hh
        avb = pav[h % 2]
        for i, kt in enumerate(range(nkt)):
            do_scores(kt, h, hh, first=(i == 0))
            if i >= 1:
                do_av(kt - 1, h, h % 2, avb, first=(i == 1), last=False)
        av_last[hh] = do_av(nkt - 1, h, h % 2, avb, first=(nkt == 1), last=True)

    ev_hh = {}
    atcf_v = 0
    for hh in range(2):
        h = 2 * j + hh
        hpar = h % 2
        avb = pav[hpar]
        a = k.emit("scalar", [("sPE", av_last[hh])] + k.gate(f"rsrow{hh}"),
                   lambda e, avb=avb, hh=hh: e.activation(
                       out=A["rsrow"][0:1, 512 * hh:512 * (hh + 1)],
                       in_=avb[64:65, 0:c.CH], func=AF.Copy),
                   [("sACT", 1)])
        rs_v = a["sACT"]
        a = k.emit("scalar", k.gate("atcf") if hh == 0 else [],
                   lambda e, avb=avb, hh=hh: e.activation(
                       out=A["atcf"][64 * hh:64 * (hh + 1), :],
                       in_=avb[0:64, 0:c.CH], func=AF.Copy),
                   [("sACT", 1)])
        atcf_v = a["sACT"]
        k.addg(BANKN[6 + hpar], ("sACT", atcf_v))
        a = k.emit("tensor", [("sACT", rs_v)] + (k.gate(BANKN[3]) if hh == 0 else []),
                   lambda e, hh=hh: e.matmul(pb[3][64 * hh:64 * (hh + 1), 0:c.CH],
                                             A["ones2f"][0:1, 0:64],
                                             A["rsrow"][0:1, 512 * hh:512 * (hh + 1)],
                                             start=True, stop=True),
                   [("sPE", 1)])
        k.setg(f"rsrow{hh}", ("sPE", a["sPE"]))
        ev_hh[hh] = a["sPE"]
    a = k.emit("vector", [("sPE", ev_hh[1])] + k.gate("rbB"),
               lambda e: e.reciprocal(out=A["rb"][:, :], in_=pb[3][:, 0:c.CH]),
               [("sDVE", 1)])
    rcp_v = a["sDVE"]
    k.addg(BANKN[3], ("sDVE", rcp_v))
    a = k.emit("vector", [("sDVE", rcp_v), ("sACT", atcf_v)] + k.gate("atc_busy"),
               lambda e, j=j: e.tensor_tensor(out=A["atc"][j][:, :], in0=A["atcf"][:, :],
                                              in1=A["rb"][:, :], op=OP.mult),
               [("sDVE", 1)])
    am_v = a["sDVE"]
    k.setg("rbB", ("sDVE", am_v))
    k.setg("atcf", ("sDVE", am_v))
    k.addg("atc_all", ("sDVE", am_v))
    k.setg(f"qh{par}", ("sPE", k.v("sPE")))


def emit_outproj_chunk(k, b, ch, A):
    c = k.cfg
    din = k.din
    q0 = ch * c.CH
    banks = A["banks"]
    atc_all = k.gate("atc_all")

    for p in range(c.NP):
        wpar = p % 2
        k.emit("gpsimd", k.gate(f"wob{wpar}") + (atc_all if p == 0 else []),
               lambda e, p=p, wpar=wpar: e.dma_start(
                   out=A["wob"][wpar][:, :], in_=din["wout"][128 * p:128 * (p + 1), :]),
               [("dGo%d" % wpar, 16)])
        gov = k.v("dGo%d" % wpar)
        for tt in range(c.CT):
            for dmc in range(2):
                g = 2 * tt + dmc
                wl = []
                if tt == 0 and dmc == 0:
                    wl = [("dGo%d" % wpar, gov)]
                    if p == 0:
                        wl += atc_all + k.gates(*BANKN)
                k.emit("tensor", wl,
                       lambda e, p=p, tt=tt, dmc=dmc, g=g, wpar=wpar: e.matmul(
                           banks[g][:, :], A["atc"][p][:, 128 * tt:128 * (tt + 1)],
                           A["wob"][wpar][:, 512 * dmc:512 * (dmc + 1)],
                           start=(p == 0), stop=False, skip_group_check=True),
                       [("sPE", 1)])
        k.setg(f"wob{wpar}", ("sPE", k.v("sPE")))
    for tt in range(c.CT):
        for dmc in range(2):
            g = 2 * tt + dmc
            k.emit("tensor", [("dGaux", k.auxv)] if g == 0 else [],
                   lambda e, g=g, dmc=dmc: e.matmul(
                       banks[g][:, :], A["ones_row"][0:1, 0:128],
                       A["boutb"][0:1, 512 * dmc:512 * (dmc + 1)],
                       start=False, stop=True, skip_group_check=True),
                   [("sPE", 1)])
    pe_op = k.v("sPE")
    k.setg("atc_busy", ("sPE", pe_op))
    k.g.pop("atc_all", None)

    for tt in range(c.CT):
        par = tt % 2
        a = k.emit("sync", k.gate(f"xpc{par}"),
                   lambda e, tt=tt, par=par: e.dma_start(
                       out=A["xpc"][par][:, :],
                       in_=din["x"][b, q0 + 128 * tt:q0 + 128 * (tt + 1), :]),
                   [("dLxr%d" % par, 16)])
        lxr = k.v("dLxr%d" % par)
        tt_done = 0
        for dmc in range(2):
            g = 2 * tt + dmc
            a = k.emit("vector", [("sPE", pe_op), ("dLxr%d" % par, lxr)] +
                       (k.gate("y1t") if dmc == 0 else []),
                       lambda e, tt=tt, dmc=dmc, g=g, par=par: e.tensor_tensor(
                           out=A["y1t"][:, 512 * dmc:512 * (dmc + 1)],
                           in0=banks[g][:, :],
                           in1=A["xpc"][par][:, 512 * dmc:512 * (dmc + 1)], op=OP.add),
                       [("sDVE", 1)])
            tt_done = a["sDVE"]
            k.addg(BANKN[g], ("sDVE", tt_done))
        k.setg(f"xpc{par}", ("sDVE", tt_done))
        a = k.emit("vector", [("sDVE", tt_done)] + k.gate("bns"),
                   lambda e: e.bn_stats(out=A["bns"][:, 0:6], in_=A["y1t"][:, 0:512]),
                   [("sDVE", 1)])
        a = k.emit("vector", [("sDVE", a["sDVE"])],
                   lambda e: e.bn_stats(out=A["bns"][:, 6:12], in_=A["y1t"][:, 512:1024]),
                   [("sDVE", 1)])
        a = k.emit("vector", [("sDVE", a["sDVE"])] + k.gate("lnt"),
                   lambda e: e.bn_aggr(out=A["lnt"][:, 0:2], in_=A["bns"][:, :]),
                   [("sDVE", 1)])
        k.setg("bns", ("sDVE", a["sDVE"]))
        st_v = a["sDVE"]
        a = k.emit("scalar", [("sDVE", st_v)],
                   lambda e: e.activation(out=A["lnt"][:, 2:3], in_=A["lnt"][:, 1:2],
                                          func=AF.Ln, bias=EPS), [("sACT", 1)])
        a = k.emit("scalar", [("sACT", a["sACT"])],
                   lambda e: e.activation(out=A["lnt"][:, 3:4], in_=A["lnt"][:, 1:2],
                                          func=AF.Ln, bias=EPS * EPS, scale=1.0 + EPS),
                   [("sACT", 1)])
        a = k.emit("scalar", [("sACT", a["sACT"])],
                   lambda e: e.activation(out=A["lnt"][:, 5:6], in_=A["lnt"][:, 2:3],
                                          func=AF.Exp, scale=-0.5), [("sACT", 1)])
        a = k.emit("vector", [("sACT", a["sACT"]), ("sDVE", st_v)],
                   lambda e: e.tensor_tensor(out=A["lnt"][:, 4:5], in0=A["lnt"][:, 2:3],
                                             in1=A["lnt"][:, 3:4], op=OP.subtract),
                   [("sDVE", 1)])
        a = k.emit("scalar", [("sDVE", a["sDVE"]), ("sACT", a.get("sACT", 0) if isinstance(a, dict) else 0)],
                   lambda e: e.activation(out=A["lnt"][:, 6:7], in_=A["lnt"][:, 4:5],
                                          func=AF.Exp, scale=0.5), [("sACT", 1)])
        a = k.emit("vector", [("sACT", a["sACT"])],
                   lambda e: e.tensor_tensor(out=A["lnt"][:, 7:8], in0=A["lnt"][:, 5:6],
                                             in1=A["lnt"][:, 6:7], op=OP.mult),
                   [("sDVE", 1)])
        r2_v = a["sDVE"]
        a = k.emit("vector", [("sDVE", r2_v)] + k.gate("x1s"),
                   lambda e: e.tensor_scalar(out=A["x1s"][:, :], in0=A["y1t"][:, :],
                                             scalar1=A["lnt"][:, 0:1],
                                             scalar2=A["lnt"][:, 5:6],
                                             op0=OP.subtract, op1=OP.mult),
                   [("sDVE", 1)])
        x1_v = a["sDVE"]
        a2 = k.emit("vector", [("sDVE", r2_v)] + k.gate("xh2s"),
                    lambda e: e.tensor_scalar(out=A["xh2s"][:, :], in0=A["y1t"][:, :],
                                              scalar1=A["lnt"][:, 0:1],
                                              scalar2=A["lnt"][:, 7:8],
                                              op0=OP.subtract, op1=OP.mult),
                    [("sDVE", 1)])
        k.setg("lnt", ("sDVE", a2["sDVE"]))
        k.setg("y1t", ("sDVE", a2["sDVE"]))
        r0 = q0 + 128 * tt
        a3 = k.emit("gpsimd", [("sDVE", x1_v)],
                    lambda e, r0=r0: e.dma_start(out=k.s["x1"][b, r0:r0 + 128, :],
                                                 in_=A["x1s"][:, :]),
                    [("dGsp", 16)])
        a3 = k.emit("gpsimd", [("sDVE", a2["sDVE"])],
                    lambda e, r0=r0: e.dma_start(out=k.s["xh2"][b, r0:r0 + 128, :],
                                                 in_=A["xh2s"][:, :]),
                    [("dGsp", 16)])
        k.setg("x1s", ("dGsp", a3["dGsp"]))
        k.setg("xh2s", ("dGsp", a3["dGsp"]))


def emit_F(k, b, F):
    c = k.cfg
    din = k.din
    pb = F["pb"]
    spill_v = k.v("dGsp")

    def load_wgrp(half, nt, wl_extra):
        par = nt % 2
        wsem = "dLw%d" % par
        for d in range(c.DMT):
            k.emit("sync", (wl_extra + k.gate(f"wgrp{nt}")) if d == 0 else [],
                   lambda e, nt=nt, d=d, half=half: e.dma_start(
                       out=F["w1h"][:, (d * c.FNTH + nt) * 128:(d * c.FNTH + nt) * 128 + 128],
                       in_=k.s["w1"][128 * d:128 * (d + 1),
                                     c.FH * half + 128 * nt:c.FH * half + 128 * (nt + 1)]),
                   [(wsem, 16)])
        k.emit("sync", [],
               lambda e, nt=nt, half=half: e.dma_start(
                   out=F["w2h"][:, c.DM * nt:c.DM * (nt + 1)],
                   in_=k.s["w2"][c.FH * half + 128 * nt:c.FH * half + 128 * (nt + 1), :]),
               [(wsem, 16)])
        return k.v(wsem)

    for half in range(2):
        grp_v = {}
        grp_v[0] = load_wgrp(half, 0, [("dGsw0", k.v("dGsw0")), ("dGsw1", k.v("dGsw1"))] if (half == 0 and b == 0) else [])
        if c.FNTH > 1:
            grp_v[1] = load_wgrp(half, 1, [])

        for tc in range(c.NTC):
            r0 = tc * c.TCH
            par = tc % 2
            for d in range(c.DMT):
                for h2 in range(2):
                    wl = k.gate(f"xh2T{par}") if (d == 0 and h2 == 0) else []
                    if d == 0 and h2 == 0 and half == 0 and tc == 0:
                        wl += [("dGsp", spill_v)]
                    k.emit("sync", wl,
                           lambda e, d=d, h2=h2, par=par, r0=r0: e.dma_start_transpose(
                               out=F["xh2T"][par][:, c.TCH * d + 128 * h2:
                                                  c.TCH * d + 128 * (h2 + 1)],
                               in_=k.s["xh2"][b, r0 + 128 * h2:r0 + 128 * (h2 + 1),
                                              128 * d:128 * (d + 1)]),
                           [("dLxh%d" % par, 16)])
            xh_v = k.v("dLxh%d" % par)
            x1_v = 0
            if half == 1:
                for h2 in range(2):
                    k.emit("sync", (k.gate(f"xr{par}") if h2 == 0 else []),
                           lambda e, h2=h2, par=par, r0=r0: e.dma_start(
                               out=F["xr"][par][:, c.DM * h2:c.DM * (h2 + 1)],
                               in_=k.s["x1"][b, r0 + 128 * h2:r0 + 128 * (h2 + 1), :]),
                           [("dLx1%d" % par, 16)])
                x1_v = k.v("dLx1%d" % par)
            silu_done = {}

            def emit_fc1(nt):
                bi = nt % 2
                bank = pb[bi]
                for d in range(c.DMT):
                    wl = []
                    if d == 0:
                        wl = [("dLw%d" % (nt % 2), grp_v[nt]),
                              ("dLxh%d" % par, xh_v)] + k.gate(BANKN[bi])
                    k.emit("tensor", wl,
                           lambda e, nt=nt, d=d, bank=bank, par=par: e.matmul(
                               bank[:, 0:c.TCH],
                               F["w1h"][:, (d * c.FNTH + nt) * 128:
                                        (d * c.FNTH + nt) * 128 + 128],
                               F["xh2T"][par][:, c.TCH * d:c.TCH * (d + 1)],
                               start=(d == 0), stop=(d == c.DMT - 1)),
                           [("sPE", 1)])
                fc1_v = k.v("sPE")
                if tc == 0 and nt + 2 < c.FNTH:
                    grp_v[nt + 2] = load_wgrp(half, nt + 2, [])
                if nt == c.FNTH - 1:
                    k.setg(f"xh2T{par}", ("sPE", fc1_v))
                ncol = half * c.FNTH + nt
                a = k.emit("scalar", [("sPE", fc1_v), ("dLtab", k.v("dLtab"))] +
                           k.gate(f"yT{nt % 4}"),
                           lambda e, nt=nt, bank=bank, ncol=ncol: e.activation(
                               out=F["yT"][:, c.TCH * nt:c.TCH * (nt + 1)],
                               in_=bank[:, 0:c.TCH], func=AF.Silu,
                               bias=F["fc1bcol"][:, ncol:ncol + 1]),
                           [("sACT", 1)])
                silu_done[nt] = a["sACT"]
                k.addg(BANKN[bi], ("sACT", a["sACT"]))

            def emit_fc2(nt):
                for tt2 in range(2):
                    for dmc in range(2):
                        g = 2 + 2 * tt2 + dmc
                        wl = [("sACT", silu_done[nt])] if (tt2 == 0 and dmc == 0) else []
                        if nt == 0 and tt2 == 0 and dmc == 0:
                            wl += k.gates(*BANKN[2:6])
                        k.emit("tensor", wl,
                               lambda e, nt=nt, tt2=tt2, dmc=dmc, g=g: e.matmul(
                                   pb[g][:, :],
                                   F["yT"][:, c.TCH * nt + 128 * tt2:
                                           c.TCH * nt + 128 * (tt2 + 1)],
                                   F["w2h"][:, c.DM * nt + 512 * dmc:
                                            c.DM * nt + 512 * (dmc + 1)],
                                   start=(nt == 0), stop=False, skip_group_check=True),
                               [("sPE", 1)])
                if tc == c.NTC - 1:
                    k.setg(f"wgrp{nt}", ("sPE", k.v("sPE")))
                k.setg(f"yT{nt % 4}", ("sPE", k.v("sPE")))

            emit_fc1(0)
            for nt in range(c.FNTH):
                if nt + 1 < c.FNTH:
                    emit_fc1(nt + 1)
                emit_fc2(nt)
            if half == 0:
                for tt2 in range(2):
                    for dmc in range(2):
                        g = 2 + 2 * tt2 + dmc
                        k.emit("tensor", [("dGaux", k.auxv)] if g == 2 else [],
                               lambda e, g=g, dmc=dmc: e.matmul(
                                   pb[g][:, :], F["ones_row"][0:1, 0:128],
                                   F["frow2"][0:1, 512 * dmc:512 * (dmc + 1)],
                                   start=False, stop=True, skip_group_check=True),
                               [("sPE", 1)])
                z_pe = k.v("sPE")
                for tt2 in range(2):
                    for dmc in range(2):
                        g = 2 + 2 * tt2 + dmc
                        zc = (2 * tc + tt2) * c.DM + 512 * dmc
                        a = k.emit("scalar",
                                   [("sPE", z_pe)] if (tt2 == 0 and dmc == 0) else [],
                                   lambda e, g=g, zc=zc: e.activation(
                                       out=F["zacc"][:, zc:zc + 512], in_=pb[g][:, :],
                                       func=AF.Copy),
                                   [("sACT", 1)])
                        k.addg(BANKN[g], ("sACT", a["sACT"]))
                k.info[("zev", tc)] = k.v("sACT")
            else:
                z_pe = k.v("sPE")
                for tt2 in range(2):
                    t2_v = 0
                    for dmc in range(2):
                        g = 2 + 2 * tt2 + dmc
                        zc = (2 * tc + tt2) * c.DM + 512 * dmc
                        wl = []
                        if dmc == 0:
                            wl = [("sPE", z_pe), ("sACT", k.info[("zev", tc)]),
                                  ("dLx1%d" % par, x1_v)] + k.gate("tfin")
                        a = k.emit("vector", wl,
                                   lambda e, g=g, zc=zc, dmc=dmc, tt2=tt2, par=par:
                                   e.scalar_tensor_tensor(
                                       out=F["tfin"][:, 512 * dmc:512 * (dmc + 1)],
                                       in0=F["zacc"][:, zc:zc + 512], scalar=0.25,
                                       in1=F["xr"][par][:, c.DM * tt2 + 512 * dmc:
                                                        c.DM * tt2 + 512 * (dmc + 1)],
                                       op0=OP.mult, op1=OP.add),
                                   [("sDVE", 1)])
                        a = k.emit("vector", [("sDVE", a["sDVE"])] + (k.gate("tfin2") if dmc == 0 else []),
                                   lambda e, g=g, dmc=dmc: e.scalar_tensor_tensor(
                                       out=F["tfin2"][:, 512 * dmc:512 * (dmc + 1)],
                                       in0=pb[g][:, :], scalar=0.25,
                                       in1=F["tfin"][:, 512 * dmc:512 * (dmc + 1)],
                                       op0=OP.mult, op1=OP.add),
                                   [("sDVE", 1)])
                        t2_v = a["sDVE"]
                        k.addg(BANKN[g], ("sDVE", t2_v))
                    k.setg("tfin", ("sDVE", t2_v))
                    if tt2 == 1:
                        k.setg(f"xr{par}", ("sDVE", t2_v))
                    a = k.emit("vector", [("sDVE", t2_v)] + k.gate("bns"),
                               lambda e: e.bn_stats(out=F["bns"][:, 0:6],
                                                    in_=F["tfin2"][:, 0:512]),
                               [("sDVE", 1)])
                    a = k.emit("vector", [("sDVE", a["sDVE"])],
                               lambda e: e.bn_stats(out=F["bns"][:, 6:12],
                                                    in_=F["tfin2"][:, 512:1024]),
                               [("sDVE", 1)])
                    a = k.emit("vector", [("sDVE", a["sDVE"])] + k.gate("lnt"),
                               lambda e: e.bn_aggr(out=F["lnt"][:, 0:2],
                                                   in_=F["bns"][:, :]),
                               [("sDVE", 1)])
                    st_v = a["sDVE"]
                    k.setg("bns", ("sDVE", st_v))
                    a = k.emit("scalar", [("sDVE", st_v)],
                               lambda e: e.activation(out=F["lnt"][:, 2:3],
                                                      in_=F["lnt"][:, 1:2],
                                                      func=AF.Ln, bias=EPS),
                               [("sACT", 1)])
                    a = k.emit("scalar", [("sACT", a["sACT"])],
                               lambda e: e.activation(out=F["lnt"][:, 5:6],
                                                      in_=F["lnt"][:, 2:3],
                                                      func=AF.Exp, scale=-0.5),
                               [("sACT", 1)])
                    opar = (2 * tc + tt2) % 2
                    a = k.emit("vector", [("sACT", a["sACT"]), ("sDVE", st_v)] + k.gate(f"outst{opar}"),
                               lambda e, opar=opar: e.tensor_scalar(
                                   out=F["outst"][opar][:, :], in0=F["tfin2"][:, :],
                                   scalar1=F["lnt"][:, 0:1], scalar2=F["lnt"][:, 5:6],
                                   op0=OP.subtract, op1=OP.mult),
                               [("sDVE", 1)])
                    k.setg("tfin2", ("sDVE", a["sDVE"]))
                    k.setg("lnt", ("sDVE", a["sDVE"]))
                    r1 = r0 + 128 * tt2
                    a2 = k.emit("gpsimd", [("sDVE", a["sDVE"])],
                                lambda e, r1=r1, opar=opar: e.dma_start(
                                    out=k.y[b, r1:r1 + 128, :],
                                    in_=F["outst"][opar][:, :]),
                                [("dGout", 16)])
                    if tt2 == 1:
                        k.setg("outst0", ("dGout", a2["dGout"]))
                        k.setg("outst1", ("dGout", a2["dGout"]))


def kernel(L0_attn_x, L0_attn_w_qkv, L0_attn_b_qkv, L0_attn_w_out, L0_attn_b_out,
           L0_ffn_ln_gamma, L0_ffn_ln_beta, L0_ffn_fc1_q, L0_ffn_fc1_scales,
           L0_ffn_fc1_bias, L0_ffn_fc2_q, L0_ffn_fc2_scales, L0_ffn_fc2_bias):
    cfg = Cfg()
    tabs = host_tables(cfg)
    wd = host_prep(cfg, L0_attn_w_qkv, L0_attn_b_qkv, L0_attn_w_out, L0_attn_b_out,
                   L0_ffn_ln_gamma, L0_ffn_ln_beta, L0_ffn_fc1_q, L0_ffn_fc1_scales,
                   L0_ffn_fc1_bias, L0_ffn_fc2_q, L0_ffn_fc2_scales, L0_ffn_fc2_bias)
    x = np.ascontiguousarray(np.asarray(L0_attn_x, dtype=np.float32))
    kk = build(cfg)
    shared = dict(tabs)
    shared.update(wd)
    in_maps = []
    for core in range(N_CORES):
        m = dict(shared)
        m["x"] = np.ascontiguousarray(x[4 * core:4 * core + 4])
        in_maps.append(m)
    res = run_bass_kernel_spmd(kk.nc, in_maps, core_ids=list(range(N_CORES)))
    return np.concatenate([r["y"] for r in res.results], axis=0)


# version 37 (final)
# speedup vs baseline: 1.1680x; 1.1680x over previous
import math
from contextlib import ExitStack

import numpy as np
import ml_dtypes

import concourse.bass as bass
import concourse.mybir as mybir
from concourse.bass_utils import run_bass_kernel_spmd

F32 = mybir.dt.float32
BF16 = mybir.dt.bfloat16
I32 = mybir.dt.int32
AF = mybir.ActivationFunctionType
OP = mybir.AluOpType
AX = mybir.AxisListType
bfdt = ml_dtypes.bfloat16

N_CORES = 8
DBG = False
DBG2 = False
DBG3 = 0
XBAR = 0
EPS = 1e-5
MAX_BIAS = 8.0
NEGBIG = -1.0e9
BANKN = ["pb0", "pb1", "pb2", "pb3", "pb4", "pb5", "pav0", "pav1"]


class Cfg:
    def __init__(self, B=4, S=2048, DM=1024, H=16, FF=4096):
        self.B, self.S, self.DM, self.H, self.FF = B, S, DM, H, FF
        self.DH = 64
        self.NP = H // 2
        self.DMT = DM // 128
        self.CH = 512
        self.NCH = S // self.CH
        self.CT = self.CH // 128
        self.ST = S // 128
        self.G = 128
        self.FH = FF // 2
        self.FNTH = self.FH // 128
        self.TCH = 256
        self.NTC = S // self.TCH
        self.VBW = 65 * H
        self.SCALE = 1.0 / math.sqrt(192.0)


def _slopes(n):
    def pow2(n):
        start = 2.0 ** (-(2.0 ** (-(math.log2(n) - 3))))
        return [start * start ** i for i in range(n)]
    if math.log2(n).is_integer():
        return pow2(n)
    cp2 = 2 ** math.floor(math.log2(n))
    return pow2(cp2) + _slopes(2 * cp2)[0::2][: n - cp2]


def host_tables(cfg):
    S, H, DH = cfg.S, cfg.H, cfg.DH
    sl = np.array(_slopes(H), dtype=np.float64)
    pos = np.arange(S, dtype=np.float64)
    inv_freq = 1.0 / 10000.0 ** (np.arange(0, DH, 2, dtype=np.float64) / DH)
    fr = np.outer(pos, inv_freq)
    sin = np.concatenate([np.sin(fr), np.sin(fr)], axis=1).T
    cos = np.concatenate([np.cos(fr), np.cos(fr)], axis=1).T
    sin_s = sin.copy()
    sin_s[: DH // 2] = -sin_s[: DH // 2]
    cos2 = np.concatenate([cos, cos], axis=0)
    sin2 = np.concatenate([sin_s, sin_s], axis=0)
    t_sincos = np.ascontiguousarray(
        np.concatenate([cos2, sin2], axis=1).astype(np.float32))
    NKT = cfg.ST
    t_hkt = np.zeros((128, H * NKT), dtype=np.float64)
    dk = np.arange(128, dtype=np.float64)
    for h in range(H):
        for kt in range(NKT):
            t_hkt[:, h * NKT + kt] = MAX_BIAS * sl[h] * (128 * kt + dk) - 24.0
    q = np.arange(S, dtype=np.float64)
    t_aug = (-math.sqrt(192.0) * MAX_BIAS * sl[:, None] * q[None, :]).astype(bfdt)
    t_band = np.where(dk[:, None] > dk[None, :], np.float32(NEGBIG),
                      np.float32(0.0)).astype(np.float32)
    t_perm = np.zeros((128, 128), dtype=np.float32)
    for (do, so) in ((0, 32), (32, 0), (64, 96), (96, 64)):
        for i in range(32):
            t_perm[so + i, do + i] = 1.0
    return dict(t_sincos=t_sincos, t_hkt=t_hkt.astype(np.float32),
                t_aug=np.ascontiguousarray(t_aug), t_band=t_band,
                t_ones_row=np.ones((1, 128), dtype=bfdt),
                t_ones2=np.ones((2, 64), dtype=bfdt),
                t_ones_col=np.ones((128, 1), dtype=bfdt),
                t_perm=np.ascontiguousarray(t_perm.astype(bfdt)),
                t_ident=np.eye(128, dtype=np.float32))


def host_prep(cfg, w_qkv, b_qkv, w_out, b_out, ln_gamma, ln_beta,
              fc1_q, fc1_scales, fc1_bias, fc2_q, fc2_scales, fc2_bias):
    DM, H = cfg.DM, cfg.H
    w_qkv = np.asarray(w_qkv)
    wq = np.asarray(w_qkv[:, :, 0:64], dtype=np.float32)
    wk = np.asarray(w_qkv[:, :, 64:128], dtype=np.float32)
    wq5 = wq.reshape(8, 128, 8, 2, 64).transpose(2, 1, 0, 3, 4).reshape(8, 128, 1024)
    wk5 = wk.reshape(8, 128, 8, 2, 64).transpose(2, 1, 0, 3, 4).reshape(8, 128, 1024)
    wqkA = np.ascontiguousarray(
        np.concatenate([wq5, wk5], axis=2).reshape(8 * 128, 2048))
    wv = np.zeros((DM, 65 * H), dtype=np.float32)
    for h in range(H):
        wv[:, 65 * h:65 * h + 64] = w_qkv[:, h, 128:192]
    b_qkv = np.asarray(b_qkv)
    bq = np.zeros((128, 2 * cfg.NP), dtype=np.float32)
    for j in range(cfg.NP):
        bq[0:64, j] = b_qkv[2 * j, 0:64]
        bq[64:128, j] = b_qkv[2 * j + 1, 0:64]
        bq[0:64, cfg.NP + j] = b_qkv[2 * j, 64:128]
        bq[64:128, cfg.NP + j] = b_qkv[2 * j + 1, 64:128]
    bva = np.zeros((1, 65 * H), dtype=np.float32)
    for h in range(H):
        bva[0, 65 * h:65 * h + 64] = b_qkv[h, 128:192]
        bva[0, 65 * h + 64] = 1.0
    fc1_q = np.asarray(fc1_q, dtype=np.int32)
    fc2_q = np.asarray(fc2_q, dtype=np.int32)
    fc1qt = np.ascontiguousarray(
        fc1_q.reshape(8, 128, 8, 512).transpose(2, 0, 1, 3).reshape(64 * 128, 512))
    fc2qt = np.ascontiguousarray(
        fc2_q.reshape(32, 128, 2, 512).transpose(0, 2, 1, 3).reshape(64 * 128, 512))
    return dict(
        wqkA=wqkA, wvaug=wv, bqk_t=bq, bvaug=bva, fc1qt=fc1qt, fc2qt=fc2qt,
        wout=np.ascontiguousarray(np.asarray(w_out).reshape(H * 64, DM), dtype=np.float32),
        bout=np.ascontiguousarray(np.asarray(b_out).reshape(1, DM), dtype=np.float32),
        gammacol=np.ascontiguousarray(np.asarray(ln_gamma).reshape(cfg.DMT, 128).T, dtype=np.float32),
        betacol=np.ascontiguousarray(np.asarray(ln_beta).reshape(cfg.DMT, 128).T, dtype=np.float32),
        fc1sc=np.ascontiguousarray(np.asarray(fc1_scales), dtype=np.float32),
        fc1b=np.ascontiguousarray(np.asarray(fc1_bias).reshape(1, cfg.FF), dtype=np.float32),
        fc2sc=np.ascontiguousarray(np.asarray(fc2_scales), dtype=np.float32),
        fc2b=np.ascontiguousarray(np.asarray(fc2_bias).reshape(1, cfg.DM), dtype=np.float32))


ENGS = ("tensor", "scalar", "vector", "sync", "gpsimd")
BARS = {"tensor": "bar_pe", "scalar": "bar_act", "vector": "bar_dve",
        "sync": "bar_sp", "gpsimd": "bar_pl"}


class K:
    def __init__(self, cfg):
        self.cfg = cfg
        self.nc = bass.Bass("TRN2", target_bir_lowering=False)
        self.stack = ExitStack()
        self.sh = {}
        self.scnt = {}
        self.barn = 0
        self.blk = None
        self._bufs = {}
        self.g = {}
        self.info = {}

    def sem(self, name):
        self.sh[name] = self.stack.enter_context(self.nc.semaphore(name))
        self.scnt[name] = 0

    def v(self, name):
        return self.scnt[name]

    def flush(self):
        for engname in ENGS:
            fns = self._bufs.get(engname)
            if fns:
                getattr(self.blk, engname)(lambda eng, fns=fns: [f(eng) for f in fns])
                self._bufs[engname] = []

    def on(self, engine, fn):
        self._bufs.setdefault(engine, []).append(fn)

    def emit(self, engine, waits, mk, incs=()):
        after = {}
        for s, a in incs:
            self.scnt[s] += a
            after[s] = self.scnt[s]
        wd = {}
        for s, v0 in waits:
            if v0 > 0:
                wd[s] = max(wd.get(s, 0), v0)
        sh = self.sh

        def fn(eng, wd=wd, mk=mk, incs=incs):
            for s, v0 in wd.items():
                eng.wait_ge(sh[s], v0)
            ins = mk(eng)
            for s, a in incs:
                if ins is None:
                    eng.sem_inc(sh[s], a)
                else:
                    ins.then_inc(sh[s], a)
        self.on(engine, fn)
        return after

    def gate(self, name):
        return list(self.g.get(name, {}).items())

    def gates(self, *names):
        out = []
        for n in names:
            out += self.gate(n)
        return out

    def setg(self, name, *pairs):
        d = {}
        for s, v0 in pairs:
            if v0 > 0:
                d[s] = max(d.get(s, 0), v0)
        self.g[name] = d

    def addg(self, name, *pairs):
        d = self.g.setdefault(name, {})
        for s, v0 in pairs:
            if v0 > 0:
                d[s] = max(d.get(s, 0), v0)

    def barrier(self, drain=True):
        self.barn += 1
        n = self.barn
        qsems = {"sync": [s for s in self.sh if s.startswith("dL") or s.startswith("dT")],
                 "gpsimd": [s for s in self.sh if s.startswith("dG")],
                 "scalar": [s for s in self.sh if s.startswith("dA")]}
        for e in ENGS:
            drains = [(s, self.scnt[s]) for s in qsems.get(e, []) if self.scnt[s] > 0] \
                if drain else []
            def f(eng, e=e, n=n, drains=drains):
                for s, v0 in drains:
                    eng.wait_ge(self.sh[s], v0)
                eng.sem_inc(self.sh[BARS[e]], 1)
                for e2 in ENGS:
                    if e2 != e:
                        eng.wait_ge(self.sh[BARS[e2]], n)
            self.on(e, f)
        for e in ENGS:
            self.scnt[BARS[e]] += 1
        self.flush()

    def sb(self, name, shape, dtype, stack=None):
        return (stack or self.stack).enter_context(self.nc.sbuf_tensor(name, shape, dtype))

    def ps(self, name, shape, dtype=F32, stack=None):
        return (stack or self.stack).enter_context(self.nc.psum_tensor(name, shape, dtype))


def build(cfg):
    k = K(cfg)
    nc = k.nc
    c = cfg
    NKT = c.ST

    din = {}
    def I(name, shape, dt=F32):
        din[name] = nc.dram_tensor(name, shape, dt, kind="ExternalInput").ap()
    I("x", [c.B, c.S, c.DM])
    I("wqkA", [c.NP * 128, 2048]); I("wvaug", [c.DM, 65 * c.H])
    I("bqk_t", [128, 2 * c.NP]); I("bvaug", [1, 65 * c.H])
    I("wout", [c.H * 64, c.DM]); I("bout", [1, c.DM])
    I("gammacol", [128, c.DMT]); I("betacol", [128, c.DMT])
    I("fc1qt", [64 * 128, 512], I32); I("fc1sc", [c.DM // c.G, c.FF]); I("fc1b", [1, c.FF])
    I("fc2qt", [64 * 128, 512], I32); I("fc2sc", [c.FF // c.G, c.DM]); I("fc2b", [1, c.DM])
    I("t_sincos", [128, 2 * c.S]); I("t_hkt", [128, c.H * NKT])
    I("t_aug", [c.H, c.S], BF16); I("t_band", [128, 128])
    I("t_ones_row", [1, 128], BF16); I("t_ones2", [2, 64], BF16)
    I("t_ones_col", [128, 1], BF16)
    I("t_perm", [128, 128], BF16)
    I("t_ident", [128, 128])
    y = nc.dram_tensor("y", [c.B, c.S, c.DM], F32, kind="ExternalOutput").ap()
    s_w1 = nc.dram_tensor("s_w1", [c.DM, c.FF], BF16, kind="Internal").ap()
    s_wqk = nc.dram_tensor("s_wqk", [c.NP * 128, 2048], BF16, kind="Internal").ap()
    s_w2 = nc.dram_tensor("s_w2", [c.FF, c.DM], BF16, kind="Internal").ap()
    s_fc1b = nc.dram_tensor("s_fc1b", [1, c.FF], F32, kind="Internal").ap()
    s_x1 = nc.dram_tensor("s_x1", [c.B, c.S, c.DM], F32,
                          kind="ExternalOutput" if DBG else "Internal").ap()
    s_xh2 = nc.dram_tensor("s_xh2", [c.B, c.S, c.DM], BF16,
                           kind="ExternalOutput" if DBG else "Internal").ap()
    d_atc = None
    if DBG:
        d_atc = nc.dram_tensor("d_atc", [c.NCH, c.NP, 128, c.CH], BF16,
                               kind="ExternalOutput").ap()
        if DBG2:
            if DBG3 & 2:
                k.d_sc = nc.dram_tensor("d_sc", [128, 512], F32, kind="ExternalOutput").ap()
                k.d_scb = k.sb("d_scb", [128, 512], F32)
            if DBG3 & 1:
                k.d_p0 = nc.dram_tensor("d_p0", [128, 512], BF16, kind="ExternalOutput").ap()
                k.d_pbuf = k.sb("d_pbuf", [128, 512], BF16)
            if DBG3 & 4:
                k.d_av = nc.dram_tensor("d_av", [128, 512], F32, kind="ExternalOutput").ap()
                k.d_avb = k.sb("d_avb", [128, 512], F32)
            if DBG3 & 16:
                k.d_vb = nc.dram_tensor("d_vb", [128, 2080], BF16, kind="ExternalOutput").ap()
            if DBG3 & 8:
                k.d_rb = nc.dram_tensor("d_rb", [64, 512], F32, kind="ExternalOutput").ap()
                k.d_rcp = nc.dram_tensor("d_rcp", [1, 512], F32, kind="ExternalOutput").ap()
                k.d_rbb = k.sb("d_rbb", [64, 512], F32)
                k.d_rcpb = k.sb("d_rcpb", [1, 512], F32)
    k.d_atc = d_atc
    k.din, k.y = din, y
    k.s = dict(w1=s_w1, w2=s_w2, fc1b=s_fc1b, x1=s_x1, xh2=s_xh2, wqk=s_wqk)

    S_ = {}
    S_["hkt"] = k.sb("hkt", [128, c.H * NKT], F32)
    S_["band"] = k.sb("band", [128, 128], F32)
    S_["ones_row"] = k.sb("ones_row", [1, 128], BF16)
    S_["ones2"] = k.sb("ones2", [2, 64], BF16)
    S_["ones_col"] = k.sb("ones_col", [128, 1], BF16)
    S_["bqk"] = k.sb("bqk", [128, 2 * c.NP], F32)
    S_["wvs"] = k.sb("wvs", [128, c.DMT * c.VBW], BF16)
    S_["identf"] = k.sb("identf", [128, 128], F32)
    S_["bvaugb"] = k.sb("bvaugb", [1, 65 * c.H], BF16)
    S_["boutb"] = k.sb("boutb", [1, c.DM], BF16)
    S_["frow2"] = k.sb("frow2", [1, c.DM], BF16)
    S_["fc1bcol"] = k.sb("fc1bcol", [128, c.FF // 128], F32)
    
    S_["permb"] = k.sb("permb", [128, 128], BF16)
    S_["probs"] = [k.sb(f"probs{i}", [128, 512], BF16) for i in range(2)]
    S_["bns"] = k.sb("bns", [128, 12], F32)
    S_["lnt"] = k.sb("lnt", [128, 8], F32)

    S_["ones2f"] = k.sb("ones2f", [1, 64], F32)
    cst = k.sb("cst", [128, 4], F32)
    for i_, v_ in enumerate([EPS, EPS * EPS, 1.0 + EPS, -128.0]):
        nc.const_aps.register(F32, v_, cst[:, i_:i_ + 1])
    S_["cst"] = cst
    pbk = [k.ps(f"pb{i}", [128, 512]) for i in range(6)]
    pavk = [k.ps(f"pav{i}", [128, 512]) for i in range(2)]
    S_["pb"], S_["pav"] = pbk, pavk
    S_["banks"] = pbk + pavk

    for e in ENGS:
        k.sem(BARS[e])
    for s in ["sPE", "sACT", "sDVE", "sPL",
              "dLx0", "dLx1", "dLsc", "dLtab", "dLq0", "dLq1", "dLq2", "dLq3",
              "dGq2", "dGq3", "dLfb",
              "dTx0", "dTx1", "dLxr0", "dLxr1", "dLw0", "dLw1",
              "dLxh0", "dLxh1", "dLx10", "dLx11",
              "dGw0", "dGw1", "dGv", "dGsw0", "dGsw1", "dGsb", "dGsp", "dGout", "dGq0", "dGq1",
              "dGsplit0", "dGsplit1", "dGo0", "dGo1", "dGaux"]:
        k.sem(s)

    with nc.Block() as blk:
        k.blk = blk
        for i_, v_ in enumerate([EPS, EPS * EPS, 1.0 + EPS, -128.0]):
            k.emit("vector", [], lambda e, i_=i_, v_=v_: e.memset(cst[:, i_:i_ + 1], v_),
                   [("sDVE", 1)])
        k.emit("vector", [], lambda e: e.memset(S_["ones2f"][:, :], 1.0), [("sDVE", 1)])
        for (dst, src) in [(S_["hkt"][:, :], din["t_hkt"][:, :]),
                           (S_["band"][:, :], din["t_band"][:, :]),
                           (S_["ones_row"][:, :], din["t_ones_row"][:, :]),
                           (S_["ones2"][:, :], din["t_ones2"][:, :]),
                           (S_["ones_col"][:, :], din["t_ones_col"][:, :]),
                           (S_["identf"][:, :], din["t_ident"][:, :]),
                           (S_["permb"][:, :], din["t_perm"][:, :]),
                           (S_["bqk"][:, :], din["bqk_t"][:, :])]:
            k.emit("sync", [], lambda e, d=dst, s=src: e.dma_start(out=d, in_=s),
                   [("dLtab", 16)])
        for (dst, src) in [(S_["bvaugb"][:, :], din["bvaug"][:, :]),
                           (S_["boutb"][:, :], din["bout"][:, :]),
                           (S_["frow2"][:, :], din["fc2b"][:, :])]:
            k.emit("gpsimd", [], lambda e, d=dst, s=src: e.dma_start(out=d, in_=s),
                   [("dGaux", 16)])

        for d in range(c.DMT):
            k.emit("gpsimd", [],
                   lambda e, d=d: e.dma_start(out=S_["wvs"][:, d * c.VBW:(d + 1) * c.VBW],
                                              in_=din["wvaug"][128 * d:128 * (d + 1), :]),
                   [("dGv", 16)])
        emit_stage0(k, S_)
        k.barrier(drain=True)

        for b in range(c.B):
            with ExitStack() as ab:
                A = dict(S_)
                A["kh"] = [k.sb(f"kh{h}_b{b}", [65, c.S], BF16, ab) for h in range(c.H)]
                A["vb"] = k.sb(f"vb_b{b}", [128, c.ST * c.VBW], BF16, ab)
                A["atc"] = [k.sb(f"atc{p}_b{b}", [128, c.CH], BF16, ab)
                            for p in range(c.NP)]
                A["atcf"] = k.sb(f"atcf_b{b}", [128, c.CH], F32, ab)
                A["rsrow"] = k.sb(f"rsrow_b{b}", [1, 1024], F32, ab)
                A["rb"] = k.sb(f"rb_b{b}", [128, 512], F32, ab)
                A["qh"] = [[k.sb(f"qh{par}_{hh}_b{b}", [65, c.CH], BF16, ab)
                            for hh in range(2)] for par in range(2)]
                A["xtc"] = k.sb(f"xtc_b{b}", [128, c.DMT * c.CH], BF16, ab)
                A["xbc"] = [k.sb(f"xbc{i}_b{b}", [128, c.DM], BF16, ab) for i in range(2)]
                A["xpc"] = [k.sb(f"xpc{i}_b{b}", [128, c.DM], F32, ab) for i in range(2)]
                A["wqkp"] = [k.sb(f"wqkp{i}_b{b}", [128, 2 * c.DMT * 128], BF16, ab)
                             for i in range(2)]
                A["scb"] = [k.sb(f"scb{i}_b{b}", [128, 2 * c.CH], F32, ab)
                            for i in range(2)]
                A["qbb"] = k.sb(f"qbb_b{b}", [128, 2 * c.CH], BF16, ab)
                A["t2"] = k.sb(f"t2_b{b}", [128, c.CH], F32, ab)
                A["t1"] = k.sb(f"t1_b{b}", [128, c.CH], F32, ab)
                A["rout"] = [k.sb(f"rout{i}_b{b}", [128, c.CH], BF16, ab)
                             for i in range(2)]
                A["wob"] = [k.sb(f"wob{i}_b{b}", [128, c.DM], BF16, ab) for i in range(2)]
                A["y1t"] = k.sb(f"y1t_b{b}", [128, c.DM], F32, ab)
                A["x1s"] = k.sb(f"x1s_b{b}", [128, c.DM], F32, ab)
                A["xh2s"] = k.sb(f"xh2s_b{b}", [128, c.DM], BF16, ab)
                emit_A(k, b, A)
                k.barrier()

            with ExitStack() as fstk:
                F = dict(S_)
                F["w1h"] = k.sb(f"w1h_b{b}", [128, c.DMT * c.FH], BF16, fstk)
                F["w2h"] = k.sb(f"w2h_b{b}", [128, c.FNTH * c.DM], BF16, fstk)
                F["yT"] = k.sb(f"yT_b{b}", [128, c.FNTH * c.TCH], BF16, fstk)
                F["xh2T"] = [k.sb(f"xh2T{i}_b{b}", [128, c.DMT * c.TCH], BF16, fstk)
                             for i in range(2)]
                F["xr"] = [k.sb(f"xr{i}_b{b}", [128, 2 * c.DM], F32, fstk)
                           for i in range(2)]
                F["zacc"] = k.sb(f"zacc_b{b}", [128, c.ST * c.DM], F32, fstk)
                F["tfin"] = k.sb(f"tfin_b{b}", [128, c.DM], F32, fstk)
                F["tfin2"] = k.sb(f"tfin2_b{b}", [128, c.DM], F32, fstk)
                F["outst"] = [k.sb(f"outst{i}_b{b}", [128, c.DM], F32, fstk)
                              for i in range(2)]
                emit_F(k, b, F)
                k.barrier()

        k.emit("gpsimd", [("dGout", k.v("dGout"))], lambda e: e.engine_nop(), [])
        k.flush()
    k.stack.close()
    return k


def emit_stage0(k, S_):
    c = k.cfg
    din = k.din
    pb = S_["pb"]
    with ExitStack() as st0:
        qsb = [k.sb(f"qsb{i}", [128, 512], I32, st0) for i in range(4)]
        scr = [k.sb(f"scr{i}", [1, 512], BF16, st0) for i in range(4)]
        qf = [k.sb(f"qf{i}", [128, 512], F32, st0) for i in range(2)]
        plainb = [k.sb(f"plainb{i}", [128, 512], BF16, st0) for i in range(2)]
        w1st = [k.sb(f"w1st{i}", [128, 512], BF16, st0) for i in range(2)]
        gcolb = k.sb("gcolb", [128, c.DMT], F32, st0)
        bcolb = k.sb("bcolb", [128, c.DMT], BF16, st0)
        fbrow = k.sb("fbrow", [1, 512], F32, st0)
        fbrow2 = k.sb("fbrow2", [1, 512], F32, st0)

        k.emit("sync", [], lambda e: e.dma_start(out=gcolb[:, :], in_=din["gammacol"][:, :]),
               [("dLtab", 16)])
        k.emit("gpsimd", [], lambda e: e.dma_start(out=bcolb[:, :], in_=din["betacol"][:, :]),
               [("dGaux", 16)])
        tab_v = k.v("dLtab")
        aux_v = k.v("dGaux")
        k.auxv = aux_v
        k.tabv = tab_v

        cvt = [k.sb(f"cvt{i}", [128, 2048], BF16, st0) for i in range(2)]
        for bi_ in range(c.NP):
            cp_ = bi_ % 2
            a = k.emit("gpsimd", k.gate(f"cvt{cp_}"),
                       lambda e, bi_=bi_, cp_=cp_: e.dma_start(
                           out=cvt[cp_][:, :],
                           in_=din["wqkA"][128 * bi_:128 * (bi_ + 1), :]),
                       [("dGv", 16)])
            gv_ = a["dGv"]
            a = k.emit("sync", [("dGv", gv_)],
                       lambda e, bi_=bi_, cp_=cp_: e.dma_start(
                           out=k.s["wqk"][128 * bi_:128 * (bi_ + 1), :],
                           in_=cvt[cp_][:, :]),
                       [("dLfb", 16)])
            k.setg(f"cvt{cp_}", ("dLfb", a["dLfb"]))

        tiles = [("w1", ncq, d) for ncq in range(c.FF // 512) for d in range(c.DMT)]
        tiles += [("w2", nt, dc) for nt in range(c.FF // 128) for dc in range(c.DM // 512)]

        for i, (kind, a_, b_) in enumerate(tiles):
            lp = i % 4
            par = i % 2
            if kind == "w1":
                ti = a_ * 8 + b_
                src_q = din["fc1qt"][128 * ti:128 * (ti + 1), :]
                src_s = din["fc1sc"][b_:b_ + 1, 512 * a_:512 * (a_ + 1)]
            else:
                ti = a_ * 2 + b_
                src_q = din["fc2qt"][128 * ti:128 * (ti + 1), :]
                src_s = din["fc2sc"][a_:a_ + 1, 512 * b_:512 * (b_ + 1)]
            k.emit("sync" if lp < 2 else "scalar", k.gate(f"qsb{lp}"),
                   lambda e, s=src_q, p=lp: e.dma_start(out=qsb[p][:, :], in_=s),
                   [("dLq%d" % lp, 16)])
            k.emit("gpsimd", k.gate(f"scr{lp}"),
                   lambda e, s=src_s, p=lp: e.dma_start(out=scr[p][:, :], in_=s),
                   [("dGq%d" % lp, 16)])
            ldv = k.v("dLq%d" % lp)
            gqv = k.v("dGq%d" % lp)
            a1 = k.emit("tensor", [("dGq%d" % lp, gqv),
                                  ("dLtab", tab_v)] + k.gate(f"s0pb{par}"),
                        lambda e, p=par, lq=lp: e.matmul(pb[p][:, :],
                                                         S_["ones_row"][0:1, 0:128],
                                                         scr[lq][0:1, :],
                                                         start=True, stop=True),
                        [("sPE", 1)])
            a2 = k.emit("scalar", [("dLq%d" % lp, ldv), ("sDVE", 4)] + k.gate(f"qf{par}"),
                        lambda e, p=par, lq=lp: e.activation(out=qf[p][:, :],
                                                             in_=qsb[lq][:, :],
                                                             func=AF.Copy, bias=-128.0),
                        [("sACT", 1)])
            k.setg(f"qsb{lp}", ("sACT", a2["sACT"]))
            k.setg(f"scr{lp}", ("sPE", a1["sPE"]))
            a3 = k.emit("vector", [("sACT", a2["sACT"]), ("sPE", a1["sPE"])] +
                        k.gate(f"plainb{par}"),
                        lambda e, p=par: e.tensor_tensor(out=plainb[p][:, :],
                                                         in0=qf[p][:, :],
                                                         in1=pb[p][:, :], op=OP.mult),
                        [("sDVE", 1)])
            k.setg(f"qf{par}", ("sDVE", a3["sDVE"]))
            k.setg(f"s0pb{par}", ("sDVE", a3["sDVE"]))
            if kind == "w1":
                a4 = k.emit("tensor", [("sDVE", a3["sDVE"]), ("dGaux", aux_v)] +
                            (k.gate("fbpsum") if b_ == 0 else []),
                            lambda e, d=b_, p=par: e.matmul(pb[2][0:1, 0:512],
                                                            bcolb[:, d:d + 1],
                                                            plainb[p][:, :], start=(d == 0),
                                                            stop=(d == c.DMT - 1)),
                            [("sPE", 1)])
                a5 = k.emit("vector", [("dLtab", tab_v)] + k.gate(f"w1st{par}"),
                            lambda e, d=b_, p=par: e.tensor_scalar(
                                out=w1st[p][:, :], in0=plainb[p][:, :],
                                scalar1=gcolb[:, d:d + 1],
                                scalar2=None, op0=OP.mult),
                            [("sDVE", 1)])
                k.setg(f"plainb{par}", ("sDVE", a5["sDVE"]), ("sPE", a4["sPE"]))
                wdst = k.s["w1"][128 * b_:128 * (b_ + 1), 512 * a_:512 * (a_ + 1)]
                a6 = k.emit("gpsimd", [("sDVE", a5["sDVE"])],
                            lambda e, d=wdst, p=par: e.dma_start(out=d, in_=w1st[p][:, :]),
                            [("dGsw%d" % par, 16)])
                k.setg(f"w1st{par}", ("dGsw%d" % par, a6["dGsw%d" % par]))
                if b_ == c.DMT - 1:
                    ncq = a_
                    a7 = k.emit("scalar", [("sPE", a4["sPE"])] + k.gate("fbrow2"),
                                lambda e: e.activation(out=fbrow2[:, :],
                                                       in_=pb[2][0:1, 0:512], func=AF.Copy),
                                [("sACT", 1)])
                    k.setg("fbpsum", ("sACT", a7["sACT"]))
                    a8 = k.emit("sync", k.gate("fbrow"),
                                lambda e, n=ncq: e.dma_start(
                                    out=fbrow[:, :],
                                    in_=din["fc1b"][:, 512 * n:512 * (n + 1)]),
                                [("dLfb", 16)])
                    a9 = k.emit("vector", [("sACT", a7["sACT"]), ("dLfb", a8["dLfb"])],
                                lambda e: e.tensor_tensor(out=fbrow2[:, :], in0=fbrow2[:, :],
                                                          in1=fbrow[:, :], op=OP.add),
                                [("sDVE", 1)])
                    k.setg("fbrow", ("sDVE", a9["sDVE"]))
                    a10 = k.emit("gpsimd", [("sDVE", a9["sDVE"])],
                                 lambda e, n=ncq: e.dma_start(
                                     out=k.s["fc1b"][:, 512 * n:512 * (n + 1)],
                                     in_=fbrow2[:, :]),
                                 [("dGsb", 16)])
                    k.setg("fbrow2", ("dGsb", a10["dGsb"]))
            else:
                wdst = k.s["w2"][128 * a_:128 * (a_ + 1), 512 * b_:512 * (b_ + 1)]
                a6 = k.emit("gpsimd", [("sDVE", a3["sDVE"])],
                            lambda e, d=wdst, p=par: e.dma_start(out=d, in_=plainb[p][:, :]),
                            [("dGsw%d" % par, 16)])
                k.setg(f"plainb{par}", ("dGsw%d" % par, a6["dGsw%d" % par]))
        sw0, sw1, sbv = k.v("dGsw0"), k.v("dGsw1"), k.v("dGsb")
        for j in range(c.FF // 128):
            k.emit("sync", [("dGsw0", sw0), ("dGsw1", sw1), ("dGsb", sbv)] if j == 0 else [],
                   lambda e, j=j: e.dma_start(
                       out=S_["fc1bcol"][:, j:j + 1],
                       in_=k.s["fc1b"][:, 128 * j:128 * (j + 1)].rearrange(
                           "o (p q) -> p (o q)", q=1)),
                   [("dLtab", 16)])


def emit_A(k, b, A):
    c = k.cfg
    din = k.din
    pb, pav = A["pb"], A["pav"]

    for h in range(c.H):
        k.emit("gpsimd" if h % 2 == 0 else "vector", [],
               lambda e, h=h: e.memset(A["kh"][h][64:65, :], 1.0),
               [("sPL", 1)])
    k.info["khones"] = k.v("sPL")
    if DBG2 and (DBG3 & 4) and b == 0:
        k.emit("vector", [], lambda e: e.memset(k.d_avb[:, :], 0.0), [("sDVE", 1)])

    def a1sc(ch):
        q0 = ch * c.CH
        sc2 = A["scb"][ch % 2]
        for tt in range(c.CT):
            par = tt % 2
            a = k.emit("sync", k.gate(f"xpc{par}"),
                       lambda e, tt=tt, par=par, q0=q0: e.dma_start(
                           out=A["xpc"][par][:, :],
                           in_=din["x"][b, q0 + 128 * tt:q0 + 128 * (tt + 1), :]),
                       [("dLx%d" % par, 16)])
            ldv = a["dLx%d" % par]
            for rnd in range(2):
                tb = pb[3 + rnd]
                for dd in range(4):
                    d = 4 * rnd + dd
                    wl = []
                    if dd == 0:
                        wl = [("dLx%d" % par, ldv), ("dLtab", k.tabv)] + k.gate(BANKN[3 + rnd])
                    k.emit("tensor", wl,
                           lambda e, d=d, dd=dd, par=par, tb=tb: e.transpose(
                               tb[:, 128 * dd:128 * (dd + 1)],
                               A["xpc"][par][:, 128 * d:128 * (d + 1)],
                               A["identf"][:, :]),
                           [("sPE", 1)])
                a2 = k.emit("scalar", [("sPE", k.v("sPE"))] + (k.gate("xtc") if tt == 0 and rnd == 0 else []),
                            lambda e, rnd=rnd, tt=tt, tb=tb: e.activation(
                                out=bass.AP(A["xtc"], 4 * rnd * c.CH + 128 * tt,
                                            [[c.DMT * c.CH, 128], [c.CH, 4], [1, 128]]),
                                in_=tb[:, 0:512], func=AF.Copy),
                            [("sACT", 1)])
                k.addg(BANKN[3 + rnd], ("sACT", a2["sACT"]))
            k.setg(f"xpc{par}", ("sPE", k.v("sPE")))
        k.info[("xtcok", ch)] = k.v("sACT")
        a = k.emit("sync", k.gate(f"scb{ch % 2}"),
                   lambda e, q0=q0, sc2=sc2: e.dma_start(
                       out=sc2[:, 0:c.CH], in_=din["t_sincos"][:, q0:q0 + c.CH]),
                   [("dLsc", 16)])
        k.emit("sync", [],
               lambda e, q0=q0, sc2=sc2: e.dma_start(
                   out=sc2[:, c.CH:2 * c.CH],
                   in_=din["t_sincos"][:, c.S + q0:c.S + q0 + c.CH]),
               [("dLsc", 16)])
        k.info[("scv", ch)] = k.v("dLsc")

    a1sc(0)
    for ch in range(c.NCH):
        q0 = ch * c.CH
        xtc_ok = k.info[("xtcok", ch)]
        sc_v = k.info[("scv", ch)]
        if XBAR >= 4:
            k.barrier()
        if c.VBW <= 512:
            pieces = [(0, c.VBW)]
        elif c.VBW <= 1024:
            pieces = [(0, 512), (512, c.VBW - 512)]
        else:
            pieces = [(0, 512), (512, 512), (1024, c.VBW - 1024)]
        vgi = 0
        for tt in range(c.CT):
            ktg = ch * c.CT + tt
            for (off, wdt) in pieces:
                bi = 3 + (vgi % 3)
                bank = pb[bi]
                for d in range(c.DMT):
                    wl = []
                    if d == 0:
                        wl = [("sACT", xtc_ok),
                              ("dGaux", k.auxv)] + k.gate(BANKN[bi])
                    k.emit("tensor", wl,
                           lambda e, d=d, tt=tt, off=off, wdt=wdt, bank=bank: e.matmul(
                               bank[:, 0:wdt],
                               A["xtc"][:, c.CH * d + 128 * tt:c.CH * d + 128 * (tt + 1)],
                               A["wvs"][:, c.VBW * d + off:c.VBW * d + off + wdt],
                               start=(d == 0), stop=False),
                           [("sPE", 1)])
                a = k.emit("tensor", [],
                           lambda e, off=off, wdt=wdt, bank=bank: e.matmul(
                               bank[:, 0:wdt], A["ones_row"][0:1, 0:128],
                               A["bvaugb"][0:1, off:off + wdt], start=False, stop=True),
                           [("sPE", 1)])
                a2 = k.emit("scalar", [("sPE", a["sPE"])],
                            lambda e, off=off, wdt=wdt, bank=bank, ktg=ktg: e.activation(
                                out=A["vb"][:, c.VBW * ktg + off:c.VBW * ktg + off + wdt],
                                in_=bank[:, 0:wdt], func=AF.Copy),
                            [("sACT", 1)])
                k.addg(BANKN[bi], ("sACT", a2["sACT"]))
                vgi += 1
        k.info["vb_ready"] = k.v("sACT")

        if XBAR >= 2:
            k.barrier()
        for s in range(c.NP + 1):
            if s < c.NP:
                emit_pair_qk_part1(k, b, ch, s, A, sc_v, xtc_ok)
            if s > 0:
                emit_pair_attention(k, b, ch, s - 1, A)
        if DBG and b == 0:
            for p in range(c.NP):
                k.emit("gpsimd", k.gate("atc_all"),
                       lambda e, p=p, ch=ch: e.dma_start(
                           out=k.d_atc[ch, p, :, :], in_=A["atc"][p][:, :]),
                       [("dGsp", 16)])
        if ch + 1 < c.NCH:
            a1sc(ch + 1)
        emit_outproj_chunk(k, b, ch, A)
        if XBAR >= 1:
            k.barrier()
    if DBG2 and b == 0:
        w0 = [("sDVE", k.v("sDVE")), ("sACT", k.v("sACT"))]
        if DBG3 & 2:
            k.emit("gpsimd", w0,
                   lambda e: e.dma_start(out=k.d_sc[:, :], in_=k.d_scb[:, :]), [("dGsp", 16)])
            w0 = []
        if DBG3 & 1:
            k.emit("gpsimd", w0,
                   lambda e: e.dma_start(out=k.d_p0[:, :], in_=k.d_pbuf[:, :]), [("dGsp", 16)])
            w0 = []
        if DBG3 & 4:
            k.emit("gpsimd", w0,
                   lambda e: e.dma_start(out=k.d_av[:, :], in_=k.d_avb[:, :]), [("dGsp", 16)])
            w0 = []
        if DBG3 & 16:
            k.emit("gpsimd", w0,
                   lambda e: e.dma_start(out=k.d_vb[:, :], in_=A["vb"][:, 0:2 * c.VBW]),
                   [("dGsp", 16)])
            w0 = []
        if DBG3 & 8:
            k.emit("gpsimd", w0,
                   lambda e: e.dma_start(out=k.d_rb[:, :], in_=k.d_rbb[:, :]), [("dGsp", 16)])
            k.emit("gpsimd", [],
                   lambda e: e.dma_start(out=k.d_rcp[:, :], in_=k.d_rcpb[:, :]), [("dGsp", 16)])


def emit_pair_qk_part1(k, b, ch, j, A, sc_v, xtc_ok):
    c = k.cfg
    din = k.din
    q0 = ch * c.CH
    NKT = c.ST
    slot = ch * c.NP + j
    par = slot % 2
    pb = A["pb"]
    qh, kh = A["qh"], A["kh"]
    spl = "dGsplit%d" % par
    WQ = c.DMT * 128

    k.emit("sync", k.gate(f"wqkp{par}"),
           lambda e, par=par, j=j: e.dma_start(
               out=A["wqkp"][par][:, :],
               in_=k.s["wqk"][128 * j:128 * (j + 1), :]),
           [("dGw%d" % par, 16)])
    wqv = k.v("dGw%d" % par)

    for hh in range(2):
        k.emit("gpsimd", k.gate(f"qh{par}") if hh == 0 else [],
               lambda e, hh=hh, par=par, j=j, q0=q0: e.dma_start(
                   out=qh[par][hh][64:65, :],
                   in_=din["t_aug"][2 * j + hh:2 * j + hh + 1, q0:q0 + c.CH]),
               [(spl, 16)])

    pe_done = {}
    for (idx, bank, colbase) in ((0, pb[0], 0), (1, pb[1], WQ)):
        for d in range(c.DMT):
            wl = []
            if d == 0:
                wl = [("dGw%d" % par, wqv), ("sACT", xtc_ok)] + k.gate(BANKN[idx])
            k.emit("tensor", wl,
                   lambda e, d=d, bank=bank, cb=colbase, par=par: e.matmul(
                       bank[:, :], A["wqkp"][par][:, cb + 128 * d:cb + 128 * (d + 1)],
                       A["xtc"][:, c.CH * d:c.CH * (d + 1)],
                       start=(d == 0), stop=(d == c.DMT - 1)),
                   [("sPE", 1)])
        pe_done[idx] = k.v("sPE")
    k.setg(f"wqkp{par}", ("sPE", pe_done[1]))
    k.addg("xtc", ("sPE", pe_done[1]))

    for idx, bank, bcol in ((0, pb[0], j), (1, pb[1], c.NP + j)):
        qo = 512 * idx
        a = k.emit("scalar", [("sPE", pe_done[idx])] + k.gate(f"qbb_{idx}"),
                   lambda e, bank=bank, bcol=bcol, qo=qo: e.activation(
                       out=A["qbb"][:, qo:qo + c.CH], in_=bank[:, :],
                       func=AF.Identity, bias=A["bqk"][:, bcol:bcol + 1]),
                   [("sACT", 1)])
        ev = a["sACT"]
        a = k.emit("tensor", [("sACT", ev)],
                   lambda e, bank=bank, qo=qo: e.matmul(
                       bank[:, :], A["permb"][:, :], A["qbb"][:, qo:qo + c.CH],
                       start=True, stop=True),
                   [("sPE", 1)])
        pm_v = a["sPE"]
        a = k.emit("vector", [("sACT", ev), ("dLsc", sc_v)] + k.gate("t1"),
                   lambda e, qo=qo, ch=ch: e.tensor_tensor(
                       out=A["t1"][:, :], in0=A["qbb"][:, qo:qo + c.CH],
                       in1=A["scb"][ch % 2][:, 0:c.CH], op=OP.mult),
                   [("sDVE", 1)])
        t1_v = a["sDVE"]
        a = k.emit("vector", [("sPE", pm_v), ("sDVE", t1_v)] + k.gate("t2"),
                   lambda e, bank=bank, ch=ch: e.tensor_tensor(
                       out=A["t2"][:, :], in0=bank[:, :],
                       in1=A["scb"][ch % 2][:, c.CH:2 * c.CH], op=OP.mult),
                   [("sDVE", 1)])
        t2_v = a["sDVE"]
        k.setg(f"qbb_{idx}", ("sDVE", t2_v), ("sPE", pm_v))
        k.addg(BANKN[idx], ("sDVE", t2_v))
        k.addg(f"scb{ch % 2}", ("sDVE", t2_v))
        a = k.emit("vector", [("sDVE", t2_v)] + k.gate(f"rout{idx}"),
                   lambda e, idx=idx: e.tensor_tensor(out=A["rout"][idx][:, :],
                                                      in0=A["t1"][:, :],
                                                      in1=A["t2"][:, :], op=OP.add),
                   [("sDVE", 1)])
        rp_v = a["sDVE"]
        k.setg("t1", ("sDVE", rp_v))
        k.setg("t2", ("sDVE", rp_v))
        if idx == 0:
            tgts = [qh[par][0][0:64, :], qh[par][1][0:64, :]]
        else:
            tgts = [kh[2 * j][0:64, q0:q0 + c.CH], kh[2 * j + 1][0:64, q0:q0 + c.CH]]
        for z, t in enumerate(tgts):
            k.emit("gpsimd", [("sDVE", rp_v)] if z == 0 else [],
                   lambda e, t=t, z=z, idx=idx: e.dma_start(
                       out=t, in_=A["rout"][idx][64 * z:64 * (z + 1), :]),
                   [(spl, 16)])
        k.setg(f"rout{idx}", (spl, k.v(spl)))
    k.info[("split", slot)] = k.v(spl)


def emit_pair_attention(k, b, ch, j, A):
    c = k.cfg
    NKT = c.ST
    slot = ch * c.NP + j
    par = slot % 2
    pb, pav = A["pb"], A["pav"]
    probs = A["probs"]
    nkt = (ch + 1) * c.CT
    spl = "dGsplit%d" % par
    spl_v = k.info[("split", slot)]
    av_last = {}
    exp_v = {0: {}, 1: {}}

    def do_scores(kt, h, hh, first):
        diag = kt >= ch * c.CT
        off = 128 * (kt - ch * c.CT) if diag else 0
        wdt = c.CH - off
        bi = 4 + kt % 2
        sbank = pb[bi]
        wl = k.gate(BANKN[bi])
        if first:
            wl += [(spl, spl_v), ("sPL", k.info["khones"])]
        a = k.emit("tensor", wl,
                   lambda e, h=h, kt=kt, off=off, wdt=wdt, sbank=sbank, hh=hh, par=par:
                   e.matmul(sbank[:, off:off + wdt],
                            A["kh"][h][0:65, 128 * kt:128 * (kt + 1)],
                            A["qh"][par][hh][0:65, off:off + wdt],
                            start=True, stop=True),
                   [("sPE", 1)])
        sc_pe = a["sPE"]
        band_v = 0
        if diag:
            a2 = k.emit("vector", [("sPE", sc_pe)],
                        lambda e, off=off, sbank=sbank: e.tensor_tensor(
                            out=sbank[:, off:off + 128], in0=sbank[:, off:off + 128],
                            in1=A["band"][:, :], op=OP.add),
                        [("sDVE", 1)])
            band_v = a2["sDVE"]
        wl = [("sPE", sc_pe)] + ([("sDVE", band_v)] if band_v else []) \
            + k.gate(f"probs{kt % 2}")
        a3 = k.emit("scalar", wl,
                    lambda e, h=h, kt=kt, off=off, wdt=wdt, sbank=sbank:
                    e.activation(out=probs[kt % 2][:, off:off + wdt],
                                 in_=sbank[:, off:off + wdt], func=AF.Exp,
                                 bias=A["hkt"][:, h * NKT + kt:h * NKT + kt + 1],
                                 scale=c.SCALE),
                    [("sACT", 1)])
        k.addg(BANKN[bi], ("sACT", a3["sACT"]))
        exp_v[hh][kt] = a3["sACT"]

    def do_av(kt, h, hpar, avb, first, last):
        diag = kt >= ch * c.CT
        off = 128 * (kt - ch * c.CT) if diag else 0
        wdt = c.CH - off
        wl = [("sACT", max(exp_v[h % 2][kt], k.info.get("vb_ready", 0)))]
        if first:
            wl += k.gate(BANKN[6 + hpar])
        a4 = k.emit("tensor", wl,
                    lambda e, h=h, kt=kt, off=off, wdt=wdt, avb=avb, first=first, last=last:
                    e.matmul(avb[0:65, off:off + wdt],
                             A["vb"][:, c.VBW * kt + 65 * h:c.VBW * kt + 65 * h + 65],
                             probs[kt % 2][:, off:off + wdt],
                             start=first, stop=last,
                             skip_group_check=True),
                    [("sPE", 1)])
        k.addg(f"probs{kt % 2}", ("sPE", a4["sPE"]))
        return a4["sPE"]

    for hh in range(2):
        h = 2 * j + hh
        avb = pav[h % 2]
        for i, kt in enumerate(range(nkt)):
            do_scores(kt, h, hh, first=(i == 0))
            if i >= 1:
                do_av(kt - 1, h, h % 2, avb, first=(i == 1), last=False)
        av_last[hh] = do_av(nkt - 1, h, h % 2, avb, first=(nkt == 1), last=True)

    ev_hh = {}
    atcf_v = 0
    for hh in range(2):
        h = 2 * j + hh
        hpar = h % 2
        avb = pav[hpar]
        a = k.emit("scalar", [("sPE", av_last[hh])] + k.gate(f"rsrow{hh}"),
                   lambda e, avb=avb, hh=hh: e.activation(
                       out=A["rsrow"][0:1, 512 * hh:512 * (hh + 1)],
                       in_=avb[64:65, 0:c.CH], func=AF.Copy),
                   [("sACT", 1)])
        rs_v = a["sACT"]
        a = k.emit("scalar", k.gate("atcf") if hh == 0 else [],
                   lambda e, avb=avb, hh=hh: e.activation(
                       out=A["atcf"][64 * hh:64 * (hh + 1), :],
                       in_=avb[0:64, 0:c.CH], func=AF.Copy),
                   [("sACT", 1)])
        atcf_v = a["sACT"]
        k.addg(BANKN[6 + hpar], ("sACT", atcf_v))
        a = k.emit("tensor", [("sACT", rs_v)] + (k.gate(BANKN[3]) if hh == 0 else []),
                   lambda e, hh=hh: e.matmul(pb[3][64 * hh:64 * (hh + 1), 0:c.CH],
                                             A["ones2f"][0:1, 0:64],
                                             A["rsrow"][0:1, 512 * hh:512 * (hh + 1)],
                                             start=True, stop=True),
                   [("sPE", 1)])
        k.setg(f"rsrow{hh}", ("sPE", a["sPE"]))
        ev_hh[hh] = a["sPE"]
    a = k.emit("vector", [("sPE", ev_hh[1])] + k.gate("rbB"),
               lambda e: e.reciprocal(out=A["rb"][:, :], in_=pb[3][:, 0:c.CH]),
               [("sDVE", 1)])
    rcp_v = a["sDVE"]
    k.addg(BANKN[3], ("sDVE", rcp_v))
    a = k.emit("vector", [("sDVE", rcp_v), ("sACT", atcf_v)] + k.gate("atc_busy"),
               lambda e, j=j: e.tensor_tensor(out=A["atc"][j][:, :], in0=A["atcf"][:, :],
                                              in1=A["rb"][:, :], op=OP.mult),
               [("sDVE", 1)])
    am_v = a["sDVE"]
    k.setg("rbB", ("sDVE", am_v))
    k.setg("atcf", ("sDVE", am_v))
    k.addg("atc_all", ("sDVE", am_v))
    k.setg(f"qh{par}", ("sPE", k.v("sPE")))


def emit_outproj_chunk(k, b, ch, A):
    c = k.cfg
    din = k.din
    q0 = ch * c.CH
    banks = A["banks"]
    atc_all = k.gate("atc_all")

    for p in range(c.NP):
        wpar = p % 2
        k.emit("gpsimd", k.gate(f"wob{wpar}") + (atc_all if p == 0 else []),
               lambda e, p=p, wpar=wpar: e.dma_start(
                   out=A["wob"][wpar][:, :], in_=din["wout"][128 * p:128 * (p + 1), :]),
               [("dGo%d" % wpar, 16)])
        gov = k.v("dGo%d" % wpar)
        for tt in range(c.CT):
            for dmc in range(2):
                g = 2 * tt + dmc
                wl = []
                if tt == 0 and dmc == 0:
                    wl = [("dGo%d" % wpar, gov)]
                    if p == 0:
                        wl += atc_all + k.gates(*BANKN)
                k.emit("tensor", wl,
                       lambda e, p=p, tt=tt, dmc=dmc, g=g, wpar=wpar: e.matmul(
                           banks[g][:, :], A["atc"][p][:, 128 * tt:128 * (tt + 1)],
                           A["wob"][wpar][:, 512 * dmc:512 * (dmc + 1)],
                           start=(p == 0), stop=False, skip_group_check=True),
                       [("sPE", 1)])
        k.setg(f"wob{wpar}", ("sPE", k.v("sPE")))
    pe_g = {}
    for tt in range(c.CT):
        for dmc in range(2):
            g = 2 * tt + dmc
            k.emit("tensor", [("dGaux", k.auxv)] if g == 0 else [],
                   lambda e, g=g, dmc=dmc: e.matmul(
                       banks[g][:, :], A["ones_row"][0:1, 0:128],
                       A["boutb"][0:1, 512 * dmc:512 * (dmc + 1)],
                       start=False, stop=True, skip_group_check=True),
                   [("sPE", 1)])
            pe_g[g] = k.v("sPE")
    pe_op = k.v("sPE")
    k.setg("atc_busy", ("sPE", pe_op))
    k.g.pop("atc_all", None)

    for tt in range(c.CT):
        par = tt % 2
        a = k.emit("sync", k.gate(f"xpc{par}"),
                   lambda e, tt=tt, par=par: e.dma_start(
                       out=A["xpc"][par][:, :],
                       in_=din["x"][b, q0 + 128 * tt:q0 + 128 * (tt + 1), :]),
                   [("dLxr%d" % par, 16)])
        lxr = k.v("dLxr%d" % par)
        tt_done = 0
        for dmc in range(2):
            g = 2 * tt + dmc
            a = k.emit("vector", [("sPE", pe_g[g]), ("dLxr%d" % par, lxr)] +
                       (k.gate("y1t") if dmc == 0 else []),
                       lambda e, tt=tt, dmc=dmc, g=g, par=par: e.tensor_tensor(
                           out=A["y1t"][:, 512 * dmc:512 * (dmc + 1)],
                           in0=banks[g][:, :],
                           in1=A["xpc"][par][:, 512 * dmc:512 * (dmc + 1)], op=OP.add),
                       [("sDVE", 1)])
            tt_done = a["sDVE"]
            k.addg(BANKN[g], ("sDVE", tt_done))
        k.setg(f"xpc{par}", ("sDVE", tt_done))
        a = k.emit("vector", [("sDVE", tt_done)] + k.gate("bns"),
                   lambda e: e.bn_stats(out=A["bns"][:, 0:6], in_=A["y1t"][:, 0:512]),
                   [("sDVE", 1)])
        a = k.emit("vector", [("sDVE", a["sDVE"])],
                   lambda e: e.bn_stats(out=A["bns"][:, 6:12], in_=A["y1t"][:, 512:1024]),
                   [("sDVE", 1)])
        a = k.emit("vector", [("sDVE", a["sDVE"])] + k.gate("lnt"),
                   lambda e: e.bn_aggr(out=A["lnt"][:, 0:2], in_=A["bns"][:, :]),
                   [("sDVE", 1)])
        k.setg("bns", ("sDVE", a["sDVE"]))
        st_v = a["sDVE"]
        a = k.emit("scalar", [("sDVE", st_v)],
                   lambda e: e.activation(out=A["lnt"][:, 2:3], in_=A["lnt"][:, 1:2],
                                          func=AF.Ln, bias=EPS), [("sACT", 1)])
        a = k.emit("scalar", [("sACT", a["sACT"])],
                   lambda e: e.activation(out=A["lnt"][:, 3:4], in_=A["lnt"][:, 1:2],
                                          func=AF.Ln, bias=EPS * EPS, scale=1.0 + EPS),
                   [("sACT", 1)])
        a = k.emit("scalar", [("sACT", a["sACT"])],
                   lambda e: e.activation(out=A["lnt"][:, 5:6], in_=A["lnt"][:, 2:3],
                                          func=AF.Exp, scale=-0.5), [("sACT", 1)])
        a = k.emit("vector", [("sACT", a["sACT"]), ("sDVE", st_v)],
                   lambda e: e.tensor_tensor(out=A["lnt"][:, 4:5], in0=A["lnt"][:, 2:3],
                                             in1=A["lnt"][:, 3:4], op=OP.subtract),
                   [("sDVE", 1)])
        a = k.emit("scalar", [("sDVE", a["sDVE"]), ("sACT", a.get("sACT", 0) if isinstance(a, dict) else 0)],
                   lambda e: e.activation(out=A["lnt"][:, 6:7], in_=A["lnt"][:, 4:5],
                                          func=AF.Exp, scale=0.5), [("sACT", 1)])
        a = k.emit("vector", [("sACT", a["sACT"])],
                   lambda e: e.tensor_tensor(out=A["lnt"][:, 7:8], in0=A["lnt"][:, 5:6],
                                             in1=A["lnt"][:, 6:7], op=OP.mult),
                   [("sDVE", 1)])
        r2_v = a["sDVE"]
        a = k.emit("vector", [("sDVE", r2_v)] + k.gate("x1s"),
                   lambda e: e.tensor_scalar(out=A["x1s"][:, :], in0=A["y1t"][:, :],
                                             scalar1=A["lnt"][:, 0:1],
                                             scalar2=A["lnt"][:, 5:6],
                                             op0=OP.subtract, op1=OP.mult),
                   [("sDVE", 1)])
        x1_v = a["sDVE"]
        a2 = k.emit("vector", [("sDVE", r2_v)] + k.gate("xh2s"),
                    lambda e: e.tensor_scalar(out=A["xh2s"][:, :], in0=A["y1t"][:, :],
                                              scalar1=A["lnt"][:, 0:1],
                                              scalar2=A["lnt"][:, 7:8],
                                              op0=OP.subtract, op1=OP.mult),
                    [("sDVE", 1)])
        k.setg("lnt", ("sDVE", a2["sDVE"]))
        k.setg("y1t", ("sDVE", a2["sDVE"]))
        r0 = q0 + 128 * tt
        a3 = k.emit("gpsimd", [("sDVE", x1_v)],
                    lambda e, r0=r0: e.dma_start(out=k.s["x1"][b, r0:r0 + 128, :],
                                                 in_=A["x1s"][:, :]),
                    [("dGsp", 16)])
        a3 = k.emit("gpsimd", [("sDVE", a2["sDVE"])],
                    lambda e, r0=r0: e.dma_start(out=k.s["xh2"][b, r0:r0 + 128, :],
                                                 in_=A["xh2s"][:, :]),
                    [("dGsp", 16)])
        k.setg("x1s", ("dGsp", a3["dGsp"]))
        k.setg("xh2s", ("dGsp", a3["dGsp"]))


def emit_F(k, b, F):
    c = k.cfg
    din = k.din
    pb = F["pb"]
    spill_v = k.v("dGsp")

    def load_wgrp(half, nt, wl_extra):
        par = nt % 2
        wsem = "dLw%d" % par
        for d in range(c.DMT):
            k.emit("sync", (wl_extra + k.gate(f"wgrp{nt}")) if d == 0 else [],
                   lambda e, nt=nt, d=d, half=half: e.dma_start(
                       out=F["w1h"][:, (d * c.FNTH + nt) * 128:(d * c.FNTH + nt) * 128 + 128],
                       in_=k.s["w1"][128 * d:128 * (d + 1),
                                     c.FH * half + 128 * nt:c.FH * half + 128 * (nt + 1)]),
                   [(wsem, 16)])
        k.emit("sync", [],
               lambda e, nt=nt, half=half: e.dma_start(
                   out=F["w2h"][:, c.DM * nt:c.DM * (nt + 1)],
                   in_=k.s["w2"][c.FH * half + 128 * nt:c.FH * half + 128 * (nt + 1), :]),
               [(wsem, 16)])
        return k.v(wsem)

    for half in range(2):
        grp_v = {}
        grp_v[0] = load_wgrp(half, 0, [("dGsw0", k.v("dGsw0")), ("dGsw1", k.v("dGsw1"))] if (half == 0 and b == 0) else [])
        if c.FNTH > 1:
            grp_v[1] = load_wgrp(half, 1, [])

        for tc in range(c.NTC):
            r0 = tc * c.TCH
            par = tc % 2
            for d in range(c.DMT):
                for h2 in range(2):
                    wl = k.gate(f"xh2T{par}") if (d == 0 and h2 == 0) else []
                    if d == 0 and h2 == 0 and half == 0 and tc == 0:
                        wl += [("dGsp", spill_v)]
                    k.emit("sync", wl,
                           lambda e, d=d, h2=h2, par=par, r0=r0: e.dma_start_transpose(
                               out=F["xh2T"][par][:, c.TCH * d + 128 * h2:
                                                  c.TCH * d + 128 * (h2 + 1)],
                               in_=k.s["xh2"][b, r0 + 128 * h2:r0 + 128 * (h2 + 1),
                                              128 * d:128 * (d + 1)]),
                           [("dLxh%d" % par, 16)])
            xh_v = k.v("dLxh%d" % par)
            x1_v = 0
            if half == 1:
                for h2 in range(2):
                    k.emit("sync", (k.gate(f"xr{par}") if h2 == 0 else []),
                           lambda e, h2=h2, par=par, r0=r0: e.dma_start(
                               out=F["xr"][par][:, c.DM * h2:c.DM * (h2 + 1)],
                               in_=k.s["x1"][b, r0 + 128 * h2:r0 + 128 * (h2 + 1), :]),
                           [("dLx1%d" % par, 16)])
                x1_v = k.v("dLx1%d" % par)
            silu_done = {}

            def emit_fc1(nt):
                bi = nt % 2
                bank = pb[bi]
                for d in range(c.DMT):
                    wl = []
                    if d == 0:
                        wl = [("dLw%d" % (nt % 2), grp_v[nt]),
                              ("dLxh%d" % par, xh_v)] + k.gate(BANKN[bi])
                    k.emit("tensor", wl,
                           lambda e, nt=nt, d=d, bank=bank, par=par: e.matmul(
                               bank[:, 0:c.TCH],
                               F["w1h"][:, (d * c.FNTH + nt) * 128:
                                        (d * c.FNTH + nt) * 128 + 128],
                               F["xh2T"][par][:, c.TCH * d:c.TCH * (d + 1)],
                               start=(d == 0), stop=(d == c.DMT - 1)),
                           [("sPE", 1)])
                fc1_v = k.v("sPE")
                if tc == 0 and nt + 2 < c.FNTH:
                    grp_v[nt + 2] = load_wgrp(half, nt + 2, [])
                if nt == c.FNTH - 1:
                    k.setg(f"xh2T{par}", ("sPE", fc1_v))
                ncol = half * c.FNTH + nt
                a = k.emit("scalar", [("sPE", fc1_v), ("dLtab", k.v("dLtab"))] +
                           k.gate(f"yT{nt % 4}"),
                           lambda e, nt=nt, bank=bank, ncol=ncol: e.activation(
                               out=F["yT"][:, c.TCH * nt:c.TCH * (nt + 1)],
                               in_=bank[:, 0:c.TCH], func=AF.Silu,
                               bias=F["fc1bcol"][:, ncol:ncol + 1]),
                           [("sACT", 1)])
                silu_done[nt] = a["sACT"]
                k.addg(BANKN[bi], ("sACT", a["sACT"]))

            def emit_fc2(nt):
                for tt2 in range(2):
                    for dmc in range(2):
                        g = 2 + 2 * tt2 + dmc
                        wl = [("sACT", silu_done[nt])] if (tt2 == 0 and dmc == 0) else []
                        if nt == 0 and tt2 == 0 and dmc == 0:
                            wl += k.gates(*BANKN[2:6])
                        k.emit("tensor", wl,
                               lambda e, nt=nt, tt2=tt2, dmc=dmc, g=g: e.matmul(
                                   pb[g][:, :],
                                   F["yT"][:, c.TCH * nt + 128 * tt2:
                                           c.TCH * nt + 128 * (tt2 + 1)],
                                   F["w2h"][:, c.DM * nt + 512 * dmc:
                                            c.DM * nt + 512 * (dmc + 1)],
                                   start=(nt == 0), stop=False, skip_group_check=True),
                               [("sPE", 1)])
                if tc == c.NTC - 1:
                    k.setg(f"wgrp{nt}", ("sPE", k.v("sPE")))
                k.setg(f"yT{nt % 4}", ("sPE", k.v("sPE")))

            emit_fc1(0)
            for nt in range(c.FNTH):
                if nt + 1 < c.FNTH:
                    emit_fc1(nt + 1)
                emit_fc2(nt)
            if half == 0:
                for tt2 in range(2):
                    for dmc in range(2):
                        g = 2 + 2 * tt2 + dmc
                        k.emit("tensor", [("dGaux", k.auxv)] if g == 2 else [],
                               lambda e, g=g, dmc=dmc: e.matmul(
                                   pb[g][:, :], F["ones_row"][0:1, 0:128],
                                   F["frow2"][0:1, 512 * dmc:512 * (dmc + 1)],
                                   start=False, stop=True, skip_group_check=True),
                               [("sPE", 1)])
                z_pe = k.v("sPE")
                for tt2 in range(2):
                    for dmc in range(2):
                        g = 2 + 2 * tt2 + dmc
                        zc = (2 * tc + tt2) * c.DM + 512 * dmc
                        a = k.emit("scalar",
                                   [("sPE", z_pe)] if (tt2 == 0 and dmc == 0) else [],
                                   lambda e, g=g, zc=zc: e.activation(
                                       out=F["zacc"][:, zc:zc + 512], in_=pb[g][:, :],
                                       func=AF.Copy),
                                   [("sACT", 1)])
                        k.addg(BANKN[g], ("sACT", a["sACT"]))
                k.info[("zev", tc)] = k.v("sACT")
            else:
                z_pe = k.v("sPE")
                for tt2 in range(2):
                    t2_v = 0
                    for dmc in range(2):
                        g = 2 + 2 * tt2 + dmc
                        zc = (2 * tc + tt2) * c.DM + 512 * dmc
                        wl = []
                        if dmc == 0:
                            wl = [("sPE", z_pe), ("sACT", k.info[("zev", tc)]),
                                  ("dLx1%d" % par, x1_v)] + k.gate("tfin")
                        a = k.emit("vector", wl,
                                   lambda e, g=g, zc=zc, dmc=dmc, tt2=tt2, par=par:
                                   e.scalar_tensor_tensor(
                                       out=F["tfin"][:, 512 * dmc:512 * (dmc + 1)],
                                       in0=F["zacc"][:, zc:zc + 512], scalar=0.25,
                                       in1=F["xr"][par][:, c.DM * tt2 + 512 * dmc:
                                                        c.DM * tt2 + 512 * (dmc + 1)],
                                       op0=OP.mult, op1=OP.add),
                                   [("sDVE", 1)])
                        a = k.emit("vector", [("sDVE", a["sDVE"])] + (k.gate("tfin2") if dmc == 0 else []),
                                   lambda e, g=g, dmc=dmc: e.scalar_tensor_tensor(
                                       out=F["tfin2"][:, 512 * dmc:512 * (dmc + 1)],
                                       in0=pb[g][:, :], scalar=0.25,
                                       in1=F["tfin"][:, 512 * dmc:512 * (dmc + 1)],
                                       op0=OP.mult, op1=OP.add),
                                   [("sDVE", 1)])
                        t2_v = a["sDVE"]
                        k.addg(BANKN[g], ("sDVE", t2_v))
                    k.setg("tfin", ("sDVE", t2_v))
                    if tt2 == 1:
                        k.setg(f"xr{par}", ("sDVE", t2_v))
                    a = k.emit("vector", [("sDVE", t2_v)] + k.gate("bns"),
                               lambda e: e.bn_stats(out=F["bns"][:, 0:6],
                                                    in_=F["tfin2"][:, 0:512]),
                               [("sDVE", 1)])
                    a = k.emit("vector", [("sDVE", a["sDVE"])],
                               lambda e: e.bn_stats(out=F["bns"][:, 6:12],
                                                    in_=F["tfin2"][:, 512:1024]),
                               [("sDVE", 1)])
                    a = k.emit("vector", [("sDVE", a["sDVE"])] + k.gate("lnt"),
                               lambda e: e.bn_aggr(out=F["lnt"][:, 0:2],
                                                   in_=F["bns"][:, :]),
                               [("sDVE", 1)])
                    st_v = a["sDVE"]
                    k.setg("bns", ("sDVE", st_v))
                    a = k.emit("scalar", [("sDVE", st_v)],
                               lambda e: e.activation(out=F["lnt"][:, 2:3],
                                                      in_=F["lnt"][:, 1:2],
                                                      func=AF.Ln, bias=EPS),
                               [("sACT", 1)])
                    a = k.emit("scalar", [("sACT", a["sACT"])],
                               lambda e: e.activation(out=F["lnt"][:, 5:6],
                                                      in_=F["lnt"][:, 2:3],
                                                      func=AF.Exp, scale=-0.5),
                               [("sACT", 1)])
                    opar = (2 * tc + tt2) % 2
                    a = k.emit("vector", [("sACT", a["sACT"]), ("sDVE", st_v)] + k.gate(f"outst{opar}"),
                               lambda e, opar=opar: e.tensor_scalar(
                                   out=F["outst"][opar][:, :], in0=F["tfin2"][:, :],
                                   scalar1=F["lnt"][:, 0:1], scalar2=F["lnt"][:, 5:6],
                                   op0=OP.subtract, op1=OP.mult),
                               [("sDVE", 1)])
                    k.setg("tfin2", ("sDVE", a["sDVE"]))
                    k.setg("lnt", ("sDVE", a["sDVE"]))
                    r1 = r0 + 128 * tt2
                    a2 = k.emit("gpsimd", [("sDVE", a["sDVE"])],
                                lambda e, r1=r1, opar=opar: e.dma_start(
                                    out=k.y[b, r1:r1 + 128, :],
                                    in_=F["outst"][opar][:, :]),
                                [("dGout", 16)])
                    if tt2 == 1:
                        k.setg("outst0", ("dGout", a2["dGout"]))
                        k.setg("outst1", ("dGout", a2["dGout"]))


def kernel(L0_attn_x, L0_attn_w_qkv, L0_attn_b_qkv, L0_attn_w_out, L0_attn_b_out,
           L0_ffn_ln_gamma, L0_ffn_ln_beta, L0_ffn_fc1_q, L0_ffn_fc1_scales,
           L0_ffn_fc1_bias, L0_ffn_fc2_q, L0_ffn_fc2_scales, L0_ffn_fc2_bias):
    cfg = Cfg()
    tabs = host_tables(cfg)
    wd = host_prep(cfg, L0_attn_w_qkv, L0_attn_b_qkv, L0_attn_w_out, L0_attn_b_out,
                   L0_ffn_ln_gamma, L0_ffn_ln_beta, L0_ffn_fc1_q, L0_ffn_fc1_scales,
                   L0_ffn_fc1_bias, L0_ffn_fc2_q, L0_ffn_fc2_scales, L0_ffn_fc2_bias)
    x = np.ascontiguousarray(np.asarray(L0_attn_x, dtype=np.float32))
    kk = build(cfg)
    shared = dict(tabs)
    shared.update(wd)
    in_maps = []
    for core in range(N_CORES):
        m = dict(shared)
        m["x"] = np.ascontiguousarray(x[4 * core:4 * core + 4])
        in_maps.append(m)
    res = run_bass_kernel_spmd(kk.nc, in_maps, core_ids=list(range(N_CORES)))
    return np.concatenate([r["y"] for r in res.results], axis=0)


# version 38
# speedup vs baseline: 1.1774x; 1.0080x over previous
import math
from contextlib import ExitStack

import numpy as np
import ml_dtypes

import concourse.bass as bass
import concourse.mybir as mybir
from concourse.bass_utils import run_bass_kernel_spmd

F32 = mybir.dt.float32
BF16 = mybir.dt.bfloat16
I32 = mybir.dt.int32
AF = mybir.ActivationFunctionType
OP = mybir.AluOpType
AX = mybir.AxisListType
bfdt = ml_dtypes.bfloat16

N_CORES = 8
DBG = False
DBG2 = False
DBG3 = 0
XBAR = 0
EPS = 1e-5
MAX_BIAS = 8.0
NEGBIG = -1.0e9
BANKN = ["pb0", "pb1", "pb2", "pb3", "pb4", "pb5", "pav0", "pav1"]


class Cfg:
    def __init__(self, B=4, S=2048, DM=1024, H=16, FF=4096):
        self.B, self.S, self.DM, self.H, self.FF = B, S, DM, H, FF
        self.DH = 64
        self.NP = H // 2
        self.DMT = DM // 128
        self.CH = 512
        self.NCH = S // self.CH
        self.CT = self.CH // 128
        self.ST = S // 128
        self.G = 128
        self.FH = FF // 2
        self.FNTH = self.FH // 128
        self.TCH = 256
        self.NTC = S // self.TCH
        self.VBW = 65 * H
        self.SCALE = 1.0 / math.sqrt(192.0)


def _slopes(n):
    def pow2(n):
        start = 2.0 ** (-(2.0 ** (-(math.log2(n) - 3))))
        return [start * start ** i for i in range(n)]
    if math.log2(n).is_integer():
        return pow2(n)
    cp2 = 2 ** math.floor(math.log2(n))
    return pow2(cp2) + _slopes(2 * cp2)[0::2][: n - cp2]


def host_tables(cfg):
    S, H, DH = cfg.S, cfg.H, cfg.DH
    sl = np.array(_slopes(H), dtype=np.float64)
    pos = np.arange(S, dtype=np.float64)
    inv_freq = 1.0 / 10000.0 ** (np.arange(0, DH, 2, dtype=np.float64) / DH)
    fr = np.outer(pos, inv_freq)
    sin = np.concatenate([np.sin(fr), np.sin(fr)], axis=1).T
    cos = np.concatenate([np.cos(fr), np.cos(fr)], axis=1).T
    sin_s = sin.copy()
    sin_s[: DH // 2] = -sin_s[: DH // 2]
    cos2 = np.concatenate([cos, cos], axis=0)
    sin2 = np.concatenate([sin_s, sin_s], axis=0)
    t_sincos = np.ascontiguousarray(
        np.concatenate([cos2, sin2], axis=1).astype(np.float32))
    NKT = cfg.ST
    t_hkt = np.zeros((128, H * NKT), dtype=np.float64)
    dk = np.arange(128, dtype=np.float64)
    for h in range(H):
        for kt in range(NKT):
            t_hkt[:, h * NKT + kt] = MAX_BIAS * sl[h] * (128 * kt + dk) - 24.0
    q = np.arange(S, dtype=np.float64)
    t_aug = (-math.sqrt(192.0) * MAX_BIAS * sl[:, None] * q[None, :]).astype(bfdt)
    t_band = np.where(dk[:, None] > dk[None, :], np.float32(NEGBIG),
                      np.float32(0.0)).astype(np.float32)
    t_perm = np.zeros((128, 128), dtype=np.float32)
    for (do, so) in ((0, 32), (32, 0), (64, 96), (96, 64)):
        for i in range(32):
            t_perm[so + i, do + i] = 1.0
    return dict(t_sincos=t_sincos, t_hkt=t_hkt.astype(np.float32),
                t_aug=np.ascontiguousarray(t_aug), t_band=t_band,
                t_ones_row=np.ones((1, 128), dtype=bfdt),
                t_ones2=np.ones((2, 64), dtype=bfdt),
                t_ones_col=np.ones((128, 1), dtype=bfdt),
                t_perm=np.ascontiguousarray(t_perm.astype(bfdt)),
                t_ident=np.eye(128, dtype=np.float32))


def host_prep(cfg, w_qkv, b_qkv, w_out, b_out, ln_gamma, ln_beta,
              fc1_q, fc1_scales, fc1_bias, fc2_q, fc2_scales, fc2_bias):
    DM, H = cfg.DM, cfg.H
    w_qkv = np.asarray(w_qkv)
    wq = np.asarray(w_qkv[:, :, 0:64], dtype=np.float32)
    wk = np.asarray(w_qkv[:, :, 64:128], dtype=np.float32)
    wq5 = wq.reshape(8, 128, 8, 2, 64).transpose(2, 1, 0, 3, 4).reshape(8, 128, 1024)
    wk5 = wk.reshape(8, 128, 8, 2, 64).transpose(2, 1, 0, 3, 4).reshape(8, 128, 1024)
    wqkA = np.ascontiguousarray(
        np.concatenate([wq5, wk5], axis=2).reshape(8 * 128, 2048))
    wv = np.zeros((DM, 65 * H), dtype=np.float32)
    for h in range(H):
        wv[:, 65 * h:65 * h + 64] = w_qkv[:, h, 128:192]
    b_qkv = np.asarray(b_qkv)
    bq = np.zeros((128, 2 * cfg.NP), dtype=np.float32)
    for j in range(cfg.NP):
        bq[0:64, j] = b_qkv[2 * j, 0:64]
        bq[64:128, j] = b_qkv[2 * j + 1, 0:64]
        bq[0:64, cfg.NP + j] = b_qkv[2 * j, 64:128]
        bq[64:128, cfg.NP + j] = b_qkv[2 * j + 1, 64:128]
    bva = np.zeros((1, 65 * H), dtype=np.float32)
    for h in range(H):
        bva[0, 65 * h:65 * h + 64] = b_qkv[h, 128:192]
        bva[0, 65 * h + 64] = 1.0
    fc1_q = np.asarray(fc1_q, dtype=np.int32)
    fc2_q = np.asarray(fc2_q, dtype=np.int32)
    fc1qt = np.ascontiguousarray(
        fc1_q.reshape(8, 128, 8, 512).transpose(2, 0, 1, 3).reshape(64 * 128, 512))
    fc2qt = np.ascontiguousarray(
        fc2_q.reshape(32, 128, 2, 512).transpose(0, 2, 1, 3).reshape(64 * 128, 512))
    return dict(
        wqkA=wqkA, wvaug=wv, bqk_t=bq, bvaug=bva, fc1qt=fc1qt, fc2qt=fc2qt,
        wout=np.ascontiguousarray(np.asarray(w_out).reshape(H * 64, DM), dtype=np.float32),
        bout=np.ascontiguousarray(np.asarray(b_out).reshape(1, DM), dtype=np.float32),
        gammacol=np.ascontiguousarray(np.asarray(ln_gamma).reshape(cfg.DMT, 128).T, dtype=np.float32),
        betacol=np.ascontiguousarray(np.asarray(ln_beta).reshape(cfg.DMT, 128).T, dtype=np.float32),
        fc1sc=np.ascontiguousarray(np.asarray(fc1_scales), dtype=np.float32),
        fc1b=np.ascontiguousarray(np.asarray(fc1_bias).reshape(1, cfg.FF), dtype=np.float32),
        fc2sc=np.ascontiguousarray(np.asarray(fc2_scales), dtype=np.float32),
        fc2b=np.ascontiguousarray(np.asarray(fc2_bias).reshape(1, cfg.DM), dtype=np.float32))


ENGS = ("tensor", "scalar", "vector", "sync", "gpsimd")
BARS = {"tensor": "bar_pe", "scalar": "bar_act", "vector": "bar_dve",
        "sync": "bar_sp", "gpsimd": "bar_pl"}


class K:
    def __init__(self, cfg):
        self.cfg = cfg
        self.nc = bass.Bass("TRN2", target_bir_lowering=False)
        self.stack = ExitStack()
        self.sh = {}
        self.scnt = {}
        self.barn = 0
        self.blk = None
        self._bufs = {}
        self.g = {}
        self.info = {}

    def sem(self, name):
        self.sh[name] = self.stack.enter_context(self.nc.semaphore(name))
        self.scnt[name] = 0

    def v(self, name):
        return self.scnt[name]

    def flush(self):
        for engname in ENGS:
            fns = self._bufs.get(engname)
            if fns:
                getattr(self.blk, engname)(lambda eng, fns=fns: [f(eng) for f in fns])
                self._bufs[engname] = []

    def on(self, engine, fn):
        self._bufs.setdefault(engine, []).append(fn)

    def emit(self, engine, waits, mk, incs=()):
        after = {}
        for s, a in incs:
            self.scnt[s] += a
            after[s] = self.scnt[s]
        wd = {}
        for s, v0 in waits:
            if v0 > 0:
                wd[s] = max(wd.get(s, 0), v0)
        sh = self.sh

        def fn(eng, wd=wd, mk=mk, incs=incs):
            for s, v0 in wd.items():
                eng.wait_ge(sh[s], v0)
            ins = mk(eng)
            for s, a in incs:
                if ins is None:
                    eng.sem_inc(sh[s], a)
                else:
                    ins.then_inc(sh[s], a)
        self.on(engine, fn)
        return after

    def gate(self, name):
        return list(self.g.get(name, {}).items())

    def gates(self, *names):
        out = []
        for n in names:
            out += self.gate(n)
        return out

    def setg(self, name, *pairs):
        d = {}
        for s, v0 in pairs:
            if v0 > 0:
                d[s] = max(d.get(s, 0), v0)
        self.g[name] = d

    def addg(self, name, *pairs):
        d = self.g.setdefault(name, {})
        for s, v0 in pairs:
            if v0 > 0:
                d[s] = max(d.get(s, 0), v0)

    def barrier(self, drain=True):
        self.barn += 1
        n = self.barn
        qsems = {"sync": [s for s in self.sh if s.startswith("dL") or s.startswith("dT")],
                 "gpsimd": [s for s in self.sh if s.startswith("dG")],
                 "scalar": [s for s in self.sh if s.startswith("dA")]}
        for e in ENGS:
            drains = [(s, self.scnt[s]) for s in qsems.get(e, []) if self.scnt[s] > 0] \
                if drain else []
            def f(eng, e=e, n=n, drains=drains):
                for s, v0 in drains:
                    eng.wait_ge(self.sh[s], v0)
                eng.sem_inc(self.sh[BARS[e]], 1)
                for e2 in ENGS:
                    if e2 != e:
                        eng.wait_ge(self.sh[BARS[e2]], n)
            self.on(e, f)
        for e in ENGS:
            self.scnt[BARS[e]] += 1
        self.flush()

    def sb(self, name, shape, dtype, stack=None):
        return (stack or self.stack).enter_context(self.nc.sbuf_tensor(name, shape, dtype))

    def ps(self, name, shape, dtype=F32, stack=None):
        return (stack or self.stack).enter_context(self.nc.psum_tensor(name, shape, dtype))


def build(cfg):
    k = K(cfg)
    nc = k.nc
    c = cfg
    NKT = c.ST

    din = {}
    def I(name, shape, dt=F32):
        din[name] = nc.dram_tensor(name, shape, dt, kind="ExternalInput").ap()
    I("x", [c.B, c.S, c.DM])
    I("wqkA", [c.NP * 128, 2048]); I("wvaug", [c.DM, 65 * c.H])
    I("bqk_t", [128, 2 * c.NP]); I("bvaug", [1, 65 * c.H])
    I("wout", [c.H * 64, c.DM]); I("bout", [1, c.DM])
    I("gammacol", [128, c.DMT]); I("betacol", [128, c.DMT])
    I("fc1qt", [64 * 128, 512], I32); I("fc1sc", [c.DM // c.G, c.FF]); I("fc1b", [1, c.FF])
    I("fc2qt", [64 * 128, 512], I32); I("fc2sc", [c.FF // c.G, c.DM]); I("fc2b", [1, c.DM])
    I("t_sincos", [128, 2 * c.S]); I("t_hkt", [128, c.H * NKT])
    I("t_aug", [c.H, c.S], BF16); I("t_band", [128, 128])
    I("t_ones_row", [1, 128], BF16); I("t_ones2", [2, 64], BF16)
    I("t_ones_col", [128, 1], BF16)
    I("t_perm", [128, 128], BF16)
    I("t_ident", [128, 128])
    y = nc.dram_tensor("y", [c.B, c.S, c.DM], F32, kind="ExternalOutput").ap()
    s_w1 = nc.dram_tensor("s_w1", [c.DM, c.FF], BF16, kind="Internal").ap()
    s_wqk = nc.dram_tensor("s_wqk", [c.NP * 128, 2048], BF16, kind="Internal").ap()
    s_w2 = nc.dram_tensor("s_w2", [c.FF, c.DM], BF16, kind="Internal").ap()
    s_fc1b = nc.dram_tensor("s_fc1b", [1, c.FF], F32, kind="Internal").ap()
    s_x1 = nc.dram_tensor("s_x1", [c.B, c.S, c.DM], F32,
                          kind="ExternalOutput" if DBG else "Internal").ap()
    s_xh2 = nc.dram_tensor("s_xh2", [c.B, c.S, c.DM], BF16,
                           kind="ExternalOutput" if DBG else "Internal").ap()
    d_atc = None
    if DBG:
        d_atc = nc.dram_tensor("d_atc", [c.NCH, c.NP, 128, c.CH], BF16,
                               kind="ExternalOutput").ap()
        if DBG2:
            if DBG3 & 2:
                k.d_sc = nc.dram_tensor("d_sc", [128, 512], F32, kind="ExternalOutput").ap()
                k.d_scb = k.sb("d_scb", [128, 512], F32)
            if DBG3 & 1:
                k.d_p0 = nc.dram_tensor("d_p0", [128, 512], BF16, kind="ExternalOutput").ap()
                k.d_pbuf = k.sb("d_pbuf", [128, 512], BF16)
            if DBG3 & 4:
                k.d_av = nc.dram_tensor("d_av", [128, 512], F32, kind="ExternalOutput").ap()
                k.d_avb = k.sb("d_avb", [128, 512], F32)
            if DBG3 & 16:
                k.d_vb = nc.dram_tensor("d_vb", [128, 2080], BF16, kind="ExternalOutput").ap()
            if DBG3 & 8:
                k.d_rb = nc.dram_tensor("d_rb", [64, 512], F32, kind="ExternalOutput").ap()
                k.d_rcp = nc.dram_tensor("d_rcp", [1, 512], F32, kind="ExternalOutput").ap()
                k.d_rbb = k.sb("d_rbb", [64, 512], F32)
                k.d_rcpb = k.sb("d_rcpb", [1, 512], F32)
    k.d_atc = d_atc
    k.din, k.y = din, y
    k.s = dict(w1=s_w1, w2=s_w2, fc1b=s_fc1b, x1=s_x1, xh2=s_xh2, wqk=s_wqk)

    S_ = {}
    S_["hkt"] = k.sb("hkt", [128, c.H * NKT], F32)
    S_["band"] = k.sb("band", [128, 128], F32)
    S_["ones_row"] = k.sb("ones_row", [1, 128], BF16)
    S_["ones2"] = k.sb("ones2", [2, 64], BF16)
    S_["ones_col"] = k.sb("ones_col", [128, 1], BF16)
    S_["bqk"] = k.sb("bqk", [128, 2 * c.NP], F32)
    S_["wvs"] = k.sb("wvs", [128, c.DMT * c.VBW], BF16)
    S_["identf"] = k.sb("identf", [128, 128], F32)
    S_["bvaugb"] = k.sb("bvaugb", [1, 65 * c.H], BF16)
    S_["boutb"] = k.sb("boutb", [1, c.DM], BF16)
    S_["frow2"] = k.sb("frow2", [1, c.DM], BF16)
    S_["fc1bcol"] = k.sb("fc1bcol", [128, c.FF // 128], F32)
    
    S_["permb"] = k.sb("permb", [128, 128], BF16)
    S_["probs"] = [k.sb(f"probs{i}", [128, 512], BF16) for i in range(2)]
    S_["bns"] = k.sb("bns", [128, 12], F32)
    S_["lnt"] = k.sb("lnt", [128, 8], F32)

    S_["ones2f"] = k.sb("ones2f", [1, 64], F32)
    cst = k.sb("cst", [128, 4], F32)
    for i_, v_ in enumerate([EPS, EPS * EPS, 1.0 + EPS, -128.0]):
        nc.const_aps.register(F32, v_, cst[:, i_:i_ + 1])
    S_["cst"] = cst
    pbk = [k.ps(f"pb{i}", [128, 512]) for i in range(6)]
    pavk = [k.ps(f"pav{i}", [128, 512]) for i in range(2)]
    S_["pb"], S_["pav"] = pbk, pavk
    S_["banks"] = pbk + pavk

    for e in ENGS:
        k.sem(BARS[e])
    for s in ["sPE", "sACT", "sDVE", "sPL",
              "dLx0", "dLx1", "dLsc", "dLtab", "dLq0", "dLq1", "dLq2", "dLq3",
              "dGq2", "dGq3", "dLfb",
              "dTx0", "dTx1", "dLxr0", "dLxr1", "dLw0", "dLw1",
              "dLxh0", "dLxh1", "dLx10", "dLx11",
              "dGw0", "dGw1", "dGv", "dGsw0", "dGsw1", "dGsb", "dGsp", "dGout", "dGq0", "dGq1",
              "dGsplit0", "dGsplit1", "dGo0", "dGo1", "dGaux"]:
        k.sem(s)

    with nc.Block() as blk:
        k.blk = blk
        for i_, v_ in enumerate([EPS, EPS * EPS, 1.0 + EPS, -128.0]):
            k.emit("vector", [], lambda e, i_=i_, v_=v_: e.memset(cst[:, i_:i_ + 1], v_),
                   [("sDVE", 1)])
        k.emit("vector", [], lambda e: e.memset(S_["ones2f"][:, :], 1.0), [("sDVE", 1)])
        for (dst, src) in [(S_["hkt"][:, :], din["t_hkt"][:, :]),
                           (S_["band"][:, :], din["t_band"][:, :]),
                           (S_["ones_row"][:, :], din["t_ones_row"][:, :]),
                           (S_["ones2"][:, :], din["t_ones2"][:, :]),
                           (S_["ones_col"][:, :], din["t_ones_col"][:, :]),
                           (S_["identf"][:, :], din["t_ident"][:, :]),
                           (S_["permb"][:, :], din["t_perm"][:, :]),
                           (S_["bqk"][:, :], din["bqk_t"][:, :])]:
            k.emit("sync", [], lambda e, d=dst, s=src: e.dma_start(out=d, in_=s),
                   [("dLtab", 16)])
        for (dst, src) in [(S_["bvaugb"][:, :], din["bvaug"][:, :]),
                           (S_["boutb"][:, :], din["bout"][:, :]),
                           (S_["frow2"][:, :], din["fc2b"][:, :])]:
            k.emit("gpsimd", [], lambda e, d=dst, s=src: e.dma_start(out=d, in_=s),
                   [("dGaux", 16)])

        for d in range(c.DMT):
            k.emit("gpsimd", [],
                   lambda e, d=d: e.dma_start(out=S_["wvs"][:, d * c.VBW:(d + 1) * c.VBW],
                                              in_=din["wvaug"][128 * d:128 * (d + 1), :]),
                   [("dGv", 16)])
        emit_stage0(k, S_)
        k.barrier(drain=True)

        for b in range(c.B):
            with ExitStack() as ab:
                A = dict(S_)
                A["kh"] = [k.sb(f"kh{h}_b{b}", [65, c.S], BF16, ab) for h in range(c.H)]
                A["vb"] = k.sb(f"vb_b{b}", [128, c.ST * c.VBW], BF16, ab)
                A["atc"] = [k.sb(f"atc{p}_b{b}", [128, c.CH], BF16, ab)
                            for p in range(c.NP)]
                A["atcf"] = k.sb(f"atcf_b{b}", [128, c.CH], F32, ab)
                A["rsrow"] = k.sb(f"rsrow_b{b}", [1, 1024], F32, ab)
                A["rb"] = k.sb(f"rb_b{b}", [128, 512], F32, ab)
                A["qh"] = [[k.sb(f"qh{par}_{hh}_b{b}", [65, c.CH], BF16, ab)
                            for hh in range(2)] for par in range(2)]
                A["xtc"] = k.sb(f"xtc_b{b}", [128, c.DMT * c.CH], BF16, ab)
                A["xbc"] = [k.sb(f"xbc{i}_b{b}", [128, c.DM], BF16, ab) for i in range(2)]
                A["xpc"] = [k.sb(f"xpc{i}_b{b}", [128, c.DM], F32, ab) for i in range(2)]
                A["wqkp"] = [k.sb(f"wqkp{i}_b{b}", [128, 2 * c.DMT * 128], BF16, ab)
                             for i in range(2)]
                A["scb"] = [k.sb(f"scb{i}_b{b}", [128, 2 * c.CH], F32, ab)
                            for i in range(2)]
                A["qbb"] = k.sb(f"qbb_b{b}", [128, 2 * c.CH], BF16, ab)
                A["t2"] = k.sb(f"t2_b{b}", [128, c.CH], F32, ab)
                A["t1"] = k.sb(f"t1_b{b}", [128, c.CH], F32, ab)
                A["rout"] = [k.sb(f"rout{i}_b{b}", [128, c.CH], BF16, ab)
                             for i in range(2)]
                A["wob"] = [k.sb(f"wob{i}_b{b}", [128, c.DM], BF16, ab) for i in range(2)]
                A["y1t"] = k.sb(f"y1t_b{b}", [128, c.DM], F32, ab)
                A["x1s"] = k.sb(f"x1s_b{b}", [128, c.DM], F32, ab)
                A["xh2s"] = k.sb(f"xh2s_b{b}", [128, c.DM], BF16, ab)
                emit_A(k, b, A)
                k.barrier()

            with ExitStack() as fstk:
                F = dict(S_)
                F["w1h"] = k.sb(f"w1h_b{b}", [128, c.DMT * c.FH], BF16, fstk)
                F["w2h"] = k.sb(f"w2h_b{b}", [128, c.FNTH * c.DM], BF16, fstk)
                F["yT"] = k.sb(f"yT_b{b}", [128, c.FNTH * c.TCH], BF16, fstk)
                F["xh2T"] = [k.sb(f"xh2T{i}_b{b}", [128, c.DMT * c.TCH], BF16, fstk)
                             for i in range(2)]
                F["xr"] = [k.sb(f"xr{i}_b{b}", [128, 2 * c.DM], F32, fstk)
                           for i in range(2)]
                F["zacc"] = k.sb(f"zacc_b{b}", [128, c.ST * c.DM], F32, fstk)
                F["tfin"] = k.sb(f"tfin_b{b}", [128, c.DM], F32, fstk)
                F["tfin2"] = k.sb(f"tfin2_b{b}", [128, c.DM], F32, fstk)
                F["outst"] = [k.sb(f"outst{i}_b{b}", [128, c.DM], F32, fstk)
                              for i in range(2)]
                emit_F(k, b, F)
                k.barrier()

        k.emit("gpsimd", [("dGout", k.v("dGout"))], lambda e: e.engine_nop(), [])
        k.flush()
    k.stack.close()
    return k


def emit_stage0(k, S_):
    c = k.cfg
    din = k.din
    pb = S_["pb"]
    with ExitStack() as st0:
        qsb = [k.sb(f"qsb{i}", [128, 512], I32, st0) for i in range(4)]
        scr = [k.sb(f"scr{i}", [1, 512], BF16, st0) for i in range(4)]
        qf = [k.sb(f"qf{i}", [128, 512], F32, st0) for i in range(2)]
        plainb = [k.sb(f"plainb{i}", [128, 512], BF16, st0) for i in range(2)]
        w1st = [k.sb(f"w1st{i}", [128, 512], BF16, st0) for i in range(2)]
        gcolb = k.sb("gcolb", [128, c.DMT], F32, st0)
        bcolb = k.sb("bcolb", [128, c.DMT], BF16, st0)
        fbrow = k.sb("fbrow", [1, 512], F32, st0)
        fbrow2 = k.sb("fbrow2", [1, 512], F32, st0)

        k.emit("sync", [], lambda e: e.dma_start(out=gcolb[:, :], in_=din["gammacol"][:, :]),
               [("dLtab", 16)])
        k.emit("gpsimd", [], lambda e: e.dma_start(out=bcolb[:, :], in_=din["betacol"][:, :]),
               [("dGaux", 16)])
        tab_v = k.v("dLtab")
        aux_v = k.v("dGaux")
        k.auxv = aux_v
        k.tabv = tab_v

        cvt = [k.sb(f"cvt{i}", [128, 2048], BF16, st0) for i in range(2)]
        for bi_ in range(c.NP):
            cp_ = bi_ % 2
            a = k.emit("gpsimd", k.gate(f"cvt{cp_}"),
                       lambda e, bi_=bi_, cp_=cp_: e.dma_start(
                           out=cvt[cp_][:, :],
                           in_=din["wqkA"][128 * bi_:128 * (bi_ + 1), :]),
                       [("dGv", 16)])
            gv_ = a["dGv"]
            a = k.emit("sync", [("dGv", gv_)],
                       lambda e, bi_=bi_, cp_=cp_: e.dma_start(
                           out=k.s["wqk"][128 * bi_:128 * (bi_ + 1), :],
                           in_=cvt[cp_][:, :]),
                       [("dLfb", 16)])
            k.setg(f"cvt{cp_}", ("dLfb", a["dLfb"]))

        tiles = [("w1", ncq, d) for ncq in range(c.FF // 512) for d in range(c.DMT)]
        tiles += [("w2", nt, dc) for nt in range(c.FF // 128) for dc in range(c.DM // 512)]

        for i, (kind, a_, b_) in enumerate(tiles):
            lp = i % 4
            par = i % 2
            if kind == "w1":
                ti = a_ * 8 + b_
                src_q = din["fc1qt"][128 * ti:128 * (ti + 1), :]
                src_s = din["fc1sc"][b_:b_ + 1, 512 * a_:512 * (a_ + 1)]
            else:
                ti = a_ * 2 + b_
                src_q = din["fc2qt"][128 * ti:128 * (ti + 1), :]
                src_s = din["fc2sc"][a_:a_ + 1, 512 * b_:512 * (b_ + 1)]
            k.emit("sync" if lp < 2 else "scalar", k.gate(f"qsb{lp}"),
                   lambda e, s=src_q, p=lp: e.dma_start(out=qsb[p][:, :], in_=s),
                   [("dLq%d" % lp, 16)])
            k.emit("gpsimd", k.gate(f"scr{lp}"),
                   lambda e, s=src_s, p=lp: e.dma_start(out=scr[p][:, :], in_=s),
                   [("dGq%d" % lp, 16)])
            ldv = k.v("dLq%d" % lp)
            gqv = k.v("dGq%d" % lp)
            a1 = k.emit("tensor", [("dGq%d" % lp, gqv),
                                  ("dLtab", tab_v)] + k.gate(f"s0pb{par}"),
                        lambda e, p=par, lq=lp: e.matmul(pb[p][:, :],
                                                         S_["ones_row"][0:1, 0:128],
                                                         scr[lq][0:1, :],
                                                         start=True, stop=True),
                        [("sPE", 1)])
            a2 = k.emit("scalar", [("dLq%d" % lp, ldv), ("sDVE", 4)] + k.gate(f"qf{par}"),
                        lambda e, p=par, lq=lp: e.activation(out=qf[p][:, :],
                                                             in_=qsb[lq][:, :],
                                                             func=AF.Copy, bias=-128.0),
                        [("sACT", 1)])
            k.setg(f"qsb{lp}", ("sACT", a2["sACT"]))
            k.setg(f"scr{lp}", ("sPE", a1["sPE"]))
            a3 = k.emit("vector", [("sACT", a2["sACT"]), ("sPE", a1["sPE"])] +
                        k.gate(f"plainb{par}"),
                        lambda e, p=par: e.tensor_tensor(out=plainb[p][:, :],
                                                         in0=qf[p][:, :],
                                                         in1=pb[p][:, :], op=OP.mult),
                        [("sDVE", 1)])
            k.setg(f"qf{par}", ("sDVE", a3["sDVE"]))
            k.setg(f"s0pb{par}", ("sDVE", a3["sDVE"]))
            if kind == "w1":
                a4 = k.emit("tensor", [("sDVE", a3["sDVE"]), ("dGaux", aux_v)] +
                            (k.gate("fbpsum") if b_ == 0 else []),
                            lambda e, d=b_, p=par: e.matmul(pb[2][0:1, 0:512],
                                                            bcolb[:, d:d + 1],
                                                            plainb[p][:, :], start=(d == 0),
                                                            stop=(d == c.DMT - 1)),
                            [("sPE", 1)])
                a5 = k.emit("vector", [("dLtab", tab_v)] + k.gate(f"w1st{par}"),
                            lambda e, d=b_, p=par: e.tensor_scalar(
                                out=w1st[p][:, :], in0=plainb[p][:, :],
                                scalar1=gcolb[:, d:d + 1],
                                scalar2=None, op0=OP.mult),
                            [("sDVE", 1)])
                k.setg(f"plainb{par}", ("sDVE", a5["sDVE"]), ("sPE", a4["sPE"]))
                wdst = k.s["w1"][128 * b_:128 * (b_ + 1), 512 * a_:512 * (a_ + 1)]
                a6 = k.emit("gpsimd", [("sDVE", a5["sDVE"])],
                            lambda e, d=wdst, p=par: e.dma_start(out=d, in_=w1st[p][:, :]),
                            [("dGsw%d" % par, 16)])
                k.setg(f"w1st{par}", ("dGsw%d" % par, a6["dGsw%d" % par]))
                if b_ == c.DMT - 1:
                    ncq = a_
                    a7 = k.emit("scalar", [("sPE", a4["sPE"])] + k.gate("fbrow2"),
                                lambda e: e.activation(out=fbrow2[:, :],
                                                       in_=pb[2][0:1, 0:512], func=AF.Copy),
                                [("sACT", 1)])
                    k.setg("fbpsum", ("sACT", a7["sACT"]))
                    a8 = k.emit("sync", k.gate("fbrow"),
                                lambda e, n=ncq: e.dma_start(
                                    out=fbrow[:, :],
                                    in_=din["fc1b"][:, 512 * n:512 * (n + 1)]),
                                [("dLfb", 16)])
                    a9 = k.emit("vector", [("sACT", a7["sACT"]), ("dLfb", a8["dLfb"])],
                                lambda e: e.tensor_tensor(out=fbrow2[:, :], in0=fbrow2[:, :],
                                                          in1=fbrow[:, :], op=OP.add),
                                [("sDVE", 1)])
                    k.setg("fbrow", ("sDVE", a9["sDVE"]))
                    a10 = k.emit("gpsimd", [("sDVE", a9["sDVE"])],
                                 lambda e, n=ncq: e.dma_start(
                                     out=k.s["fc1b"][:, 512 * n:512 * (n + 1)],
                                     in_=fbrow2[:, :]),
                                 [("dGsb", 16)])
                    k.setg("fbrow2", ("dGsb", a10["dGsb"]))
            else:
                wdst = k.s["w2"][128 * a_:128 * (a_ + 1), 512 * b_:512 * (b_ + 1)]
                a6 = k.emit("gpsimd", [("sDVE", a3["sDVE"])],
                            lambda e, d=wdst, p=par: e.dma_start(out=d, in_=plainb[p][:, :]),
                            [("dGsw%d" % par, 16)])
                k.setg(f"plainb{par}", ("dGsw%d" % par, a6["dGsw%d" % par]))
        sw0, sw1, sbv = k.v("dGsw0"), k.v("dGsw1"), k.v("dGsb")
        for j in range(c.FF // 128):
            k.emit("sync", [("dGsw0", sw0), ("dGsw1", sw1), ("dGsb", sbv)] if j == 0 else [],
                   lambda e, j=j: e.dma_start(
                       out=S_["fc1bcol"][:, j:j + 1],
                       in_=k.s["fc1b"][:, 128 * j:128 * (j + 1)].rearrange(
                           "o (p q) -> p (o q)", q=1)),
                   [("dLtab", 16)])


def emit_A(k, b, A):
    c = k.cfg
    din = k.din
    pb, pav = A["pb"], A["pav"]

    for h in range(c.H):
        k.emit("gpsimd" if h % 2 == 0 else "vector", [],
               lambda e, h=h: e.memset(A["kh"][h][64:65, :], 1.0),
               [("sPL", 1)])
    k.info["khones"] = k.v("sPL")
    if DBG2 and (DBG3 & 4) and b == 0:
        k.emit("vector", [], lambda e: e.memset(k.d_avb[:, :], 0.0), [("sDVE", 1)])

    def a1sc(ch):
        q0 = ch * c.CH
        sc2 = A["scb"][ch % 2]
        for tt in range(c.CT):
            par = tt % 2
            a = k.emit("sync", k.gate(f"xpc{par}"),
                       lambda e, tt=tt, par=par, q0=q0: e.dma_start(
                           out=A["xpc"][par][:, :],
                           in_=din["x"][b, q0 + 128 * tt:q0 + 128 * (tt + 1), :]),
                       [("dLx%d" % par, 16)])
            ldv = a["dLx%d" % par]
            for rnd in range(2):
                tb = pb[3 + rnd]
                for dd in range(4):
                    d = 4 * rnd + dd
                    wl = []
                    if dd == 0:
                        wl = [("dLx%d" % par, ldv), ("dLtab", k.tabv)] + k.gate(BANKN[3 + rnd])
                    k.emit("tensor", wl,
                           lambda e, d=d, dd=dd, par=par, tb=tb: e.transpose(
                               tb[:, 128 * dd:128 * (dd + 1)],
                               A["xpc"][par][:, 128 * d:128 * (d + 1)],
                               A["identf"][:, :]),
                           [("sPE", 1)])
                a2 = k.emit("scalar", [("sPE", k.v("sPE"))] + (k.gate("xtc") if tt == 0 and rnd == 0 else []),
                            lambda e, rnd=rnd, tt=tt, tb=tb: e.activation(
                                out=bass.AP(A["xtc"], 4 * rnd * c.CH + 128 * tt,
                                            [[c.DMT * c.CH, 128], [c.CH, 4], [1, 128]]),
                                in_=tb[:, 0:512], func=AF.Copy),
                            [("sACT", 1)])
                k.addg(BANKN[3 + rnd], ("sACT", a2["sACT"]))
            k.setg(f"xpc{par}", ("sPE", k.v("sPE")))
        k.info[("xtcok", ch)] = k.v("sACT")
        a = k.emit("sync", k.gate(f"scb{ch % 2}"),
                   lambda e, q0=q0, sc2=sc2: e.dma_start(
                       out=sc2[:, 0:c.CH], in_=din["t_sincos"][:, q0:q0 + c.CH]),
                   [("dLsc", 16)])
        k.emit("sync", [],
               lambda e, q0=q0, sc2=sc2: e.dma_start(
                   out=sc2[:, c.CH:2 * c.CH],
                   in_=din["t_sincos"][:, c.S + q0:c.S + q0 + c.CH]),
               [("dLsc", 16)])
        k.info[("scv", ch)] = k.v("dLsc")

    a1sc(0)
    for ch in range(c.NCH):
        q0 = ch * c.CH
        xtc_ok = k.info[("xtcok", ch)]
        sc_v = k.info[("scv", ch)]
        if XBAR >= 4:
            k.barrier()
        if c.VBW <= 512:
            pieces = [(0, c.VBW)]
        elif c.VBW <= 1024:
            pieces = [(0, 512), (512, c.VBW - 512)]
        else:
            pieces = [(0, 512), (512, 512), (1024, c.VBW - 1024)]
        vgi = 0
        for tt in range(c.CT):
            ktg = ch * c.CT + tt
            for (off, wdt) in pieces:
                bi = 3 + (vgi % 3)
                bank = pb[bi]
                for d in range(c.DMT):
                    wl = []
                    if d == 0:
                        wl = [("sACT", xtc_ok),
                              ("dGaux", k.auxv)] + k.gate(BANKN[bi])
                    k.emit("tensor", wl,
                           lambda e, d=d, tt=tt, off=off, wdt=wdt, bank=bank: e.matmul(
                               bank[:, 0:wdt],
                               A["xtc"][:, c.CH * d + 128 * tt:c.CH * d + 128 * (tt + 1)],
                               A["wvs"][:, c.VBW * d + off:c.VBW * d + off + wdt],
                               start=(d == 0), stop=False),
                           [("sPE", 1)])
                a = k.emit("tensor", [],
                           lambda e, off=off, wdt=wdt, bank=bank: e.matmul(
                               bank[:, 0:wdt], A["ones_row"][0:1, 0:128],
                               A["bvaugb"][0:1, off:off + wdt], start=False, stop=True),
                           [("sPE", 1)])
                a2 = k.emit("scalar", [("sPE", a["sPE"])],
                            lambda e, off=off, wdt=wdt, bank=bank, ktg=ktg: e.activation(
                                out=A["vb"][:, c.VBW * ktg + off:c.VBW * ktg + off + wdt],
                                in_=bank[:, 0:wdt], func=AF.Copy),
                            [("sACT", 1)])
                k.addg(BANKN[bi], ("sACT", a2["sACT"]))
                vgi += 1
        k.info["vb_ready"] = k.v("sACT")

        if XBAR >= 2:
            k.barrier()
        for s in range(c.NP + 1):
            if s < c.NP:
                emit_pair_qk_part1(k, b, ch, s, A, sc_v, xtc_ok)
            if s > 0:
                emit_pair_attention(k, b, ch, s - 1, A)
        if DBG and b == 0:
            for p in range(c.NP):
                k.emit("gpsimd", k.gate("atc_all"),
                       lambda e, p=p, ch=ch: e.dma_start(
                           out=k.d_atc[ch, p, :, :], in_=A["atc"][p][:, :]),
                       [("dGsp", 16)])
        if ch + 1 < c.NCH:
            a1sc(ch + 1)
        emit_outproj_chunk(k, b, ch, A)
        if XBAR >= 1:
            k.barrier()
    if DBG2 and b == 0:
        w0 = [("sDVE", k.v("sDVE")), ("sACT", k.v("sACT"))]
        if DBG3 & 2:
            k.emit("gpsimd", w0,
                   lambda e: e.dma_start(out=k.d_sc[:, :], in_=k.d_scb[:, :]), [("dGsp", 16)])
            w0 = []
        if DBG3 & 1:
            k.emit("gpsimd", w0,
                   lambda e: e.dma_start(out=k.d_p0[:, :], in_=k.d_pbuf[:, :]), [("dGsp", 16)])
            w0 = []
        if DBG3 & 4:
            k.emit("gpsimd", w0,
                   lambda e: e.dma_start(out=k.d_av[:, :], in_=k.d_avb[:, :]), [("dGsp", 16)])
            w0 = []
        if DBG3 & 16:
            k.emit("gpsimd", w0,
                   lambda e: e.dma_start(out=k.d_vb[:, :], in_=A["vb"][:, 0:2 * c.VBW]),
                   [("dGsp", 16)])
            w0 = []
        if DBG3 & 8:
            k.emit("gpsimd", w0,
                   lambda e: e.dma_start(out=k.d_rb[:, :], in_=k.d_rbb[:, :]), [("dGsp", 16)])
            k.emit("gpsimd", [],
                   lambda e: e.dma_start(out=k.d_rcp[:, :], in_=k.d_rcpb[:, :]), [("dGsp", 16)])


def emit_pair_qk_part1(k, b, ch, j, A, sc_v, xtc_ok):
    c = k.cfg
    din = k.din
    q0 = ch * c.CH
    NKT = c.ST
    slot = ch * c.NP + j
    par = slot % 2
    pb = A["pb"]
    qh, kh = A["qh"], A["kh"]
    spl = "dGsplit%d" % par
    WQ = c.DMT * 128

    k.emit("sync", k.gate(f"wqkp{par}"),
           lambda e, par=par, j=j: e.dma_start(
               out=A["wqkp"][par][:, :],
               in_=k.s["wqk"][128 * j:128 * (j + 1), :]),
           [("dGw%d" % par, 16)])
    wqv = k.v("dGw%d" % par)

    for hh in range(2):
        k.emit("gpsimd", k.gate(f"qh{par}") if hh == 0 else [],
               lambda e, hh=hh, par=par, j=j, q0=q0: e.dma_start(
                   out=qh[par][hh][64:65, :],
                   in_=din["t_aug"][2 * j + hh:2 * j + hh + 1, q0:q0 + c.CH]),
               [(spl, 16)])

    pe_done = {}
    for (idx, bank, colbase) in ((0, pb[0], 0), (1, pb[1], WQ)):
        for d in range(c.DMT):
            wl = []
            if d == 0:
                wl = [("dGw%d" % par, wqv), ("sACT", xtc_ok)] + k.gate(BANKN[idx])
            k.emit("tensor", wl,
                   lambda e, d=d, bank=bank, cb=colbase, par=par: e.matmul(
                       bank[:, :], A["wqkp"][par][:, cb + 128 * d:cb + 128 * (d + 1)],
                       A["xtc"][:, c.CH * d:c.CH * (d + 1)],
                       start=(d == 0), stop=(d == c.DMT - 1)),
                   [("sPE", 1)])
        pe_done[idx] = k.v("sPE")
    k.setg(f"wqkp{par}", ("sPE", pe_done[1]))
    k.addg("xtc", ("sPE", pe_done[1]))

    for idx, bank, bcol in ((0, pb[0], j), (1, pb[1], c.NP + j)):
        qo = 512 * idx
        a = k.emit("scalar", [("sPE", pe_done[idx])] + k.gate(f"qbb_{idx}"),
                   lambda e, bank=bank, bcol=bcol, qo=qo: e.activation(
                       out=A["qbb"][:, qo:qo + c.CH], in_=bank[:, :],
                       func=AF.Identity, bias=A["bqk"][:, bcol:bcol + 1]),
                   [("sACT", 1)])
        ev = a["sACT"]
        a = k.emit("tensor", [("sACT", ev)],
                   lambda e, bank=bank, qo=qo: e.matmul(
                       bank[:, :], A["permb"][:, :], A["qbb"][:, qo:qo + c.CH],
                       start=True, stop=True),
                   [("sPE", 1)])
        pm_v = a["sPE"]
        a = k.emit("vector", [("sACT", ev), ("dLsc", sc_v)] + k.gate("t1"),
                   lambda e, qo=qo, ch=ch: e.tensor_tensor(
                       out=A["t1"][:, :], in0=A["qbb"][:, qo:qo + c.CH],
                       in1=A["scb"][ch % 2][:, 0:c.CH], op=OP.mult),
                   [("sDVE", 1)])
        t1_v = a["sDVE"]
        a = k.emit("vector", [("sPE", pm_v), ("sDVE", t1_v)] + k.gate("t2"),
                   lambda e, bank=bank, ch=ch: e.tensor_tensor(
                       out=A["t2"][:, :], in0=bank[:, :],
                       in1=A["scb"][ch % 2][:, c.CH:2 * c.CH], op=OP.mult),
                   [("sDVE", 1)])
        t2_v = a["sDVE"]
        k.setg(f"qbb_{idx}", ("sDVE", t2_v), ("sPE", pm_v))
        k.addg(BANKN[idx], ("sDVE", t2_v))
        k.addg(f"scb{ch % 2}", ("sDVE", t2_v))
        a = k.emit("vector", [("sDVE", t2_v)] + k.gate(f"rout{idx}"),
                   lambda e, idx=idx: e.tensor_tensor(out=A["rout"][idx][:, :],
                                                      in0=A["t1"][:, :],
                                                      in1=A["t2"][:, :], op=OP.add),
                   [("sDVE", 1)])
        rp_v = a["sDVE"]
        k.setg("t1", ("sDVE", rp_v))
        k.setg("t2", ("sDVE", rp_v))
        if idx == 0:
            tgts = [qh[par][0][0:64, :], qh[par][1][0:64, :]]
        else:
            tgts = [kh[2 * j][0:64, q0:q0 + c.CH], kh[2 * j + 1][0:64, q0:q0 + c.CH]]
        for z, t in enumerate(tgts):
            k.emit("gpsimd", [("sDVE", rp_v)] if z == 0 else [],
                   lambda e, t=t, z=z, idx=idx: e.dma_start(
                       out=t, in_=A["rout"][idx][64 * z:64 * (z + 1), :]),
                   [(spl, 16)])
        k.setg(f"rout{idx}", (spl, k.v(spl)))
    k.info[("split", slot)] = k.v(spl)


def emit_pair_attention(k, b, ch, j, A):
    c = k.cfg
    NKT = c.ST
    slot = ch * c.NP + j
    par = slot % 2
    pb, pav = A["pb"], A["pav"]
    probs = A["probs"]
    nkt = (ch + 1) * c.CT
    spl = "dGsplit%d" % par
    spl_v = k.info[("split", slot)]
    av_last = {}
    exp_v = {0: {}, 1: {}}

    def do_scores(kt, h, hh, first):
        diag = kt >= ch * c.CT
        off = 128 * (kt - ch * c.CT) if diag else 0
        wdt = c.CH - off
        bi = 4 + kt % 2
        sbank = pb[bi]
        wl = k.gate(BANKN[bi])
        if first:
            wl += [(spl, spl_v), ("sPL", k.info["khones"])]
        a = k.emit("tensor", wl,
                   lambda e, h=h, kt=kt, off=off, wdt=wdt, sbank=sbank, hh=hh, par=par:
                   e.matmul(sbank[:, off:off + wdt],
                            A["kh"][h][0:65, 128 * kt:128 * (kt + 1)],
                            A["qh"][par][hh][0:65, off:off + wdt],
                            start=True, stop=True),
                   [("sPE", 1)])
        sc_pe = a["sPE"]
        band_v = 0
        if diag:
            a2 = k.emit("vector", [("sPE", sc_pe)],
                        lambda e, off=off, sbank=sbank: e.tensor_tensor(
                            out=sbank[:, off:off + 128], in0=sbank[:, off:off + 128],
                            in1=A["band"][:, :], op=OP.add),
                        [("sDVE", 1)])
            band_v = a2["sDVE"]
        wl = [("sPE", sc_pe)] + ([("sDVE", band_v)] if band_v else []) \
            + k.gate(f"probs{kt % 2}")
        a3 = k.emit("scalar", wl,
                    lambda e, h=h, kt=kt, off=off, wdt=wdt, sbank=sbank:
                    e.activation(out=probs[kt % 2][:, off:off + wdt],
                                 in_=sbank[:, off:off + wdt], func=AF.Exp,
                                 bias=A["hkt"][:, h * NKT + kt:h * NKT + kt + 1],
                                 scale=c.SCALE),
                    [("sACT", 1)])
        k.addg(BANKN[bi], ("sACT", a3["sACT"]))
        exp_v[hh][kt] = a3["sACT"]

    def do_av(kt, h, hpar, avb, first, last):
        diag = kt >= ch * c.CT
        off = 128 * (kt - ch * c.CT) if diag else 0
        wdt = c.CH - off
        wl = [("sACT", max(exp_v[h % 2][kt], k.info.get("vb_ready", 0)))]
        if first:
            wl += k.gate(BANKN[6 + hpar])
        a4 = k.emit("tensor", wl,
                    lambda e, h=h, kt=kt, off=off, wdt=wdt, avb=avb, first=first, last=last:
                    e.matmul(avb[0:65, off:off + wdt],
                             A["vb"][:, c.VBW * kt + 65 * h:c.VBW * kt + 65 * h + 65],
                             probs[kt % 2][:, off:off + wdt],
                             start=first, stop=last,
                             skip_group_check=True),
                    [("sPE", 1)])
        k.addg(f"probs{kt % 2}", ("sPE", a4["sPE"]))
        return a4["sPE"]

    for hh in range(2):
        h = 2 * j + hh
        avb = pav[h % 2]
        for i, kt in enumerate(range(nkt)):
            do_scores(kt, h, hh, first=(i == 0))
            if i >= 1:
                do_av(kt - 1, h, h % 2, avb, first=(i == 1), last=False)
        av_last[hh] = do_av(nkt - 1, h, h % 2, avb, first=(nkt == 1), last=True)

    ev_hh = {}
    atcf_v = 0
    for hh in range(2):
        h = 2 * j + hh
        hpar = h % 2
        avb = pav[hpar]
        a = k.emit("scalar", [("sPE", av_last[hh])] + k.gate(f"rsrow{hh}"),
                   lambda e, avb=avb, hh=hh: e.activation(
                       out=A["rsrow"][0:1, 512 * hh:512 * (hh + 1)],
                       in_=avb[64:65, 0:c.CH], func=AF.Copy),
                   [("sACT", 1)])
        rs_v = a["sACT"]
        a = k.emit("scalar", k.gate("atcf") if hh == 0 else [],
                   lambda e, avb=avb, hh=hh: e.activation(
                       out=A["atcf"][64 * hh:64 * (hh + 1), :],
                       in_=avb[0:64, 0:c.CH], func=AF.Copy),
                   [("sACT", 1)])
        atcf_v = a["sACT"]
        k.addg(BANKN[6 + hpar], ("sACT", atcf_v))
        a = k.emit("tensor", [("sACT", rs_v)] + (k.gate(BANKN[3]) if hh == 0 else []),
                   lambda e, hh=hh: e.matmul(pb[3][64 * hh:64 * (hh + 1), 0:c.CH],
                                             A["ones2f"][0:1, 0:64],
                                             A["rsrow"][0:1, 512 * hh:512 * (hh + 1)],
                                             start=True, stop=True),
                   [("sPE", 1)])
        k.setg(f"rsrow{hh}", ("sPE", a["sPE"]))
        ev_hh[hh] = a["sPE"]
    a = k.emit("vector", [("sPE", ev_hh[1])] + k.gate("rbB"),
               lambda e: e.reciprocal(out=A["rb"][:, :], in_=pb[3][:, 0:c.CH]),
               [("sDVE", 1)])
    rcp_v = a["sDVE"]
    k.addg(BANKN[3], ("sDVE", rcp_v))
    a = k.emit("vector", [("sDVE", rcp_v), ("sACT", atcf_v)] + k.gate("atc_busy"),
               lambda e, j=j: e.tensor_tensor(out=A["atc"][j][:, :], in0=A["atcf"][:, :],
                                              in1=A["rb"][:, :], op=OP.mult),
               [("sDVE", 1)])
    am_v = a["sDVE"]
    k.setg("rbB", ("sDVE", am_v))
    k.setg("atcf", ("sDVE", am_v))
    k.addg("atc_all", ("sDVE", am_v))
    k.setg(f"qh{par}", ("sPE", k.v("sPE")))


def emit_outproj_chunk(k, b, ch, A):
    c = k.cfg
    din = k.din
    q0 = ch * c.CH
    banks = A["banks"]
    atc_all = k.gate("atc_all")

    for p in range(c.NP):
        wpar = p % 2
        k.emit("gpsimd", k.gate(f"wob{wpar}") + (atc_all if p == 0 else []),
               lambda e, p=p, wpar=wpar: e.dma_start(
                   out=A["wob"][wpar][:, :], in_=din["wout"][128 * p:128 * (p + 1), :]),
               [("dGo%d" % wpar, 16)])
        gov = k.v("dGo%d" % wpar)
        for tt in range(c.CT):
            for dmc in range(2):
                g = 2 * tt + dmc
                wl = []
                if tt == 0 and dmc == 0:
                    wl = [("dGo%d" % wpar, gov)]
                    if p == 0:
                        wl += atc_all + k.gates(*BANKN)
                k.emit("tensor", wl,
                       lambda e, p=p, tt=tt, dmc=dmc, g=g, wpar=wpar: e.matmul(
                           banks[g][:, :], A["atc"][p][:, 128 * tt:128 * (tt + 1)],
                           A["wob"][wpar][:, 512 * dmc:512 * (dmc + 1)],
                           start=(p == 0), stop=False, skip_group_check=True),
                       [("sPE", 1)])
        k.setg(f"wob{wpar}", ("sPE", k.v("sPE")))
    pe_g = {}
    for tt in range(c.CT):
        for dmc in range(2):
            g = 2 * tt + dmc
            k.emit("tensor", [("dGaux", k.auxv)] if g == 0 else [],
                   lambda e, g=g, dmc=dmc: e.matmul(
                       banks[g][:, :], A["ones_row"][0:1, 0:128],
                       A["boutb"][0:1, 512 * dmc:512 * (dmc + 1)],
                       start=False, stop=True, skip_group_check=True),
                   [("sPE", 1)])
            pe_g[g] = k.v("sPE")
    pe_op = k.v("sPE")
    k.setg("atc_busy", ("sPE", pe_op))
    k.g.pop("atc_all", None)

    for tt in range(c.CT):
        par = tt % 2
        a = k.emit("sync", k.gate(f"xpc{par}"),
                   lambda e, tt=tt, par=par: e.dma_start(
                       out=A["xpc"][par][:, :],
                       in_=din["x"][b, q0 + 128 * tt:q0 + 128 * (tt + 1), :]),
                   [("dLxr%d" % par, 16)])
        lxr = k.v("dLxr%d" % par)
        tt_done = 0
        for dmc in range(2):
            g = 2 * tt + dmc
            a = k.emit("vector", [("sPE", pe_g[g]), ("dLxr%d" % par, lxr)] +
                       (k.gate("y1t") if dmc == 0 else []),
                       lambda e, tt=tt, dmc=dmc, g=g, par=par: e.tensor_tensor(
                           out=A["y1t"][:, 512 * dmc:512 * (dmc + 1)],
                           in0=banks[g][:, :],
                           in1=A["xpc"][par][:, 512 * dmc:512 * (dmc + 1)], op=OP.add),
                       [("sDVE", 1)])
            tt_done = a["sDVE"]
            k.addg(BANKN[g], ("sDVE", tt_done))
        k.setg(f"xpc{par}", ("sDVE", tt_done))
        a = k.emit("vector", [("sDVE", tt_done)] + k.gate("bns"),
                   lambda e: e.bn_stats(out=A["bns"][:, 0:6], in_=A["y1t"][:, 0:512]),
                   [("sDVE", 1)])
        a = k.emit("vector", [("sDVE", a["sDVE"])],
                   lambda e: e.bn_stats(out=A["bns"][:, 6:12], in_=A["y1t"][:, 512:1024]),
                   [("sDVE", 1)])
        a = k.emit("vector", [("sDVE", a["sDVE"])] + k.gate("lnt"),
                   lambda e: e.bn_aggr(out=A["lnt"][:, 0:2], in_=A["bns"][:, :]),
                   [("sDVE", 1)])
        k.setg("bns", ("sDVE", a["sDVE"]))
        st_v = a["sDVE"]
        a = k.emit("scalar", [("sDVE", st_v)],
                   lambda e: e.activation(out=A["lnt"][:, 2:3], in_=A["lnt"][:, 1:2],
                                          func=AF.Ln, bias=EPS), [("sACT", 1)])
        a = k.emit("scalar", [("sACT", a["sACT"])],
                   lambda e: e.activation(out=A["lnt"][:, 3:4], in_=A["lnt"][:, 1:2],
                                          func=AF.Ln, bias=EPS * EPS, scale=1.0 + EPS),
                   [("sACT", 1)])
        a = k.emit("scalar", [("sACT", a["sACT"])],
                   lambda e: e.activation(out=A["lnt"][:, 5:6], in_=A["lnt"][:, 2:3],
                                          func=AF.Exp, scale=-0.5), [("sACT", 1)])
        a = k.emit("vector", [("sACT", a["sACT"]), ("sDVE", st_v)],
                   lambda e: e.tensor_tensor(out=A["lnt"][:, 4:5], in0=A["lnt"][:, 2:3],
                                             in1=A["lnt"][:, 3:4], op=OP.subtract),
                   [("sDVE", 1)])
        a = k.emit("scalar", [("sDVE", a["sDVE"]), ("sACT", a.get("sACT", 0) if isinstance(a, dict) else 0)],
                   lambda e: e.activation(out=A["lnt"][:, 6:7], in_=A["lnt"][:, 4:5],
                                          func=AF.Exp, scale=0.5), [("sACT", 1)])
        a = k.emit("vector", [("sACT", a["sACT"])],
                   lambda e: e.tensor_tensor(out=A["lnt"][:, 7:8], in0=A["lnt"][:, 5:6],
                                             in1=A["lnt"][:, 6:7], op=OP.mult),
                   [("sDVE", 1)])
        r2_v = a["sDVE"]
        a = k.emit("vector", [("sDVE", r2_v)] + k.gate("x1s"),
                   lambda e: e.tensor_scalar(out=A["x1s"][:, :], in0=A["y1t"][:, :],
                                             scalar1=A["lnt"][:, 0:1],
                                             scalar2=A["lnt"][:, 5:6],
                                             op0=OP.subtract, op1=OP.mult),
                   [("sDVE", 1)])
        x1_v = a["sDVE"]
        a2 = k.emit("vector", [("sDVE", r2_v)] + k.gate("xh2s"),
                    lambda e: e.tensor_scalar(out=A["xh2s"][:, :], in0=A["y1t"][:, :],
                                              scalar1=A["lnt"][:, 0:1],
                                              scalar2=A["lnt"][:, 7:8],
                                              op0=OP.subtract, op1=OP.mult),
                    [("sDVE", 1)])
        k.setg("lnt", ("sDVE", a2["sDVE"]))
        k.setg("y1t", ("sDVE", a2["sDVE"]))
        r0 = q0 + 128 * tt
        a3 = k.emit("gpsimd", [("sDVE", x1_v)],
                    lambda e, r0=r0: e.dma_start(out=k.s["x1"][b, r0:r0 + 128, :],
                                                 in_=A["x1s"][:, :]),
                    [("dGsp", 16)])
        a3 = k.emit("gpsimd", [("sDVE", a2["sDVE"])],
                    lambda e, r0=r0: e.dma_start(out=k.s["xh2"][b, r0:r0 + 128, :],
                                                 in_=A["xh2s"][:, :]),
                    [("dGsp", 16)])
        k.setg("x1s", ("dGsp", a3["dGsp"]))
        k.setg("xh2s", ("dGsp", a3["dGsp"]))


def emit_F(k, b, F):
    c = k.cfg
    din = k.din
    pb = F["pb"]
    spill_v = k.v("dGsp")

    def load_wgrp(half, nt, wl_extra):
        par = nt % 2
        wsem = "dLw%d" % par
        for d in range(c.DMT):
            k.emit("sync", (wl_extra + k.gate(f"wgrp{nt}")) if d == 0 else [],
                   lambda e, nt=nt, d=d, half=half: e.dma_start(
                       out=F["w1h"][:, (d * c.FNTH + nt) * 128:(d * c.FNTH + nt) * 128 + 128],
                       in_=k.s["w1"][128 * d:128 * (d + 1),
                                     c.FH * half + 128 * nt:c.FH * half + 128 * (nt + 1)]),
                   [(wsem, 16)])
        k.emit("sync", [],
               lambda e, nt=nt, half=half: e.dma_start(
                   out=F["w2h"][:, c.DM * nt:c.DM * (nt + 1)],
                   in_=k.s["w2"][c.FH * half + 128 * nt:c.FH * half + 128 * (nt + 1), :]),
               [(wsem, 16)])
        return k.v(wsem)

    for half in range(2):
        grp_v = {}
        grp_v[0] = load_wgrp(half, 0, [("dGsw0", k.v("dGsw0")), ("dGsw1", k.v("dGsw1"))] if (half == 0 and b == 0) else [])
        if c.FNTH > 1:
            grp_v[1] = load_wgrp(half, 1, [])

        for tc in range(c.NTC):
            r0 = tc * c.TCH
            par = tc % 2
            for d in range(c.DMT):
                for h2 in range(2):
                    wl = k.gate(f"xh2T{par}") if (d == 0 and h2 == 0) else []
                    if d == 0 and h2 == 0 and half == 0 and tc == 0:
                        wl += [("dGsp", spill_v)]
                    k.emit("sync", wl,
                           lambda e, d=d, h2=h2, par=par, r0=r0: e.dma_start_transpose(
                               out=F["xh2T"][par][:, c.TCH * d + 128 * h2:
                                                  c.TCH * d + 128 * (h2 + 1)],
                               in_=k.s["xh2"][b, r0 + 128 * h2:r0 + 128 * (h2 + 1),
                                              128 * d:128 * (d + 1)]),
                           [("dLxh%d" % par, 16)])
            xh_v = k.v("dLxh%d" % par)
            x1_v = 0
            if half == 1:
                for h2 in range(2):
                    k.emit("sync", (k.gate(f"xr{par}") if h2 == 0 else []),
                           lambda e, h2=h2, par=par, r0=r0: e.dma_start(
                               out=F["xr"][par][:, c.DM * h2:c.DM * (h2 + 1)],
                               in_=k.s["x1"][b, r0 + 128 * h2:r0 + 128 * (h2 + 1), :]),
                           [("dLx1%d" % par, 16)])
                x1_v = k.v("dLx1%d" % par)
            silu_done = {}

            def emit_fc1(nt):
                bi = nt % 2
                bank = pb[bi]
                for d in range(c.DMT):
                    wl = []
                    if d == 0:
                        wl = [("dLw%d" % (nt % 2), grp_v[nt]),
                              ("dLxh%d" % par, xh_v)] + k.gate(BANKN[bi])
                    k.emit("tensor", wl,
                           lambda e, nt=nt, d=d, bank=bank, par=par: e.matmul(
                               bank[:, 0:c.TCH],
                               F["w1h"][:, (d * c.FNTH + nt) * 128:
                                        (d * c.FNTH + nt) * 128 + 128],
                               F["xh2T"][par][:, c.TCH * d:c.TCH * (d + 1)],
                               start=(d == 0), stop=(d == c.DMT - 1)),
                           [("sPE", 1)])
                fc1_v = k.v("sPE")
                if tc == 0 and nt + 2 < c.FNTH:
                    grp_v[nt + 2] = load_wgrp(half, nt + 2, [])
                if nt == c.FNTH - 1:
                    k.setg(f"xh2T{par}", ("sPE", fc1_v))
                ncol = half * c.FNTH + nt
                a = k.emit("scalar", [("sPE", fc1_v), ("dLtab", k.v("dLtab"))] +
                           k.gate(f"yT{nt % 4}"),
                           lambda e, nt=nt, bank=bank, ncol=ncol: e.activation(
                               out=F["yT"][:, c.TCH * nt:c.TCH * (nt + 1)],
                               in_=bank[:, 0:c.TCH], func=AF.Silu,
                               bias=F["fc1bcol"][:, ncol:ncol + 1]),
                           [("sACT", 1)])
                silu_done[nt] = a["sACT"]
                k.addg(BANKN[bi], ("sACT", a["sACT"]))

            def emit_fc2(nt):
                for tt2 in range(2):
                    for dmc in range(2):
                        g = 2 + 2 * tt2 + dmc
                        wl = [("sACT", silu_done[nt])] if (tt2 == 0 and dmc == 0) else []
                        if nt == 0 and tt2 == 0 and dmc == 0:
                            wl += k.gates(*BANKN[2:6])
                        k.emit("tensor", wl,
                               lambda e, nt=nt, tt2=tt2, dmc=dmc, g=g: e.matmul(
                                   pb[g][:, :],
                                   F["yT"][:, c.TCH * nt + 128 * tt2:
                                           c.TCH * nt + 128 * (tt2 + 1)],
                                   F["w2h"][:, c.DM * nt + 512 * dmc:
                                            c.DM * nt + 512 * (dmc + 1)],
                                   start=(nt == 0), stop=False, skip_group_check=True),
                               [("sPE", 1)])
                if tc == c.NTC - 1:
                    k.setg(f"wgrp{nt}", ("sPE", k.v("sPE")))
                k.setg(f"yT{nt % 4}", ("sPE", k.v("sPE")))

            emit_fc1(0)
            for nt in range(c.FNTH):
                if nt + 1 < c.FNTH:
                    emit_fc1(nt + 1)
                emit_fc2(nt)
            if half == 0:
                zpe_g = {}
                for tt2 in range(2):
                    for dmc in range(2):
                        g = 2 + 2 * tt2 + dmc
                        k.emit("tensor", [("dGaux", k.auxv)] if g == 2 else [],
                               lambda e, g=g, dmc=dmc: e.matmul(
                                   pb[g][:, :], F["ones_row"][0:1, 0:128],
                                   F["frow2"][0:1, 512 * dmc:512 * (dmc + 1)],
                                   start=False, stop=True, skip_group_check=True),
                               [("sPE", 1)])
                        zpe_g[g] = k.v("sPE")
                z_pe = k.v("sPE")
                for tt2 in range(2):
                    for dmc in range(2):
                        g = 2 + 2 * tt2 + dmc
                        zc = (2 * tc + tt2) * c.DM + 512 * dmc
                        a = k.emit("scalar", [("sPE", zpe_g[g])],
                                   lambda e, g=g, zc=zc: e.activation(
                                       out=F["zacc"][:, zc:zc + 512], in_=pb[g][:, :],
                                       func=AF.Copy),
                                   [("sACT", 1)])
                        k.addg(BANKN[g], ("sACT", a["sACT"]))
                k.info[("zev", tc)] = k.v("sACT")
            else:
                z_pe = k.v("sPE")
                for tt2 in range(2):
                    t2_v = 0
                    for dmc in range(2):
                        g = 2 + 2 * tt2 + dmc
                        zc = (2 * tc + tt2) * c.DM + 512 * dmc
                        wl = []
                        if dmc == 0:
                            wl = [("sPE", z_pe), ("sACT", k.info[("zev", tc)]),
                                  ("dLx1%d" % par, x1_v)] + k.gate("tfin")
                        a = k.emit("vector", wl,
                                   lambda e, g=g, zc=zc, dmc=dmc, tt2=tt2, par=par:
                                   e.scalar_tensor_tensor(
                                       out=F["tfin"][:, 512 * dmc:512 * (dmc + 1)],
                                       in0=F["zacc"][:, zc:zc + 512], scalar=0.25,
                                       in1=F["xr"][par][:, c.DM * tt2 + 512 * dmc:
                                                        c.DM * tt2 + 512 * (dmc + 1)],
                                       op0=OP.mult, op1=OP.add),
                                   [("sDVE", 1)])
                        a = k.emit("vector", [("sDVE", a["sDVE"])] + (k.gate("tfin2") if dmc == 0 else []),
                                   lambda e, g=g, dmc=dmc: e.scalar_tensor_tensor(
                                       out=F["tfin2"][:, 512 * dmc:512 * (dmc + 1)],
                                       in0=pb[g][:, :], scalar=0.25,
                                       in1=F["tfin"][:, 512 * dmc:512 * (dmc + 1)],
                                       op0=OP.mult, op1=OP.add),
                                   [("sDVE", 1)])
                        t2_v = a["sDVE"]
                        k.addg(BANKN[g], ("sDVE", t2_v))
                    k.setg("tfin", ("sDVE", t2_v))
                    if tt2 == 1:
                        k.setg(f"xr{par}", ("sDVE", t2_v))
                    a = k.emit("vector", [("sDVE", t2_v)] + k.gate("bns"),
                               lambda e: e.bn_stats(out=F["bns"][:, 0:6],
                                                    in_=F["tfin2"][:, 0:512]),
                               [("sDVE", 1)])
                    a = k.emit("vector", [("sDVE", a["sDVE"])],
                               lambda e: e.bn_stats(out=F["bns"][:, 6:12],
                                                    in_=F["tfin2"][:, 512:1024]),
                               [("sDVE", 1)])
                    a = k.emit("vector", [("sDVE", a["sDVE"])] + k.gate("lnt"),
                               lambda e: e.bn_aggr(out=F["lnt"][:, 0:2],
                                                   in_=F["bns"][:, :]),
                               [("sDVE", 1)])
                    st_v = a["sDVE"]
                    k.setg("bns", ("sDVE", st_v))
                    a = k.emit("scalar", [("sDVE", st_v)],
                               lambda e: e.activation(out=F["lnt"][:, 2:3],
                                                      in_=F["lnt"][:, 1:2],
                                                      func=AF.Ln, bias=EPS),
                               [("sACT", 1)])
                    a = k.emit("scalar", [("sACT", a["sACT"])],
                               lambda e: e.activation(out=F["lnt"][:, 5:6],
                                                      in_=F["lnt"][:, 2:3],
                                                      func=AF.Exp, scale=-0.5),
                               [("sACT", 1)])
                    opar = (2 * tc + tt2) % 2
                    a = k.emit("vector", [("sACT", a["sACT"]), ("sDVE", st_v)] + k.gate(f"outst{opar}"),
                               lambda e, opar=opar: e.tensor_scalar(
                                   out=F["outst"][opar][:, :], in0=F["tfin2"][:, :],
                                   scalar1=F["lnt"][:, 0:1], scalar2=F["lnt"][:, 5:6],
                                   op0=OP.subtract, op1=OP.mult),
                               [("sDVE", 1)])
                    k.setg("tfin2", ("sDVE", a["sDVE"]))
                    k.setg("lnt", ("sDVE", a["sDVE"]))
                    r1 = r0 + 128 * tt2
                    a2 = k.emit("gpsimd", [("sDVE", a["sDVE"])],
                                lambda e, r1=r1, opar=opar: e.dma_start(
                                    out=k.y[b, r1:r1 + 128, :],
                                    in_=F["outst"][opar][:, :]),
                                [("dGout", 16)])
                    if tt2 == 1:
                        k.setg("outst0", ("dGout", a2["dGout"]))
                        k.setg("outst1", ("dGout", a2["dGout"]))


def kernel(L0_attn_x, L0_attn_w_qkv, L0_attn_b_qkv, L0_attn_w_out, L0_attn_b_out,
           L0_ffn_ln_gamma, L0_ffn_ln_beta, L0_ffn_fc1_q, L0_ffn_fc1_scales,
           L0_ffn_fc1_bias, L0_ffn_fc2_q, L0_ffn_fc2_scales, L0_ffn_fc2_bias):
    cfg = Cfg()
    tabs = host_tables(cfg)
    wd = host_prep(cfg, L0_attn_w_qkv, L0_attn_b_qkv, L0_attn_w_out, L0_attn_b_out,
                   L0_ffn_ln_gamma, L0_ffn_ln_beta, L0_ffn_fc1_q, L0_ffn_fc1_scales,
                   L0_ffn_fc1_bias, L0_ffn_fc2_q, L0_ffn_fc2_scales, L0_ffn_fc2_bias)
    x = np.ascontiguousarray(np.asarray(L0_attn_x, dtype=np.float32))
    kk = build(cfg)
    shared = dict(tabs)
    shared.update(wd)
    in_maps = []
    for core in range(N_CORES):
        m = dict(shared)
        m["x"] = np.ascontiguousarray(x[4 * core:4 * core + 4])
        in_maps.append(m)
    res = run_bass_kernel_spmd(kk.nc, in_maps, core_ids=list(range(N_CORES)))
    return np.concatenate([r["y"] for r in res.results], axis=0)
